# version 01
# baseline (speedup 1.0000x reference)
import contextlib
import numpy as np
import concourse.bass as bass
import concourse.mybir as mybir
from concourse.bass_utils import run_bass_kernel_spmd
from concourse.alu_op_type import AluOpType as AL

F32 = mybir.dt.float32
BF16 = mybir.dt.bfloat16
AF = mybir.ActivationFunctionType

HID = 2048
HD = 64
NH = 32
NHP = 16
ROT = 32
S = 4096
HF = 2048
C = 128
NCH = HF // C
EPS = 1e-5
NE = 32
IFF = 1408
NI = IFF // 128
SI = 2816
NSI = SI // 128
SPAN = 512
NSPAN = HF // SPAN
THETA = 10000.0
SCALE_BASE = 512.0

KDEBUG = {}
KPHASES = [99]
KSUB = [99]
_CACHE = {}



def _gamma():
    return 1.0 - 2.0 ** (-5.0 - np.arange(NH, dtype=np.float64))


def _rot_tables():
    t = np.arange(S, dtype=np.float64)
    r = np.arange(0, ROT, 2, dtype=np.float64)
    inv_freq = 1.0 / THETA ** (r / ROT)
    freqs = np.repeat(t[:, None] * inv_freq[None, :], 2, axis=-1)
    base_scale = (r + 0.4 * ROT) / (1.4 * ROT)
    power = (t - S // 2) / SCALE_BASE
    scale = np.concatenate([base_scale[None, :] ** power[:, None]] * 2, axis=-1)
    cos = np.cos(freqs); sin = np.sin(freqs)
    sgn = np.where(np.arange(ROT) % 2 == 0, -1.0, 1.0)
    Cq = np.ones((HD, S)); Sq = np.zeros((HD, S))
    Ck = np.ones((HD, S)); Sk = np.zeros((HD, S))
    Cq[:ROT] = (cos * scale).T
    Ck[:ROT] = (cos / scale).T
    Sq[:ROT] = (sin * scale).T * sgn[:, None]
    Sk[:ROT] = (sin / scale).T * sgn[:, None]
    return [np.concatenate([a, a], 0).astype(np.float32) for a in (Cq, Sq, Ck, Sk)]


def _const_tables():
    g = _gamma()
    j = np.arange(C)
    d = j[None, :] - j[:, None]
    p128 = np.arange(128)
    dtb = np.zeros((128, NHP * 256), np.float32)
    for hp in range(NHP):
        for w in range(2):
            D = np.where(d >= 0, np.power(g[2 * hp + w], np.maximum(d, 0)), 0.0)
            dtb[:, hp * 256 + w * 128:hp * 256 + (w + 1) * 128] = D
    xib = np.zeros((128, NHP * 128), np.float32)
    etb = np.zeros((128, NHP * 128), np.float32)
    g128 = np.zeros((128, NHP), np.float32)
    for hp in range(NHP):
        gp = np.where(p128 >= 64, g[2 * hp + 1], g[2 * hp])
        xib[:, hp * 128:(hp + 1) * 128] = np.power(gp[:, None], (j + 1)[None, :])
        etb[:, hp * 128:(hp + 1) * 128] = np.power(gp[None, :], (C - 1 - j)[:, None])
        g128[:, hp] = np.power(gp, C)
    M = np.zeros((HD, HD), np.float32)
    for i in range(ROT // 2):
        M[2 * i, 2 * i + 1] = 1.0
        M[2 * i + 1, 2 * i] = 1.0
    bdp = np.zeros((128, 128), np.float32)
    bdp[:HD, :HD] = M; bdp[HD:, HD:] = M
    sel_e = np.zeros((32, NE * 128), np.float32)
    for e in range(NE):
        sel_e[e, e * 128:(e + 1) * 128] = 1.0
    sel64 = np.zeros((128, 2), np.float32)
    sel64[:64, 0] = 1.0; sel64[64:, 1] = 1.0
    rep2 = np.zeros((2, 128), np.float32)
    rep2[0, :64] = 1.0; rep2[1, 64:] = 1.0
    return dict(dtb=dtb, xib=xib, etb=etb, g128=g128, bdp=bdp,
                iden=np.eye(128, dtype=np.float32), sel_e=sel_e,
                sel64=sel64, rep2=rep2)


def _prep_shared(inp):
    f = np.float32
    ct = np.ascontiguousarray
    wqkv = np.asarray(inp["L0_attn_wqkv"], f)
    w3 = wqkv.reshape(NH, 3, HD, HID)
    d = {}
    d["wqT"] = ct(w3[:, 0].reshape(NH * HD, HID).T)
    d["wkT"] = ct(w3[:, 1].reshape(NH * HD, HID).T)
    d["wvT"] = ct(w3[:, 2].reshape(NH * HD, HID).T)
    d["wgT"] = ct(np.asarray(inp["L0_attn_wg"], f).T)
    d["woT"] = ct(np.asarray(inp["L0_attn_wo"], f).T)
    d["gwT"] = ct(np.asarray(inp["L0_ffn_gate_w"], f).T)
    d["RgT"] = ct(np.asarray(inp["L0_ffn_Rg"], f).T)
    d["RuT"] = ct(np.asarray(inp["L0_ffn_Ru"], f).T)
    d["RdT"] = ct(np.asarray(inp["L0_ffn_Rd"], f).T)
    for nm in ("Ug", "Cg", "Uu", "Cu", "Ud"):
        d[nm + "T"] = ct(np.asarray(inp["L0_ffn_" + nm], f).transpose(0, 2, 1))
    Cd = np.asarray(inp["L0_ffn_Cd"], f)
    d["CdT"] = ct(Cd.reshape(NE, 16, 128, 128).transpose(1, 0, 3, 2))
    for nm in ("s_Rg", "s_Ru", "s_Ug", "s_Uu", "s_Ud", "s_Cg", "s_Cu", "s_Rd", "s_Cd"):
        d[nm.replace("_", "") + "T"] = ct(np.asarray(inp["L0_ffn_" + nm], f).T)
    d["gnw2"] = ct(np.asarray(inp["L0_attn_gn_weight"], f).reshape(NHP, 2).T)
    d["gnb2"] = ct(np.asarray(inp["L0_attn_gn_bias"], f).reshape(NHP, 2).T)
    d.update(_const_tables())
    return d


def _prep_core(inp, shared, rot, core):
    b, own = core // 2, core % 2
    xb = np.asarray(inp["L0_attn_x"], np.float32)[b]
    perm = np.r_[own * HF:(own + 1) * HF, (1 - own) * HF:(2 - own) * HF]
    m = dict(shared)
    m["x"] = np.ascontiguousarray(xb[perm])
    m["cq"] = np.ascontiguousarray(rot[0][:, perm])
    m["sq"] = np.ascontiguousarray(rot[1][:, perm])
    m["ck"] = np.ascontiguousarray(rot[2][:, perm])
    m["sk"] = np.ascontiguousarray(rot[3][:, perm])
    m["mA"] = np.full((128, 1), float(own), np.float32)
    m["mB"] = np.full((128, 1), float(1 - own), np.float32)
    return m



class Sem:
    __slots__ = ("h", "v")

    def __init__(self, h):
        self.h = h
        self.v = 0


class B:
    def __init__(self):
        nc = bass.Bass("TRN2", target_bir_lowering=False)
        self.nc = nc
        self.EL = {k: [] for k in ("sync", "tensor", "vector", "scalar", "gpsimd")}
        self.bankg = {}
        self.es = contextlib.ExitStack()
        self.es.__enter__()
        self.taps = {}
        self._declare()

    def op(self, eng, fn=None, waits=(), incs=()):
        ws = tuple(w for w in waits if w is not None)
        self.EL[eng].append((fn, ws, tuple(incs)))

    def W(self, s, v=None):
        return (s.h, s.v if v is None else v)

    def I(self, s, n=1):
        s.v += n
        return (s.h, n)

    def mmw(self, keys, extra=()):
        ws = [w for w in extra if w is not None]
        seen = set()
        for k in keys:
            g = self.bankg.get(k)
            if g is not None and g not in seen:
                ws.append(g); seen.add(g)
        return ws

    def setg(self, keys, w):
        for k in keys:
            self.bankg[k] = w

    def _declare(self):
        nc = self.nc
        self.D = {}

        def din(name, shape, dt=F32):
            self.D[name] = nc.dram_tensor(name, shape, dt, kind="ExternalInput").ap()

        din("x", [S, HID])
        for n in ("cq", "sq", "ck", "sk"):
            din(n, [128, S])
        for n in ("wqT", "wkT", "wvT", "wgT", "woT"):
            din(n, [HID, HID])
        din("gwT", [HID, 32])
        din("RgT", [HID, 128]); din("RuT", [HID, 128]); din("RdT", [IFF, 128])
        din("UgT", [NE, 128, 128]); din("CgT", [NE, 128, IFF])
        din("UuT", [NE, 128, 128]); din("CuT", [NE, 128, IFF])
        din("UdT", [NE, 128, 128]); din("CdT", [16, NE, 128, 128])
        din("sRgT", [HID, 128]); din("sRuT", [HID, 128])
        din("sUgT", [128, 128]); din("sUuT", [128, 128]); din("sUdT", [128, 128])
        din("sCgT", [128, SI]); din("sCuT", [128, SI])
        din("sRdT", [SI, 128]); din("sCdT", [128, HID])
        din("gnw2", [2, NHP]); din("gnb2", [2, NHP])
        din("dtb", [128, NHP * 256]); din("xib", [128, NHP * 128])
        din("etb", [128, NHP * 128]); din("g128", [128, NHP])
        din("bdp", [128, 128]); din("iden", [128, 128])
        din("sel_e", [32, NE * 128]); din("sel64", [128, 2]); din("rep2", [2, 128])
        din("mA", [128, 1]); din("mB", [128, 1])
        self.out = nc.dram_tensor("out", [HF, HID], F32, kind="ExternalOutput").ap()
        self.ln1_sp = nc.dram_tensor("ln1_sp", [S, HID], BF16, kind="Internal").ap()
        self.ln2_sp = nc.dram_tensor("ln2_sp", [HF, HID], BF16, kind="Internal").ap()
        self.kv_sp = nc.dram_tensor("kv_sp", [NHP, 2, 128, HF], BF16, kind="Internal").ap()
        self.ret_sp = nc.dram_tensor("ret_sp", [NHP, 128, HF], F32, kind="Internal").ap()
        self.h1_sp = nc.dram_tensor("h1_sp", [HF, HID], F32, kind="Internal").ap()
        self.mish_sp = nc.dram_tensor("mish_sp", [16, 128, HF], BF16, kind="Internal").ap()

        if KDEBUG:
            def tap(name, shape, dt=F32):
                self.taps[name] = nc.dram_tensor("tap_" + name, shape, dt, kind="ExternalOutput").ap()
            if KDEBUG.get("ln1"): tap("ln1", [S, HID], BF16)
            if KDEBUG.get("p0dbg"): tap("p0dbg", [128, 32 * 8])
            if KDEBUG.get("p0dbg"): tap("p0dbg2", [128, 32 * 4])
            if KDEBUG.get("spb"): tap("spb", [128, 1024])
            if KDEBUG.get("ret"): tap("ret", [NHP, 128, HF])
            if KDEBUG.get("gn"): tap("gn", [128, 2 * NHP])
            if KDEBUG.get("h1"): tap("h1", [HF, HID])
            if KDEBUG.get("we"): tap("we", [128, NHP * 32])
            if KDEBUG.get("rg"): tap("rg", [128, HF], BF16)
            if KDEBUG.get("srdu"): tap("srdu", [128, HF], BF16)
            if KDEBUG.get("rdu"): tap("rdu", [128, NE * SPAN], BF16)

        ARN_N = 94208
        self.ARN = self.es.enter_context(nc.sbuf_tensor("b_ARN", [128, ARN_N], BF16))
        reqs = []

        def rq(name, units, ph0, ph1, f32=False, nparts=128):
            reqs.append((name, units, f32, ph0, ph1, nparts))

        rq("dtb", NHP * 256, 0, 3)
        rq("xib", NHP * 128, 0, 3)
        rq("etb", NHP * 128, 0, 3)
        rq("sel_e", NE * 128, 5, 7, nparts=32)
        rq("BIGA", 16 * HF, 1, 7)
        for n in ("cqb", "sqb", "ckb", "skb"):
            rq(n, HF, 1, 3)
        for n in ("qraw", "kraw", "vfm", "vtm", "keta", "xiq"):
            rq(n, HF, 1, 3)
        rq("qp", HF, 1, 4)
        rq("kp", HF, 1, 3)
        rq("t2", HF, 1, 4)
        rq("stagf", 2 * HF, 0, 7, f32=True)
        rq("wq0", HF, 2, 3); rq("wq1", HF, 2, 3)
        rq("wk0", HF, 1, 2); rq("wk1", HF, 1, 2)
        rq("wv0", HF, 1, 2); rq("wv1", HF, 1, 2)
        rq("wgen0", HF, 4, 5); rq("wgen1", HF, 4, 5)
        rq("xtb0", HID, 0, 1); rq("xtb1", HID, 0, 1)
        rq("lnt0", HID, 0, 1); rq("lnt1", HID, 0, 1)
        rq("ln2t0", HID, 4, 4); rq("ln2t1", HID, 4, 4)
        rq("xtf0", 2 * HID, 4, 4, f32=True); rq("xtf1", 2 * HID, 4, 4, f32=True)
        rq("h1t0", 2 * HID, 4, 7, f32=True); rq("h1t1", 2 * HID, 4, 4, f32=True)
        rq("strip0", 2 * HID, 4, 4, f32=True); rq("strip1", 2 * HID, 4, 4, f32=True)
        rq("mstrip0", HID, 4, 4); rq("mstrip1", HID, 4, 4)
        rq("msh0", 1024, 4, 4); rq("msh1", 1024, 4, 4)
        rq("rg_sb", HF, 5, 7); rq("ru_sb", HF, 5, 7)
        rq("srg_sb", HF, 5, 6); rq("sru_sb", HF, 5, 6)
        rq("srdu_sb", HF, 5, 7)
        rq("weT", HF, 5, 7, nparts=32)
        rq("gwsb", NHP * 32, 5, 5)
        rq("rdT_sb", NI * 128, 5, 7)
        rq("sCg_sb", SI, 5, 6); rq("sCu_sb", SI, 5, 6)
        rq("sRd_sb", NSI * 128, 5, 6)
        rq("sU_sb", 3 * 128, 5, 6)
        rq("sCd_sb", HID, 5, 7)
        rq("sgb", HF, 6, 6)
        rq("hsg", HF, 6, 6); rq("hsu", HF, 6, 6)
        rq("lgf", 2 * HF, 5, 5, f32=True, nparts=32)
        rq("lgtm", 2 * NHP * 32, 5, 5, f32=True)
        rq("wetm", 2 * NHP * 32, 5, 5, f32=True)
        rq("smX", 6 * HF, 6, 6)
        rq("hg_sb", SPAN, 7, 7); rq("hu_sb", SPAN, 7, 7)
        rq("rd_sb", SPAN, 7, 7); rq("wb_sb", SPAN, 7, 7)
        rq("gblk", SPAN, 7, 7); rq("mblk", SPAN, 7, 7)
        for i in range(2):
            rq(f"ug{i}", 128, 7, 7); rq(f"cg{i}", IFF, 7, 7)
            rq(f"uu{i}", 128, 7, 7); rq(f"cu{i}", IFF, 7, 7)
            rq(f"ud{i}", 128, 7, 7)
            rq(f"cdw{i}", NE * 128, 7, 7)
        rq("OASM", 2 * 4 * HID, 7, 7, f32=True)
        placed = []
        self.buf = {}
        for name, units, f32, ph0, ph1, nparts in sorted(reqs, key=lambda r: -r[1]):
            cand = 0
            while True:
                lo, hi = cand, cand + units
                clash = None
                for (o, u, a0, a1) in placed:
                    if not (hi <= o or o + u <= lo) and not (ph1 < a0 or a1 < ph0):
                        clash = o + u
                        break
                if clash is None:
                    break
                cand = (clash + 15) // 16 * 16
            assert cand + units <= ARN_N, f"arena overflow at {name}: {cand}+{units}"
            placed.append((cand, units, ph0, ph1))
            ap = self.ARN[0:nparts, cand:cand + units]
            if f32:
                ap = ap.bitcast(F32)
            self.buf[name] = ap
        self._arena_peak = max(o + u for (o, u, _, _) in placed)
        bf = self.buf
        self.dtb, self.xib, self.etb, self.sel_e = bf["dtb"], bf["xib"], bf["etb"], bf["sel_e"]
        self.BIGA = bf["BIGA"]
        self.cqb, self.sqb, self.ckb, self.skb = bf["cqb"], bf["sqb"], bf["ckb"], bf["skb"]
        self.qraw, self.kraw, self.qp, self.kp = bf["qraw"], bf["kraw"], bf["qp"], bf["kp"]
        self.t2, self.vfm, self.vtm, self.keta, self.xiq = bf["t2"], bf["vfm"], bf["vtm"], bf["keta"], bf["xiq"]
        self.stagf = bf["stagf"]
        self.wqs = [bf["wq0"], bf["wq1"]]
        self.wks = [bf["wk0"], bf["wk1"]]
        self.wvs = [bf["wv0"], bf["wv1"]]
        self.wgen = [bf["wgen0"], bf["wgen1"]]
        self.rg_sb, self.ru_sb = bf["rg_sb"], bf["ru_sb"]
        self.srg_sb, self.sru_sb, self.srdu_sb = bf["srg_sb"], bf["sru_sb"], bf["srdu_sb"]
        self.weT, self.gwsb, self.rdT_sb = bf["weT"], bf["gwsb"], bf["rdT_sb"]
        self.sCg_sb, self.sCu_sb, self.sRd_sb = bf["sCg_sb"], bf["sCu_sb"], bf["sRd_sb"]
        self.sU_sb, self.sCd_sb, self.sgb = bf["sU_sb"], bf["sCd_sb"], bf["sgb"]
        self.hsg, self.hsu = bf["hsg"], bf["hsu"]
        self.lgf, self.lgtm, self.wetm = bf["lgf"], bf["lgtm"], bf["wetm"]
        self.smX = bf["smX"]
        self.hg_sb, self.hu_sb, self.rd_sb, self.wb_sb = bf["hg_sb"], bf["hu_sb"], bf["rd_sb"], bf["wb_sb"]
        self.gblk, self.mblk = bf["gblk"], bf["mblk"]
        self.eW = [dict(ug=bf[f"ug{i}"], cg=bf[f"cg{i}"], uu=bf[f"uu{i}"], cu=bf[f"cu{i}"], ud=bf[f"ud{i}"]) for i in range(2)]
        self.cdW = [bf["cdw0"], bf["cdw1"]]
        self.OASM = bf["OASM"]
        self.xt = [bf["xtb0"], bf["xtb1"]]
        self.lnt = [bf["lnt0"], bf["lnt1"]]
        self.ln2t = [bf["ln2t0"], bf["ln2t1"]]
        self.xtf = [bf["xtf0"], bf["xtf1"]]
        self.h1t = [bf["h1t0"], bf["h1t1"]]
        self.strip = [bf["strip0"], bf["strip1"]]
        self.mstrip = [bf["mstrip0"], bf["mstrip1"]]
        self.msh = [bf["msh0"], bf["msh1"]]
        sb = lambda name, shape, dt=BF16: self.es.enter_context(nc.sbuf_tensor(name, shape, dt))
        self.g128 = sb("s_g128", [128, NHP], F32)
        self.bdp = sb("s_bdp", [128, 128])
        self.idb = sb("s_idb", [128, 128])
        self.idf = sb("s_idf", [128, 128], F32)
        self.sel64 = sb("s_sel64", [128, 2], F32)
        self.rep2 = sb("s_rep2", [2, 128], F32)
        self.gnw2 = sb("s_gnw2", [2, NHP], F32)
        self.gnb2 = sb("s_gnb2", [2, NHP], F32)
        self.mAb = sb("s_mA", [128, 1], F32)
        self.mBb = sb("s_mB", [128, 1], F32)
        self.S_a = sb("s_Sa", [128, 1024], F32)
        self.S_b = sb("s_Sb", [128, 1024], F32)
        self.S_hist = sb("s_Shist", [128, NCH * 64])
        self.stats = sb("s_stats", [128, NHP * 120], F32)
        self.mv_all = sb("s_mv", [128, 2 * NHP], F32)
        self.st3 = sb("s_st3", [128, 3 * NHP], F32)
        self.fin2 = sb("s_fin2", [2, 4 * NHP], F32)
        self.A_vec = sb("s_Avec", [128, NHP], F32)
        self.C_vec = sb("s_Cvec", [128, NHP], F32)
        self.sm1 = sb("s_sm1", [128, 8], F32)
        self.sm4 = sb("s_sm4", [2, 3 * NHP], F32)
        self.lst = sb("l_lst", [128, 24], F32)
        self.ssep = sb("s_ssep", [128, 2], F32)
        self.lmv = sb("l_lmv", [128, 2], F32)
        self.PF = [self.es.enter_context(nc.psum_tensor(f"p_f{i}", [128, 1024], F32)) for i in range(3)]
        self.PBT = self.es.enter_context(nc.psum_tensor("p_bt", [128, 2048], BF16))
        for n in ("qd", "qg", "te", "ve", "se", "os"):
            setattr(self, n, Sem(self.es.enter_context(nc.semaphore("sem_" + n))))

    def vsep(self):
        self.op("vector", lambda v: v.tensor_copy(self.ssep[:, 0:1], self.ssep[:, 1:2]))

    PK4 = ("pf0a", "pf0b", "pf1a", "pf1b")

    def pf_half(self, sp):
        pf = self.PF[sp // 2]
        return pf[:, (sp % 2) * 512:(sp % 2 + 1) * 512], ("pf0a", "pf0b", "pf1a", "pf1b")[sp]

    def emit_ln(self, xbuf, outbuf, first_wait, out_wait):
        p, I, W = self.op, self.I, self.W
        ve, se = self.ve, self.se
        sm1 = self.sm1
        p("vector", lambda v, a=xbuf: v.tensor_reduce(sm1[:, 5:6], a[:], mybir.AxisListType.X, AL.add),
          (first_wait,))
        p("scalar", lambda s, a=xbuf: s.activation(self.stagf[:, 0:2048], a[:], AF.Square),
          (first_wait,), [I(se)])
        p("vector", lambda v: v.tensor_reduce(sm1[:, 6:7], self.stagf[:, 0:2048], mybir.AxisListType.X, AL.add),
          (W(se),))
        p("vector", lambda v: v.tensor_scalar(sm1[:, 5:6], sm1[:, 5:6], 1.0 / HID, None, AL.mult))
        p("vector", lambda v: v.tensor_scalar(sm1[:, 6:7], sm1[:, 6:7], 1.0 / HID, None, AL.mult))
        p("vector", lambda v: v.tensor_tensor(sm1[:, 3:4], sm1[:, 5:6], sm1[:, 5:6], AL.mult))
        self.vsep()
        p("vector", lambda v: v.tensor_tensor(sm1[:, 0:1], sm1[:, 6:7], sm1[:, 3:4], AL.subtract))
        self.vsep()
        p("vector", lambda v: v.tensor_scalar(sm1[:, 0:1], sm1[:, 0:1], EPS, None, AL.add), (), [I(ve)])
        p("scalar", lambda s: s.activation(sm1[:, 1:2], sm1[:, 0:1], AF.Sqrt), (W(ve),), [I(se)])
        p("vector", lambda v: v.reciprocal(sm1[:, 2:3], sm1[:, 1:2]), (W(se),), [I(ve)])
        p("scalar", lambda s: s.activation(sm1[:, 7:8], sm1[:, 2:3], AF.Identity), (W(ve),))
        p("scalar", lambda s: s.activation(sm1[:, 3:4], sm1[:, 2:3], AF.Square), (), [I(se)])
        p("vector", lambda v: v.tensor_tensor(sm1[:, 3:4], sm1[:, 3:4], sm1[:, 0:1], AL.mult), (W(se),))
        self.vsep()
        p("vector", lambda v: v.tensor_scalar(sm1[:, 3:4], sm1[:, 3:4], -0.5, 1.5, AL.mult, AL.add))
        self.vsep()
        p("vector", lambda v: v.tensor_tensor(sm1[:, 2:3], sm1[:, 7:8], sm1[:, 3:4], AL.mult))
        self.vsep()
        p("vector", lambda v: v.tensor_tensor(sm1[:, 4:5], sm1[:, 5:6], sm1[:, 2:3], AL.mult))
        self.vsep()
        p("vector", lambda v: v.tensor_scalar(sm1[:, 4:5], sm1[:, 4:5], -1.0, None, AL.mult))
        self.vsep()
        p("vector", lambda v, a=xbuf, o=outbuf: v.tensor_scalar(o[:], a[:], sm1[:, 2:3], sm1[:, 4:5], AL.mult, AL.add),
          (out_wait,), [I(ve)])
        return W(ve)

    def load_lnfm(self, src, rows0, wait):
        p, I, W = self.op, self.I, self.W
        waits = wait if isinstance(wait, tuple) and (not wait or isinstance(wait[0], tuple)) else (wait,)
        for cc in range(16):
            p("sync", lambda s, c=cc: s.dma_start(
                out=self.BIGA[:, c * HF:(c + 1) * HF],
                in_=src[rows0:rows0 + HF, c * 128:(c + 1) * 128], transpose=True),
              waits if cc == 0 else (), [I(self.qd, 16)])
        return W(self.qd)

    def load_rot(self, half0, which, waits=()):
        p, I, W = self.op, self.I, self.W
        for i, (name, buf) in enumerate(which):
            p("gpsimd", lambda g, n=name, bf=buf: g.dma_start(out=bf[:], in_=self.D[n][:, half0:half0 + HF]),
              waits if i == 0 else (), [I(self.qg, 16)])
        return W(self.qg)

    def wslice(self, dst, wname, hp, waits=()):
        p, I, W = self.op, self.I, self.W
        p("gpsimd", lambda g, d=dst, n=wname, h=hp: g.dma_start(
            out=d[:].rearrange("p (c f) -> p c f", f=128),
            in_=self.D[n][:, h * 128:(h + 1) * 128].rearrange("(c p) f -> p c f", p=128)),
          waits, [I(self.qg, 16)])
        return W(self.qg)

    def proj(self, wbuf, waits):
        p, I, W = self.op, self.I, self.W
        ws = self.mmw(self.PK4, waits)
        for fi in range(16):
            for sp in range(4):
                tgt, _ = self.pf_half(sp)
                p("tensor", lambda t, tg=tgt, w=wbuf, fi=fi, sp=sp: t.matmul(
                    tg, w[:, fi * 128:(fi + 1) * 128],
                    self.BIGA[:, fi * HF + sp * 512: fi * HF + sp * 512 + 512],
                    start=(fi == 0), stop=(fi == 15)),
                  ws if (fi == 0 and sp == 0) else (),
                  [I(self.te)] if (fi == 15 and sp == 3) else ())
        return W(self.te)

    def evac2(self, eng, dstbuf, wait):
        p, I, W = self.op, self.I, self.W
        sem = self.ve if eng == "vector" else self.se
        for h in range(2):
            if eng == "vector":
                p(eng, lambda E, d=dstbuf, h=h: E.tensor_copy(d[:, h * 1024:(h + 1) * 1024], self.PF[h][:, :]),
                  (wait,) if h == 0 else (), [I(sem)] if h == 1 else [])
            else:
                p(eng, lambda E, d=dstbuf, h=h: E.copy(d[:, h * 1024:(h + 1) * 1024], self.PF[h][:, :]),
                  (wait,) if h == 0 else (), [I(sem)] if h == 1 else [])
        w = W(sem)
        self.setg(self.PK4, w)
        return w

    def rot_apply(self, rawbuf, outbuf, ctab, stab, wait_raw):
        p, I, W = self.op, self.I, self.W
        ws = self.mmw(self.PK4, (wait_raw, self.const_done))
        for sp in range(4):
            tgt, _ = self.pf_half(sp)
            p("tensor", lambda t, tg=tgt, r=rawbuf, sp=sp: t.matmul(
                tg, self.bdp[:], r[:, sp * 512:(sp + 1) * 512], start=True, stop=True),
              ws if sp == 0 else (), [I(self.te)] if sp == 3 else ())
        wte = W(self.te)
        for h in range(2):
            p("vector", lambda v, h=h, st=stab: v.tensor_tensor(
                self.t2[:, h * 1024:(h + 1) * 1024], self.PF[h][:, :], st[:, h * 1024:(h + 1) * 1024], AL.mult),
              (wte,) if h == 0 else (), [I(self.ve)] if h == 1 else [])
        self.setg(self.PK4, W(self.ve))
        p("vector", lambda v, r=rawbuf, o=outbuf, ct=ctab: v.tensor_tensor(o[:], r[:], ct[:], AL.mult))
        p("vector", lambda v, o=outbuf: v.tensor_tensor(o[:], o[:], self.t2[:], AL.add), (), [I(self.ve)])
        return W(self.ve)

    def transposes_to(self, outbuf, srcbuf, wait_src, eta_hp=None):
        p, I, W = self.op, self.I, self.W
        ws = self.mmw(("pbt",), (wait_src, self.const_done))
        for cc in range(16):
            p("tensor", lambda t, c=cc, sb_=srcbuf: t.transpose(
                self.PBT[:, c * 128:(c + 1) * 128], sb_[:, c * 128:(c + 1) * 128], self.idb[:]),
              ws if cc == 0 else (), [I(self.te)] if cc == 15 else ())
        wte = W(self.te)
        if eta_hp is None:
            p("vector", lambda v, o=outbuf: v.tensor_copy(o[:], self.PBT[:, :]), (wte,), [I(self.ve)])
        else:
            p("vector", lambda v, o=outbuf, hp=eta_hp: v.tensor_tensor(
                o[:].rearrange("p (c f) -> p c f", f=128),
                self.PBT[:, :].rearrange("p (c f) -> p c f", f=128),
                self.etb[:, None, hp * 128:(hp + 1) * 128].broadcast_to((128, 16, 128)), AL.mult),
              (wte,), [I(self.ve)])
        self.setg(("pbt",), W(self.ve))
        return W(self.ve)

    def state_mm4(self, waits):
        p, I, W = self.op, self.I, self.W
        ws = self.mmw(("pf2a", "pf2b"), waits)
        for cc in range(NCH):
            for w in range(2):
                p("tensor", lambda t, c=cc, w=w: t.matmul(
                    self.PF[2][w * 64:(w + 1) * 64, c * 64:(c + 1) * 64],
                    self.keta[:, c * 128 + w * 64: c * 128 + w * 64 + 64],
                    self.vtm[:, c * 128 + w * 64: c * 128 + w * 64 + 64],
                    start=True, stop=True, tile_position=(0, w * 64)),
                  ws if (cc == 0 and w == 0) else (),
                  [I(self.te)] if (cc == NCH - 1 and w == 1) else ())
        return W(self.te)

    def state_chain(self, hp, wait_mm4, hist):
        p, I, W = self.op, self.I, self.W
        bufs = [self.S_a, self.S_b]
        for cc in range(NCH):
            src, dst = bufs[cc % 2], bufs[(cc + 1) % 2]
            if hist:
                p("vector", lambda v, c=cc, s=src, hp=hp: v.tensor_copy(
                    self.S_hist[:, c * 64:(c + 1) * 64], s[:, hp * 64:(hp + 1) * 64]))
            p("vector", lambda v, c=cc, s=src, d=dst, hp=hp: v.scalar_tensor_tensor(
                d[:, hp * 64:(hp + 1) * 64], s[:, hp * 64:(hp + 1) * 64], self.g128[:, hp:hp + 1],
                self.PF[2][:, c * 64:(c + 1) * 64], AL.mult, AL.add),
              (wait_mm4,) if cc == 0 else (),
              [I(self.ve)] if cc == NCH - 1 else [])
        self.setg(("pf2a", "pf2b"), W(self.ve))
        return W(self.ve)

    def retention_chunks(self, hp, waits_ready, do_spill, stats_off, spill_wait=None):
        p, I, W = self.op, self.I, self.W
        wv_hist = W(self.ve)
        for cc in range(NCH):
            par = cc % 2
            ak = ("pf0a", "pf0b")[par]
            bk = ("pf1a", "pf1b")[par]
            gk = ("pf2a", "pf2b")[par]
            atA = self.PF[0][:, par * 512: par * 512 + 128]
            atB = self.PF[1][:, par * 512: par * 512 + 128]
            crB = self.PF[1][:, par * 512 + 128: par * 512 + 256]
            rg = self.PF[2][:, par * 512: par * 512 + 128]
            ws = self.mmw((ak,), tuple(waits_ready) + (wv_hist,) if cc == 0 else ())
            p("tensor", lambda t, c=cc, a=atA: t.matmul(
                a, self.kp[0:64, c * 128:(c + 1) * 128], self.qp[0:64, c * 128:(c + 1) * 128],
                start=True, stop=True), ws, [I(self.te)])
            ws = self.mmw((bk,))
            p("tensor", lambda t, c=cc, a=atB: t.matmul(
                a, self.kp[64:128, c * 128:(c + 1) * 128], self.qp[64:128, c * 128:(c + 1) * 128],
                start=True, stop=True), ws, [I(self.te)])
            w_at = W(self.te)
            p("vector", lambda v, c=cc, a=atA, hp=hp: v.tensor_tensor(
                self.t2[:, (c % 4) * 256:(c % 4) * 256 + 128], a, self.dtb[:, hp * 256:hp * 256 + 128], AL.mult),
              (w_at,), [I(self.ve)])
            self.setg((ak,), W(self.ve))
            p("vector", lambda v, c=cc, a=atB, hp=hp: v.tensor_tensor(
                self.t2[:, (c % 4) * 256 + 128:(c % 4) * 256 + 256], a, self.dtb[:, hp * 256 + 128:hp * 256 + 256], AL.mult),
              (), [I(self.ve)])
            w_atd = W(self.ve)
            ws2 = self.mmw((gk,), (w_atd,))
            p("tensor", lambda t, c=cc, r=rg: t.matmul(
                r[0:64, :], self.vtm[:, c * 128: c * 128 + 64],
                self.t2[:, (c % 4) * 256:(c % 4) * 256 + 128],
                start=True, stop=False, tile_position=(0, 0), skip_group_check=True), ws2)
            p("tensor", lambda t, c=cc, r=rg: t.matmul(
                r[64:128, :], self.vtm[:, c * 128 + 64: c * 128 + 128],
                self.t2[:, (c % 4) * 256 + 128:(c % 4) * 256 + 256],
                start=True, stop=True, tile_position=(0, 64), skip_group_check=True))
            p("tensor", lambda t, c=cc, r=rg: t.matmul(
                r[0:64, :], self.S_hist[0:64, c * 64:(c + 1) * 64],
                self.xiq[0:64, c * 128:(c + 1) * 128],
                start=False, stop=True, tile_position=(0, 0), skip_group_check=True),
              (), [I(self.te)])
            p("tensor", lambda t, c=cc, a=crB: t.matmul(
                a[64:128, :], self.S_hist[64:128, c * 64:(c + 1) * 64],
                self.xiq[64:128, c * 128:(c + 1) * 128],
                start=True, stop=True, tile_position=(64, 64), skip_group_check=True),
              (), [I(self.te)])
            w_ret = W(self.te)
            p("vector", lambda v, c=cc, r=rg: v.tensor_copy(self.stagf[:, c * 128:(c + 1) * 128], r),
              (w_ret, spill_wait) if (cc == 0 and do_spill) else (w_ret,), [I(self.ve)])
            self.setg((gk,), W(self.ve))
            p("vector", lambda v, c=cc, a=crB: v.tensor_tensor(
                self.stagf[64:128, c * 128:(c + 1) * 128], a[64:128, :],
                self.stagf[64:128, c * 128:(c + 1) * 128], AL.add), (), [I(self.ve)])
            self.setg((bk,), W(self.ve))
            if not do_spill:
                p("vector", lambda v, c=cc, so=stats_off: v.bn_stats(
                    self.stats[:, so + c * 6: so + (c + 1) * 6], self.stagf[:, c * 128:(c + 1) * 128]),
                  (), [I(self.ve)])
        if do_spill:
            for i in range(4):
                p("vector", lambda v, i=i, so=stats_off: v.bn_stats(
                    self.stats[:, so + i * 6: so + (i + 1) * 6], self.stagf[:, i * 512:(i + 1) * 512]))
            p("vector", lambda v: v.tensor_copy(self.sm1[:, 5:6], self.stagf[:, 0:1]), (), [I(self.ve)])
        return W(self.ve)

    def build(self):
        p, I, W = self.op, self.I, self.W
        qd, qg, te, ve, se, os_ = self.qd, self.qg, self.te, self.ve, self.se, self.os
        D = self.D

        for dst, src in ((self.dtb, D["dtb"]), (self.xib, D["xib"]), (self.etb, D["etb"]),
                         (self.bdp, D["bdp"]), (self.idb, D["iden"]),
                         (self.g128, D["g128"]), (self.idf, D["iden"]), (self.sel64, D["sel64"]),
                         (self.rep2, D["rep2"]), (self.gnw2, D["gnw2"]), (self.gnb2, D["gnb2"]),
                         (self.mAb, D["mA"]), (self.mBb, D["mB"])):
            ndim = len(src.shape)
            p("gpsimd", lambda g, a=dst, b=src: g.dma_start(out=a[:], in_=b[:, :] if len(b.shape) == 2 else b[:, :, :]),
              (), [I(qg, 16)])
        self.const_done = W(qg)

        tile_order = list(range(16, 32)) + list(range(16))
        lnw = {}
        for idx, t in enumerate(tile_order):
            bi = idx % 2
            p("gpsimd", lambda g, b=bi, t=t: g.dma_start(out=self.xt[b][:], in_=D["x"][t * 128:(t + 1) * 128, :]),
              (W(ve),) if idx >= 2 else (), [I(qg, 16)])
            w_apply = self.emit_ln(self.xt[bi], self.lnt[bi], W(qg), W(qg) if idx >= 2 else None)
            p("gpsimd", lambda g, b=bi, t=t: g.dma_start(out=self.ln1_sp[t * 128:(t + 1) * 128, :], in_=self.lnt[b][:]),
              (w_apply,), [I(qg, 16)])
            if KDEBUG.get("p0dbg"):
                p("gpsimd", lambda g, t=t: g.dma_start(out=self.taps["p0dbg"][:, t * 8:(t + 1) * 8], in_=self.sm1[:]),
                  (), [I(qg, 16)])
                p("gpsimd", lambda g, t=t: g.dma_start(out=self.taps["p0dbg2"][:, t * 4:t * 4 + 2], in_=self.lmv[:]),
                  (), [I(qg, 16)])
                p("gpsimd", lambda g, t=t: g.dma_start(out=self.taps["p0dbg2"][:, t * 4 + 2:t * 4 + 4], in_=self.sm1[:, 5:7]),
                  (), [I(qg, 16)])
            if KDEBUG.get("ln1"):
                p("gpsimd", lambda g, b=bi, t=t: g.dma_start(out=self.taps["ln1"][t * 128:(t + 1) * 128, :], in_=self.lnt[b][:]),
                  (), [I(qg, 16)])
            lnw[t] = W(qg)
        ln_pb_done, ln_pa_done = lnw[31], lnw[15]
        if KPHASES[0] < 1:
            return self._finish_early()

        w_lnfm = self.load_lnfm(self.ln1_sp, HF, ln_pb_done)
        w_rot = self.load_rot(HF, (("ck", self.ckb), ("sk", self.skb)))
        p("vector", lambda v: v.memset(self.S_a[:], 0.0), (), [I(ve)])
        for hp in range(NHP):
            bi = hp % 2
            wk_w = self.wslice(self.wks[bi], "wkT", hp, (W(te),) if hp >= 2 else ())
            wv_w = self.wslice(self.wvs[bi], "wvT", hp)
            wte = self.proj(self.wks[bi], (wk_w, w_lnfm if hp == 0 else None,
                                           w_rot if hp == 0 else None,
                                           self.const_done if hp == 0 else None))
            w_kraw = self.evac2("scalar", self.kraw, wte)
            w_kp = self.rot_apply(self.kraw, self.kp, self.ckb, self.skb, w_kraw)
            wte_v = self.proj(self.wvs[bi], (wv_w,))
            w_vfm = self.evac2("vector", self.vfm, wte_v)
            w_vtm = self.transposes_to(self.vtm, self.vfm, w_vfm)
            w_keta = self.transposes_to(self.keta, self.kp, w_kp, eta_hp=hp)
            p("gpsimd", lambda g, hp=hp: g.dma_start(out=self.kv_sp[hp, 0, :, :], in_=self.kp[:]), (w_kp,), [I(qg, 16)])
            p("gpsimd", lambda g, hp=hp: g.dma_start(out=self.kv_sp[hp, 1, :, :], in_=self.vtm[:]), (w_vtm,), [I(qg, 16)])
            w_mm4 = self.state_mm4((w_keta, w_vtm))
            self.state_chain(hp, w_mm4, hist=False)
        if KDEBUG.get("spb"):
            p("gpsimd", lambda g: g.dma_start(out=self.taps["spb"][:, :], in_=self.S_a[:]), (W(ve),), [I(qg, 16)])
        if KPHASES[0] < 2:
            return self._finish_early()

        p("vector", lambda v: v.tensor_scalar(self.S_a[:], self.S_a[:], self.mAb[:, 0:1], None, AL.mult),
          (self.const_done,), [I(ve)])
        w_lnfm = self.load_lnfm(self.ln1_sp, 0, (ln_pa_done, W(te)))
        w_rot = self.load_rot(0, (("cq", self.cqb), ("sq", self.sqb), ("ck", self.ckb), ("sk", self.skb)),
                              waits=(W(ve),))
        for hp in range(NHP):
            bi = hp % 2
            wq_w = self.wslice(self.wqs[bi], "wqT", hp, (W(te),) if hp >= 2 else (ln_pa_done, W(te)))
            wk_w = self.wslice(self.wks[bi], "wkT", hp)
            wv_w = self.wslice(self.wvs[bi], "wvT", hp)
            wte = self.proj(self.wqs[bi], (wq_w, w_lnfm if hp == 0 else None, w_rot if hp == 0 else None))
            w_qraw = self.evac2("scalar", self.qraw, wte)
            w_qp = self.rot_apply(self.qraw, self.qp, self.cqb, self.sqb, w_qraw)
            p("vector", lambda v, hp=hp: v.tensor_tensor(
                self.xiq[:].rearrange("p (c f) -> p c f", f=128),
                self.qp[:].rearrange("p (c f) -> p c f", f=128),
                self.xib[:, None, hp * 128:(hp + 1) * 128].broadcast_to((128, 16, 128)), AL.mult),
              (w_qp,), [I(ve)])
            w_xiq = W(ve)
            wte = self.proj(self.wks[bi], (wk_w,))
            w_kraw = self.evac2("scalar", self.kraw, wte)
            w_kp = self.rot_apply(self.kraw, self.kp, self.ckb, self.skb, w_kraw)
            wte_v = self.proj(self.wvs[bi], (wv_w,))
            w_vfm = self.evac2("vector", self.vfm, wte_v)
            w_vtm = self.transposes_to(self.vtm, self.vfm, w_vfm)
            w_keta = self.transposes_to(self.keta, self.kp, w_kp, eta_hp=hp)
            w_mm4 = self.state_mm4((w_keta, w_vtm))
            self.state_chain(hp, w_mm4, hist=True)
            w_rc = self.retention_chunks(hp, (w_xiq, w_keta), do_spill=True, stats_off=hp * 120,
                                         spill_wait=W(self.qd) if hp > 0 else None)
            p("sync", lambda s, hp=hp: s.dma_start(out=self.ret_sp[hp, :, :], in_=self.stagf[:]), (w_rc,), [I(qd, 16)])
            if KDEBUG.get("ret"):
                p("gpsimd", lambda g, hp=hp: g.dma_start(out=self.taps["ret"][hp, :, :], in_=self.stagf[:]), (w_rc,), [I(qg, 16)])
        w_ret_spill = W(qd)
        if KPHASES[0] < 3:
            return self._finish_early()

        p("vector", lambda v: v.tensor_scalar(self.S_a[:], self.S_a[:], self.mBb[:, 0:1], None, AL.mult), (), [I(ve)])
        w_lnfm = self.load_lnfm(self.ln1_sp, HF, W(te))
        w_rot = self.load_rot(HF, (("cq", self.cqb), ("sq", self.sqb)), waits=(W(ve),))
        for hp in range(NHP):
            bi = hp % 2
            wq_w = self.wslice(self.wqs[bi], "wqT", hp, (W(te),) if hp >= 2 else ())
            p("sync", lambda s, hp=hp: s.dma_start(out=self.kp[:], in_=self.kv_sp[hp, 0, :, :]),
              (W(te),), [I(qd, 16)])
            p("sync", lambda s, hp=hp: s.dma_start(out=self.vtm[:], in_=self.kv_sp[hp, 1, :, :]), (), [I(qd, 16)])
            w_kv = W(qd)
            wte = self.proj(self.wqs[bi], (wq_w, w_lnfm if hp == 0 else None, w_rot if hp == 0 else None))
            w_qraw = self.evac2("scalar", self.qraw, wte)
            w_qp = self.rot_apply(self.qraw, self.qp, self.cqb, self.sqb, w_qraw)
            p("vector", lambda v, hp=hp: v.tensor_tensor(
                self.xiq[:].rearrange("p (c f) -> p c f", f=128),
                self.qp[:].rearrange("p (c f) -> p c f", f=128),
                self.xib[:, None, hp * 128:(hp + 1) * 128].broadcast_to((128, 16, 128)), AL.mult),
              (w_qp,), [I(ve)])
            w_xiq = W(ve)
            w_keta = self.transposes_to(self.keta, self.kp, w_kv, eta_hp=hp)
            w_mm4 = self.state_mm4((w_keta, w_kv))
            self.state_chain(hp, w_mm4, hist=True)
            self.retention_chunks(hp, (w_xiq, w_keta), do_spill=False, stats_off=hp * 120 + 24)

        if KPHASES[0] < 4:
            return self._finish_early()
        self._stats_finalize()
        w_mish = self._mish(ln_pa_done)
        w_h1_all = self._wo_phase(w_mish, w_ret_spill)
        if KPHASES[0] < 5:
            return self._finish_early()
        self._ffn(w_h1_all)

        return self._emit()

    def _finish_early(self):
        return self._emit()

    def _emit(self):
        p, W = self.op, self.W
        p("sync", None, (W(self.qd), W(self.qg), W(self.te), W(self.ve), W(self.se), W(self.os)))
        blk = self.es.enter_context(self.nc.Block())

        def mk(eng_name):
            def f(eng):
                for fn, waits, incs in self.EL[eng_name]:
                    for (h, v) in waits:
                        eng.wait_ge(h, v)
                    inst = fn(eng) if fn is not None else None
                    if inst is not None:
                        for (h, n) in incs:
                            inst.then_inc(h, n)
            return f

        blk.sync(mk("sync"))
        blk.tensor(mk("tensor"))
        blk.vector(mk("vector"))
        blk.scalar(mk("scalar"))
        blk.gpsimd(mk("gpsimd"))
        return self.nc

    def _stats_finalize(self):
        p, I, W = self.op, self.I, self.W
        te, ve, se = self.te, self.ve, self.se
        st3, fin2, sm4, sm1 = self.st3, self.fin2, self.sm4, self.sm1
        scr = self.stagf
        for hp in range(NHP):
            b0 = hp * 120
            p("vector", lambda v, b0=b0: v.tensor_reduce(sm1[:, 0:1], self.stats[:, b0 + 1:b0 + 24:3], mybir.AxisListType.X, AL.add))
            p("vector", lambda v, b0=b0: v.tensor_reduce(sm1[:, 1:2], self.stats[:, b0 + 25:b0 + 120:3], mybir.AxisListType.X, AL.add))
            p("vector", lambda v, b0=b0: v.tensor_reduce(sm1[:, 2:3], self.stats[:, b0 + 2:b0 + 24:3], mybir.AxisListType.X, AL.add))
            p("vector", lambda v, b0=b0: v.tensor_reduce(sm1[:, 3:4], self.stats[:, b0 + 26:b0 + 120:3], mybir.AxisListType.X, AL.add))
            p("vector", lambda v, b0=b0: v.tensor_tensor(scr[:, 1100:1108], self.stats[:, b0 + 1:b0 + 24:3], self.stats[:, b0 + 1:b0 + 24:3], AL.mult))
            p("vector", lambda v, b0=b0: v.tensor_tensor(scr[:, 1108:1140], self.stats[:, b0 + 25:b0 + 120:3], self.stats[:, b0 + 25:b0 + 120:3], AL.mult))
            p("vector", lambda v: v.tensor_reduce(sm1[:, 4:5], scr[:, 1100:1108], mybir.AxisListType.X, AL.add))
            p("vector", lambda v: v.tensor_reduce(sm1[:, 5:6], scr[:, 1108:1140], mybir.AxisListType.X, AL.add))
            p("vector", lambda v: v.tensor_scalar(sm1[:, 1:2], sm1[:, 1:2], 64.0, None, AL.mult))
            self.vsep()
            p("vector", lambda v: v.scalar_tensor_tensor(sm1[:, 0:1], sm1[:, 0:1], 256.0, sm1[:, 1:2], AL.mult, AL.add))
            self.vsep()
            p("vector", lambda v, hp=hp: v.tensor_scalar(st3[:, hp:hp + 1], sm1[:, 0:1], 1.0 / (S * HD / 64), None, AL.mult))
            p("vector", lambda v: v.tensor_tensor(sm1[:, 2:3], sm1[:, 2:3], sm1[:, 3:4], AL.add))
            self.vsep()
            p("vector", lambda v: v.scalar_tensor_tensor(sm1[:, 2:3], sm1[:, 4:5], 256.0, sm1[:, 2:3], AL.mult, AL.add))
            self.vsep()
            p("vector", lambda v: v.scalar_tensor_tensor(sm1[:, 2:3], sm1[:, 5:6], 64.0, sm1[:, 2:3], AL.mult, AL.add))
            self.vsep()
            p("vector", lambda v: v.tensor_scalar(sm1[:, 2:3], sm1[:, 2:3], 1.0 / (S * HD / 64), None, AL.mult))
            p("vector", lambda v, hp=hp: v.tensor_tensor(scr[:, 1140:1141], st3[:, hp:hp + 1], st3[:, hp:hp + 1], AL.mult))
            self.vsep()
            p("vector", lambda v, hp=hp: v.tensor_tensor(st3[:, NHP + hp:NHP + hp + 1], sm1[:, 2:3], scr[:, 1140:1141], AL.subtract))
            p("vector", lambda v, hp=hp: v.tensor_copy(st3[:, 2 * NHP + hp:2 * NHP + hp + 1], scr[:, 1140:1141]))
        p("vector", lambda v: v.engine_nop() if False else v.tensor_copy(sm1[:, 6:7], st3[:, 0:1]), (), [I(ve)])
        p("tensor", lambda t: t.matmul(self.PF[2][0:2, 0:3 * NHP], self.sel64[:], st3[:], start=True, stop=True),
          self.mmw(("pf2a",), (W(ve), self.const_done)), [I(te)])
        p("vector", lambda v: v.tensor_copy(fin2[:, 0:3 * NHP], self.PF[2][0:2, 0:3 * NHP]), (W(te),), [I(ve)])
        self.setg(("pf2a", "pf2b"), W(ve))
        self.vsep()
        p("vector", lambda v: v.tensor_scalar(fin2[:, 0:3 * NHP], fin2[:, 0:3 * NHP], 1.0 / 64, None, AL.mult))
        self.vsep()
        p("vector", lambda v: v.tensor_tensor(sm4[:, 0:NHP], fin2[:, 0:NHP], fin2[:, 0:NHP], AL.mult))
        p("vector", lambda v: v.tensor_tensor(fin2[:, NHP:2 * NHP], fin2[:, NHP:2 * NHP], fin2[:, 2 * NHP:3 * NHP], AL.add))
        self.vsep()
        p("vector", lambda v: v.tensor_tensor(fin2[:, NHP:2 * NHP], fin2[:, NHP:2 * NHP], sm4[:, 0:NHP], AL.subtract))
        self.vsep()
        p("vector", lambda v: v.tensor_scalar(fin2[:, NHP:2 * NHP], fin2[:, NHP:2 * NHP], EPS, None, AL.add), (), [I(ve)])
        p("scalar", lambda s: s.activation(sm4[:, NHP:2 * NHP], fin2[:, NHP:2 * NHP], AF.Sqrt), (W(ve),), [I(se)])
        p("vector", lambda v: v.reciprocal(sm4[:, 0:NHP], sm4[:, NHP:2 * NHP]), (W(se),), [I(ve)])
        p("scalar", lambda s: s.activation(sm4[:, 2 * NHP:3 * NHP], sm4[:, 0:NHP], AF.Identity), (W(ve),))
        p("scalar", lambda s: s.activation(sm4[:, NHP:2 * NHP], sm4[:, 0:NHP], AF.Square), (), [I(se)])
        p("vector", lambda v: v.tensor_tensor(sm4[:, NHP:2 * NHP], sm4[:, NHP:2 * NHP], fin2[:, NHP:2 * NHP], AL.mult), (W(se),))
        self.vsep()
        p("vector", lambda v: v.tensor_scalar(sm4[:, NHP:2 * NHP], sm4[:, NHP:2 * NHP], -0.5, 1.5, AL.mult, AL.add))
        self.vsep()
        p("vector", lambda v: v.tensor_tensor(sm4[:, 0:NHP], sm4[:, 2 * NHP:3 * NHP], sm4[:, NHP:2 * NHP], AL.mult))
        self.vsep()
        p("vector", lambda v: v.tensor_tensor(fin2[:, 3 * NHP:4 * NHP], self.gnw2[:], sm4[:, 0:NHP], AL.mult))
        self.vsep()
        p("vector", lambda v: v.tensor_tensor(sm4[:, NHP:2 * NHP], fin2[:, 0:NHP], fin2[:, 3 * NHP:4 * NHP], AL.mult))
        self.vsep()
        p("vector", lambda v: v.tensor_tensor(fin2[:, 0:NHP], self.gnb2[:], sm4[:, NHP:2 * NHP], AL.subtract), (), [I(ve)])
        p("tensor", lambda t: t.matmul(self.PF[2][:, 0:NHP], self.rep2[:], fin2[0:2, 3 * NHP:4 * NHP], start=True, stop=True),
          self.mmw(("pf2a",), (W(ve),)))
        p("tensor", lambda t: t.matmul(self.PF[2][:, 512:512 + NHP], self.rep2[:], fin2[0:2, 0:NHP], start=True, stop=True),
          self.mmw(("pf2b",)), [I(te)])
        p("vector", lambda v: v.tensor_copy(self.A_vec[:], self.PF[2][:, 0:NHP]), (W(te),))
        p("vector", lambda v: v.tensor_copy(self.C_vec[:], self.PF[2][:, 512:512 + NHP]), (), [I(ve)])
        self.setg(("pf2a", "pf2b"), W(ve))
        if KDEBUG.get("gn"):
            p("gpsimd", lambda g: g.dma_start(out=self.taps["gn"][:, 0:NHP], in_=self.A_vec[:]), (W(ve),), [I(self.qg, 16)])
            p("gpsimd", lambda g: g.dma_start(out=self.taps["gn"][:, NHP:2 * NHP], in_=self.C_vec[:]), (), [I(self.qg, 16)])

    def _mish(self, ln_pa_done):
        p, I, W = self.op, self.I, self.W
        te, ve, se, qg = self.te, self.ve, self.se, self.qg
        w_lnfm = self.load_lnfm(self.ln1_sp, 0, W(self.te))
        keys3 = (("pf0a", "pf0b"), ("pf1a", "pf1b"), ("pf2a", "pf2b"))
        for fo in range(16):
            bi = fo % 2
            wg_w = self.wslice(self.wgen[bi], "wgT", fo, (W(te),))
            for half in range(2):
                mi = (fo * 2 + half) % 2
                k3 = keys3[(fo * 2 + half) % 3]
                pf = self.PF[(fo * 2 + half) % 3]
                ws = self.mmw(k3, (wg_w, w_lnfm if fo == 0 else None))
                for fi in range(16):
                    for sp in range(2):
                        p("tensor", lambda t, pf=pf, w=self.wgen[bi], fi=fi, sp=sp, half=half: t.matmul(
                            pf[:, sp * 512:(sp + 1) * 512], w[:, fi * 128:(fi + 1) * 128],
                            self.BIGA[:, fi * HF + half * 1024 + sp * 512: fi * HF + half * 1024 + sp * 512 + 512],
                            start=(fi == 0), stop=(fi == 15)),
                          ws if (fi == 0 and sp == 0) else (),
                          [I(te)] if (fi == 15 and sp == 1) else ())
                wte = W(te)
                p("scalar", lambda s, pf=pf: s.activation(self.stagf[:, 0:1024], pf[:, :], AF.Exp), (wte,))
                p("scalar", lambda s: s.activation(self.stagf[:, 0:1024], self.stagf[:, 0:1024], AF.Ln, bias=1.0))
                p("scalar", lambda s: s.activation(self.stagf[:, 0:1024], self.stagf[:, 0:1024], AF.Tanh), (), [I(se)])
                p("vector", lambda v, pf=pf, mi=mi: v.tensor_tensor(
                    self.msh[mi][:], pf[:, :], self.stagf[:, 0:1024], AL.mult),
                  (W(se), W(qg, self.qg.v)), [I(ve)])
                self.setg(k3, W(ve))
                p("gpsimd", lambda g, fo=fo, half=half, mi=mi: g.dma_start(
                    out=self.mish_sp[fo, :, half * 1024:(half + 1) * 1024], in_=self.msh[mi][:]),
                  (W(ve),), [I(qg, 16)])
        return W(qg)

    def _wo_phase(self, w_mish, w_ret_spill):
        p, I, W = self.op, self.I, self.W
        qd, qg, te, ve = self.qd, self.qg, self.te, self.ve
        w_free = W(te)
        for cc in range(16):
            p("gpsimd", lambda g, c=cc: g.dma_start(
                out=self.BIGA[:, c * HF:(c + 1) * HF], in_=self.D["woT"][c * 128:(c + 1) * 128, :]),
              (w_free,) if cc == 0 else (), [I(qg, 16)])
        w_wo = W(qg)
        for tt in range(16):
            bi = tt % 2
            p("sync", lambda s, tt=tt, b=bi: s.dma_start(
                out=self.strip[b][:].rearrange("p (h f) -> p h f", f=128),
                in_=self.ret_sp[:, :, tt * 128:(tt + 1) * 128].rearrange("h p f -> p h f")),
              (W(ve),) if tt >= 2 else (w_ret_spill, W(te), W(ve)), [I(qd, 16)])
            p("sync", lambda s, tt=tt, b=bi: s.dma_start(
                out=self.mstrip[b][:].rearrange("p (h f) -> p h f", f=128),
                in_=self.mish_sp[:, :, tt * 128:(tt + 1) * 128].rearrange("h p f -> p h f")),
              (w_mish, W(te)) if tt <= 1 else (), [I(qd, 16)])
            p("sync", lambda s, tt=tt, b=bi: s.dma_start(out=self.xtf[b][:], in_=self.D["x"][tt * 128:(tt + 1) * 128, :]),
              (W(te), W(ve)) if tt <= 1 else (), [I(qd, 16)])
            w_strip = W(qd)
            for hp in range(NHP):
                p("vector", lambda v, b=bi, hp=hp: v.tensor_scalar(
                    self.stagf[:, hp * 128:(hp + 1) * 128],
                    self.strip[b][:, hp * 128:(hp + 1) * 128],
                    self.A_vec[:, hp:hp + 1], self.C_vec[:, hp:hp + 1], AL.mult, AL.add),
                  (w_strip,) if hp == 0 else ())
            p("vector", lambda v, b=bi: v.tensor_tensor(
                self.qp[:], self.stagf[:, 0:2048], self.mstrip[b][:], AL.mult), (), [I(ve)])
            w_z = W(ve)
            ws = self.mmw(self.PK4, (w_z, w_wo if tt == 0 else None))
            for fi in range(16):
                for sp in range(4):
                    tgt, _ = self.pf_half(sp)
                    p("tensor", lambda t, tg=tgt, fi=fi, sp=sp: t.matmul(
                        tg, self.qp[:, fi * 128:(fi + 1) * 128],
                        self.BIGA[:, fi * HF + sp * 512: fi * HF + sp * 512 + 512],
                        start=(fi == 0), stop=(fi == 15)),
                      ws if (fi == 0 and sp == 0) else (),
                      [I(te)] if (fi == 15 and sp == 3) else ())
            wte = W(te)
            for h in range(2):
                p("vector", lambda v, b=bi, h=h: v.tensor_tensor(
                    self.h1t[b][:, h * 1024:(h + 1) * 1024], self.PF[h][:, :],
                    self.xtf[b][:, h * 1024:(h + 1) * 1024], AL.add),
                  (wte,) if h == 0 else (), [I(ve)] if h == 1 else [])
            self.setg(self.PK4, W(ve))
            w_apply = self.emit_ln(self.h1t[bi], self.ln2t[bi], W(ve), W(qg) if tt >= 2 else None)
            p("gpsimd", lambda g, b=bi, tt=tt: g.dma_start(out=self.h1_sp[tt * 128:(tt + 1) * 128, :], in_=self.h1t[b][:]),
              (w_apply,), [I(qg, 16)])
            p("gpsimd", lambda g, b=bi, tt=tt: g.dma_start(out=self.ln2_sp[tt * 128:(tt + 1) * 128, :], in_=self.ln2t[b][:]),
              (), [I(qg, 16)])
            if KDEBUG.get("h1"):
                p("gpsimd", lambda g, b=bi, tt=tt: g.dma_start(out=self.taps["h1"][tt * 128:(tt + 1) * 128, :], in_=self.h1t[b][:]),
                  (), [I(qg, 16)])
        return W(qg)

    def _ffn(self, w_h1_all):
        p, I, W = self.op, self.I, self.W
        qd, qg, te, ve, se = self.qd, self.qg, self.te, self.ve, self.se
        D = self.D
        w_ln2 = self.load_lnfm(self.ln2_sp, 0, w_h1_all)
        p("gpsimd", lambda g: g.dma_start(
            out=self.gwsb[:].rearrange("p (c f) -> p c f", f=32),
            in_=D["gwT"].rearrange("(c p) f -> p c f", p=128)), (W(te), W(self.ve)), [I(qg, 16)])
        p("gpsimd", lambda g: g.dma_start(
            out=self.rdT_sb[:].rearrange("p (c f) -> p c f", f=128),
            in_=D["RdT"].rearrange("(c p) f -> p c f", p=128)), (), [I(qg, 16)])
        p("gpsimd", lambda g: g.dma_start(out=self.sCg_sb[:], in_=D["sCgT"][:, :]), (), [I(qg, 16)])
        p("gpsimd", lambda g: g.dma_start(out=self.sCu_sb[:], in_=D["sCuT"][:, :]), (), [I(qg, 16)])
        p("gpsimd", lambda g: g.dma_start(
            out=self.sRd_sb[:].rearrange("p (c f) -> p c f", f=128),
            in_=D["sRdT"].rearrange("(c p) f -> p c f", p=128)), (), [I(qg, 16)])
        p("gpsimd", lambda g: g.dma_start(out=self.sU_sb[:, 0:128], in_=D["sUgT"][:, :]), (), [I(qg, 16)])
        p("gpsimd", lambda g: g.dma_start(out=self.sU_sb[:, 128:256], in_=D["sUuT"][:, :]), (), [I(qg, 16)])
        p("gpsimd", lambda g: g.dma_start(out=self.sU_sb[:, 256:384], in_=D["sUdT"][:, :]), (), [I(qg, 16)])
        p("gpsimd", lambda g: g.dma_start(out=self.sCd_sb[:], in_=D["sCdT"][:, :]), (), [I(qg, 16)])
        p("gpsimd", lambda g: g.dma_start(out=self.sel_e[:], in_=D["sel_e"][:, :]), (), [I(qg, 16)])
        w_small = W(qg)

        def projL(dst, wname, extra=()):
            wl = self.wslice(self.wgen[0], wname, 0, (W(te),))
            wte = self.proj(self.wgen[0], (wl,) + tuple(extra))
            return self.evac2("vector", dst, wte)

        w_rg = projL(self.rg_sb, "RgT", (w_ln2,))
        w_ru = projL(self.ru_sb, "RuT")
        w_srg = projL(self.srg_sb, "sRgT")
        w_sru = projL(self.sru_sb, "sRuT")
        if KDEBUG.get("rg"):
            p("gpsimd", lambda g: g.dma_start(out=self.taps["rg"][:, :], in_=self.rg_sb[:]), (w_rg,), [I(qg, 16)])

        for pas in range(2):
            ws = self.mmw(("pf2a", "pf2b"), (w_small,) if pas == 0 else ())
            for fi in range(16):
                for sp in range(2):
                    p("tensor", lambda t, fi=fi, sp=sp, pas=pas: t.matmul(
                        self.PF[2][0:32, sp * 512:sp * 512 + 512],
                        self.gwsb[:, fi * 32:(fi + 1) * 32],
                        self.BIGA[:, fi * HF + pas * 1024 + sp * 512: fi * HF + pas * 1024 + sp * 512 + 512],
                        start=(fi == 0), stop=(fi == 15)),
                      ws if (fi == 0 and sp == 0) else (),
                      [I(te)] if (fi == 15 and sp == 1) else ())
            p("vector", lambda v, pas=pas: v.tensor_copy(self.lgf[:, pas * 1024:(pas + 1) * 1024], self.PF[2][0:32, :]),
              (W(te),), [I(ve)])
            self.setg(("pf2a", "pf2b"), W(ve))
        w_lg = W(ve)

        ws = self.mmw(("pf2a", "pf2b"), (w_lg,))
        for tl in range(16):
            p("tensor", lambda t, tl=tl: t.transpose(
                self.PF[2][:, tl * 32:(tl + 1) * 32],
                self.lgf[:, tl * 128:(tl + 1) * 128], self.idf[0:32, 0:32]),
              ws if tl == 0 else (), [I(te)] if tl == 15 else ())
        p("vector", lambda v: v.tensor_copy(self.lgtm[:], self.PF[2][:, 0:512]), (W(te),), [I(ve)])
        self.setg(("pf2a", "pf2b"), W(ve))

        sm1 = self.sm1
        for tl in range(16):
            lg = self.lgtm[:, tl * 32:(tl + 1) * 32]
            wetile = self.wetm[:, tl * 32:(tl + 1) * 32]
            msk = self.stagf[:, 64:96]
            p("vector", lambda v, lg=lg: v.tensor_reduce(sm1[:, 0:1], lg, mybir.AxisListType.X, AL.max))
            p("vector", lambda v, lg=lg, we=wetile: v.tensor_copy(we, lg))
            for rnd in range(3):
                mcol = sm1[:, 0:1] if rnd == 0 else sm1[:, 1:2]
                self.vsep()
                p("vector", lambda v, we=wetile, mc=mcol, mk=msk: v.tensor_scalar(mk, we, mc[:, 0:1], None, AL.is_ge))
                self.vsep()
                p("vector", lambda v, we=wetile, mk=msk: v.scalar_tensor_tensor(we, mk, -1e30, we, AL.mult, AL.add))
                self.vsep()
                p("vector", lambda v, we=wetile: v.tensor_reduce(sm1[:, 1:2], we, mybir.AxisListType.X, AL.max))
            self.vsep()
            p("vector", lambda v: v.tensor_scalar(sm1[:, 2:3], sm1[:, 0:1], -1.0, None, AL.mult))
            self.vsep()
            p("vector", lambda v, lg=lg: v.tensor_scalar(self.stagf[:, 0:32], lg, sm1[:, 2:3], None, AL.add), (), [I(ve)])
            p("scalar", lambda s: s.activation(self.stagf[:, 32:64], self.stagf[:, 0:32], AF.Exp), (W(ve),), [I(se)])
            p("vector", lambda v: v.tensor_reduce(sm1[:, 3:4], self.stagf[:, 32:64], mybir.AxisListType.X, AL.add), (W(se),))
            p("vector", lambda v, lg=lg, mk=msk: v.tensor_scalar(mk, lg, sm1[:, 1:2], None, AL.is_ge))
            self.vsep()
            p("vector", lambda v, mk=msk: v.tensor_tensor(self.stagf[:, 32:64], self.stagf[:, 32:64], mk, AL.mult))
            self.vsep()
            p("vector", lambda v: v.tensor_reduce(sm1[:, 4:5], self.stagf[:, 32:64], mybir.AxisListType.X, AL.add))
            self.vsep()
            p("vector", lambda v: v.scalar_tensor_tensor(sm1[:, 4:5], sm1[:, 3:4], 1e-20, sm1[:, 4:5], AL.mult, AL.add))
            p("vector", lambda v: v.reciprocal(sm1[:, 5:6], sm1[:, 4:5]), (), [I(ve)])
            p("scalar", lambda s: s.activation(sm1[:, 6:7], sm1[:, 5:6], AF.Identity), (W(ve),), [I(se)])
            p("vector", lambda v, we=wetile: v.tensor_scalar(we, self.stagf[:, 32:64], sm1[:, 6:7], None, AL.mult), (W(se),), [I(ve)])
        w_we = W(ve)
        if KDEBUG.get("we"):
            p("gpsimd", lambda g: g.dma_start(out=self.taps["we"][:, :], in_=self.wetm[:]), (w_we,), [I(qg, 16)])

        for grp in range(4):
            ws = self.mmw(("pf2a", "pf2b"), (w_we,) if grp == 0 else ())
            for q in range(4):
                tl = grp * 4 + q
                p("tensor", lambda t, tl=tl, q=q: t.transpose(
                    self.PF[2][0:32, q * 128:(q + 1) * 128],
                    self.wetm[:, tl * 32:(tl + 1) * 32], self.idf[:]),
                  ws if q == 0 else (), [I(te)] if q == 3 else ())
            p("vector", lambda v, grp=grp: v.tensor_copy(self.weT[:, grp * 512:(grp + 1) * 512], self.PF[2][0:32, 0:512]),
              (W(te),), [I(ve)])
            self.setg(("pf2a", "pf2b"), W(ve))
        w_weT = W(ve)

        def k128(dst, lhsT, rhsbuf, waits):
            ws = self.mmw(self.PK4, waits)
            for sp in range(4):
                tgt, _ = self.pf_half(sp)
                p("tensor", lambda t, tg=tgt, l=lhsT, r=rhsbuf, sp=sp: t.matmul(
                    tg, l, r[:, sp * 512:(sp + 1) * 512], start=True, stop=True),
                  ws if sp == 0 else (), [I(te)] if sp == 3 else ())
            return self.evac2("vector", dst, W(te))

        w_hsg = k128(self.hsg, self.sU_sb[:, 0:128], self.srg_sb, (w_srg, w_small))
        w_hsu = k128(self.hsu, self.sU_sb[:, 128:256], self.sru_sb, (w_sru,))

        def smblk(ft):
            return (self.BIGA[:, ft * HF:(ft + 1) * HF] if ft < 16
                    else self.smX[:, (ft - 16) * HF:(ft - 15) * HF])

        for ft in range(NSI):
            ws = self.mmw(self.PK4, (w_hsg, w_lg) if ft == 0 else ())
            for sp in range(4):
                tgt, _ = self.pf_half(sp)
                p("tensor", lambda t, tg=tgt, ft=ft, sp=sp: t.matmul(
                    tg, self.sCg_sb[:, ft * 128:(ft + 1) * 128],
                    self.hsg[:, sp * 512:(sp + 1) * 512], start=True, stop=True),
                  ws if sp == 0 else (), [I(te)] if sp == 3 else ())
            wte = W(te)
            for h in range(2):
                p("scalar", lambda s, h=h: s.activation(self.sgb[:, h * 1024:(h + 1) * 1024], self.PF[h][:, :], AF.Silu),
                  (wte,) if h == 0 else (), [I(se)] if h == 1 else [])
            self.setg(self.PK4, W(se))
            w_sg = W(se)
            ws = self.mmw(self.PK4)
            for sp in range(4):
                tgt, _ = self.pf_half(sp)
                p("tensor", lambda t, tg=tgt, ft=ft, sp=sp: t.matmul(
                    tg, self.sCu_sb[:, ft * 128:(ft + 1) * 128],
                    self.hsu[:, sp * 512:(sp + 1) * 512], start=True, stop=True),
                  ws if sp == 0 else (), [I(te)] if sp == 3 else ())
            wte = W(te)
            for h in range(2):
                p("vector", lambda v, ft=ft, h=h: v.tensor_tensor(
                    smblk(ft)[:, h * 1024:(h + 1) * 1024], self.PF[h][:, :],
                    self.sgb[:, h * 1024:(h + 1) * 1024], AL.mult),
                  (wte, w_sg) if h == 0 else (), [I(ve)] if h == 1 else [])
            self.setg(self.PK4, W(ve))
        ws = self.mmw(self.PK4, (W(ve),))
        for ki in range(NSI):
            for sp in range(4):
                tgt, _ = self.pf_half(sp)
                p("tensor", lambda t, tg=tgt, ki=ki, sp=sp: t.matmul(
                    tg, self.sRd_sb[:, ki * 128:(ki + 1) * 128],
                    smblk(ki)[:, sp * 512:(sp + 1) * 512],
                    start=(ki == 0), stop=(ki == NSI - 1)),
                  ws if (ki == 0 and sp == 0) else (),
                  [I(te)] if (ki == NSI - 1 and sp == 3) else ())
        w_srd = self.evac2("vector", self.sru_sb, W(te))
        w_srdu = k128(self.srdu_sb, self.sU_sb[:, 256:384], self.sru_sb, (w_srd,))
        if KDEBUG.get("srdu"):
            p("gpsimd", lambda g: g.dma_start(out=self.taps["srdu"][:, :], in_=self.srdu_sb[:]), (w_srdu,), [I(qg, 16)])

        for sp_i in range(NSPAN):
            self._moe_span(sp_i, w_weT, w_rg, w_ru)
            self._out_span(sp_i)

    def _moe_span(self, sp_i, w_weT, w_rg, w_ru):
        p, I, W = self.op, self.I, self.W
        qg, te, ve, se = self.qg, self.te, self.ve, self.se
        D = self.D
        t0 = sp_i * SPAN
        for e in range(NE):
            bi = e % 2
            ew = self.eW[bi]
            wld = (W(te), W(ve))
            p("gpsimd", lambda g, ew=ew, e=e: g.dma_start(out=ew["ug"][:], in_=D["UgT"][e, :, :]), wld, [I(qg, 16)])
            p("gpsimd", lambda g, ew=ew, e=e: g.dma_start(out=ew["cg"][:], in_=D["CgT"][e, :, :]), (), [I(qg, 16)])
            p("gpsimd", lambda g, ew=ew, e=e: g.dma_start(out=ew["uu"][:], in_=D["UuT"][e, :, :]), (), [I(qg, 16)])
            p("gpsimd", lambda g, ew=ew, e=e: g.dma_start(out=ew["cu"][:], in_=D["CuT"][e, :, :]), (), [I(qg, 16)])
            p("gpsimd", lambda g, ew=ew, e=e: g.dma_start(out=ew["ud"][:], in_=D["UdT"][e, :, :]), (), [I(qg, 16)])
            w_ew = W(qg)
            ws = self.mmw(("pf2a",), (w_weT,) if (e == 0 and sp_i == 0) else ())
            p("tensor", lambda t, e=e: t.matmul(
                self.PF[2][:, 0:512], self.sel_e[:, e * 128:(e + 1) * 128],
                self.weT[:, t0:t0 + SPAN], start=True, stop=True), ws, [I(te)])
            p("scalar", lambda s: s.copy(self.wb_sb[:], self.PF[2][:, 0:512]), (W(te),), [I(se)])
            self.setg(("pf2a",), W(se))
            w_wb = W(se)
            ws = self.mmw(("pf0a",), (w_ew, w_rg) if (e == 0 and sp_i == 0) else (w_ew,))
            p("tensor", lambda t, ew=ew: t.matmul(
                self.PF[0][:, 0:512], ew["ug"][:], self.rg_sb[:, t0:t0 + SPAN],
                start=True, stop=True), ws, [I(te)])
            p("scalar", lambda s: s.copy(self.hg_sb[:], self.PF[0][:, 0:512]), (W(te),), [I(se)])
            self.setg(("pf0a",), W(se))
            w_hg = W(se)
            ws = self.mmw(("pf1a",), (w_ru,) if (e == 0 and sp_i == 0) else ())
            p("tensor", lambda t, ew=ew: t.matmul(
                self.PF[1][:, 0:512], ew["uu"][:], self.ru_sb[:, t0:t0 + SPAN],
                start=True, stop=True), ws, [I(te)])
            p("vector", lambda v: v.tensor_tensor(self.hu_sb[:], self.PF[1][:, 0:512], self.wb_sb[:], AL.mult),
              (W(te), w_wb), [I(ve)])
            self.setg(("pf1a",), W(ve))
            w_hu = W(ve)
            for ft in range(NI):
                ws = self.mmw(("pf0b",) if ft % 2 == 0 else ("pf0a",), (w_hg,) if ft == 0 else ())
                cgreg = self.PF[0][:, 512:1024] if ft % 2 == 0 else self.PF[0][:, 0:512]
                cgk = ("pf0b",) if ft % 2 == 0 else ("pf0a",)
                p("tensor", lambda t, ew=ew, ft=ft, r=cgreg: t.matmul(
                    r, ew["cg"][:, ft * 128:(ft + 1) * 128], self.hg_sb[:],
                    start=True, stop=True), ws, [I(te)])
                p("scalar", lambda s, r=cgreg: s.activation(self.gblk[:], r, AF.Silu), (W(te),), [I(se)])
                self.setg(cgk, W(se))
                w_g = W(se)
                ws = self.mmw(("pf1b",) if ft % 2 == 0 else ("pf1a",), (w_hu,) if ft == 0 else ())
                cureg = self.PF[1][:, 512:1024] if ft % 2 == 0 else self.PF[1][:, 0:512]
                cuk = ("pf1b",) if ft % 2 == 0 else ("pf1a",)
                p("tensor", lambda t, ew=ew, ft=ft, r=cureg: t.matmul(
                    r, ew["cu"][:, ft * 128:(ft + 1) * 128], self.hu_sb[:],
                    start=True, stop=True), ws, [I(te)])
                p("vector", lambda v, r=cureg: v.tensor_tensor(self.mblk[:], r, self.gblk[:], AL.mult),
                  (W(te), w_g), [I(ve)])
                self.setg(cuk, W(ve))
                w_m = W(ve)
                ws2 = self.mmw(("pf2b",), (w_m,)) if ft == 0 else (w_m,)
                p("tensor", lambda t, ft=ft: t.matmul(
                    self.PF[2][:, 512:1024], self.rdT_sb[:, ft * 128:(ft + 1) * 128], self.mblk[:],
                    start=(ft == 0), stop=(ft == NI - 1)),
                  ws2 if ft == 0 else (w_m,), [I(te)] if ft == NI - 1 else ())
            p("scalar", lambda s: s.copy(self.rd_sb[:], self.PF[2][:, 512:1024]), (W(te),), [I(se)])
            self.setg(("pf2b",), W(se))
            w_rd = W(se)
            ws = self.mmw(("pf0a",), (w_rd,))
            p("tensor", lambda t, ew=ew: t.matmul(
                self.PF[0][:, 0:512], ew["ud"][:], self.rd_sb[:], start=True, stop=True), ws, [I(te)])
            p("vector", lambda v, e=e: v.tensor_copy(self.BIGA[:, e * SPAN:(e + 1) * SPAN], self.PF[0][:, 0:512]),
              (W(te),), [I(ve)])
            self.setg(("pf0a",), W(ve))
        if KDEBUG.get("rdu") and sp_i == 0:
            p("gpsimd", lambda g: g.dma_start(out=self.taps["rdu"][:, :], in_=self.BIGA[:, 0:NE * SPAN]), (W(ve),), [I(qg, 16)])

    def _out_span(self, sp_i):
        p, I, W = self.op, self.I, self.W
        qd, qg, te, ve, os_ = self.qd, self.qg, self.te, self.ve, self.os
        D = self.D
        w_rdu = W(self.ve)
        w_oasm_free = W(os_)
        for fo in range(16):
            bi = fo % 2
            p("gpsimd", lambda g, fo=fo, b=bi: g.dma_start(
                out=self.cdW[b][:].rearrange("p (e f) -> p e f", f=128),
                in_=D["CdT"][fo].rearrange("e p f -> p e f")),
              (W(te), W(ve)), [I(qg, 16)])
            w_cd = W(qg)
            pk = ("pf0a", "pf0b", "pf1a", "pf1b")[fo % 4]
            pf = self.PF[(fo % 4) // 2]
            reg = pf[:, (fo % 2) * 512:(fo % 2) * 512 + 512]
            ws = self.mmw((pk,), (w_cd, w_rdu) if fo == 0 else (w_cd,))
            for e in range(NE + 1):
                if e < NE:
                    lhsT = self.cdW[bi][:, e * 128:(e + 1) * 128]
                    rhs = self.BIGA[:, e * SPAN: (e + 1) * SPAN]
                else:
                    lhsT = self.sCd_sb[:, fo * 128:(fo + 1) * 128]
                    rhs = self.srdu_sb[:, sp_i * SPAN: (sp_i + 1) * SPAN]
                p("tensor", lambda t, l=lhsT, r=rhs, e=e, reg=reg: t.matmul(
                    reg, l, r, start=(e == 0), stop=(e == NE)),
                  ws if e == 0 else (), [I(te)] if e == NE else ())
            wte = W(te)
            st = self.stagf[:, (fo % 2) * 512:(fo % 2) * 512 + 512]
            p("vector", lambda v, st=st, reg=reg: v.tensor_copy(st, reg), (wte,), [I(ve)])
            self.setg((pk,), W(ve))
            w_st = W(ve)
            tk = "pf2a" if fo % 2 == 0 else "pf2b"
            treg = self.PF[2][:, 0:512] if fo % 2 == 0 else self.PF[2][:, 512:1024]
            ws2 = self.mmw((tk,), (w_st,))
            for q in range(4):
                p("tensor", lambda t, st=st, q=q, treg=treg: t.transpose(
                    treg[:, q * 128:(q + 1) * 128], st[:, q * 128:(q + 1) * 128], self.idf[:]),
                  ws2 if q == 0 else (), [I(te)] if q == 3 else ())
            wtt = W(te)
            for q in range(4):
                p("vector", lambda v, q=q, fo=fo, treg=treg: v.tensor_copy(
                    self.OASM[:, q * HID + fo * 128: q * HID + (fo + 1) * 128],
                    treg[:, q * 128:(q + 1) * 128]),
                  (wtt, w_oasm_free) if (q == 0 and fo == 0) else ((wtt,) if q == 0 else ()),
                  [I(ve)] if q == 3 else [])
            self.setg((tk,), W(ve))
        w_asm = W(ve)
        for q in range(4):
            row = sp_i * SPAN + q * 128
            p("sync", lambda s, row=row: s.dma_start(out=self.h1t[0][:], in_=self.h1_sp[row:row + 128, :]),
              (W(ve),), [I(qd, 16)])
            p("vector", lambda v, q=q: v.tensor_tensor(
                self.OASM[:, q * HID:(q + 1) * HID], self.OASM[:, q * HID:(q + 1) * HID],
                self.h1t[0][:], AL.add), (W(qd), w_asm), [I(ve)])
            p("sync", lambda s, row=row, q=q: s.dma_start(out=self.out[row:row + 128, :],
                                                          in_=self.OASM[:, q * HID:(q + 1) * HID]),
              (W(ve),), [I(os_, 16)])


def _build_cached():
    key = "nc"
    if key not in _CACHE:
        b = B()
        nc = b.build()
        _CACHE[key] = (nc, b.taps, b)
    return _CACHE[key]


def kernel(**inputs):
    nc, taps, _b = _build_cached()
    shared = _prep_shared(inputs)
    rot = _rot_tables()
    in_maps = [_prep_core(inputs, shared, rot, c) for c in range(8)]
    res = run_bass_kernel_spmd(nc, in_maps, core_ids=list(range(8)))
    out = np.empty((4, S, HID), np.float32)
    for c in range(8):
        bb, own = c // 2, c % 2
        out[bb, own * HF:(own + 1) * HF] = res.results[c]["out"]
    if KDEBUG:
        kernel.last_taps = [{k: res.results[c]["tap_" + k] for k in taps} for c in range(8)]
    return out


# version 03
# speedup vs baseline: 1.2032x; 1.2032x over previous
import contextlib
import numpy as np
import concourse.bass as bass
import concourse.mybir as mybir
from concourse.bass_utils import run_bass_kernel_spmd
from concourse.alu_op_type import AluOpType as AL

F32 = mybir.dt.float32
BF16 = mybir.dt.bfloat16
AF = mybir.ActivationFunctionType

HID = 2048
HD = 64
NH = 32
NHP = 16
ROT = 32
S = 4096
HF = 2048
C = 128
NCH = HF // C
EPS = 1e-5
NE = 32
IFF = 1408
NI = IFF // 128
SI = 2816
NSI = SI // 128
SPAN = 1024
NSPAN = HF // SPAN
THETA = 10000.0
SCALE_BASE = 512.0

KDEBUG = {}
KPHASES = [99]
KSUB = [99]
_CACHE = {}



def _gamma():
    return 1.0 - 2.0 ** (-5.0 - np.arange(NH, dtype=np.float64))


def _rot_tables():
    t = np.arange(S, dtype=np.float64)
    r = np.arange(0, ROT, 2, dtype=np.float64)
    inv_freq = 1.0 / THETA ** (r / ROT)
    freqs = np.repeat(t[:, None] * inv_freq[None, :], 2, axis=-1)
    base_scale = (r + 0.4 * ROT) / (1.4 * ROT)
    power = (t - S // 2) / SCALE_BASE
    scale = np.concatenate([base_scale[None, :] ** power[:, None]] * 2, axis=-1)
    cos = np.cos(freqs); sin = np.sin(freqs)
    sgn = np.where(np.arange(ROT) % 2 == 0, -1.0, 1.0)
    Cq = np.ones((HD, S)); Sq = np.zeros((HD, S))
    Ck = np.ones((HD, S)); Sk = np.zeros((HD, S))
    Cq[:ROT] = (cos * scale).T
    Ck[:ROT] = (cos / scale).T
    Sq[:ROT] = (sin * scale).T * sgn[:, None]
    Sk[:ROT] = (sin / scale).T * sgn[:, None]
    return [np.concatenate([a, a], 0).astype(np.float32) for a in (Cq, Sq, Ck, Sk)]


def _const_tables():
    g = _gamma()
    j = np.arange(C)
    d = j[None, :] - j[:, None]
    p128 = np.arange(128)
    dtb = np.zeros((128, NHP * 256), np.float32)
    for hp in range(NHP):
        for w in range(2):
            D = np.where(d >= 0, np.power(g[2 * hp + w], np.maximum(d, 0)), 0.0)
            dtb[:, hp * 256 + w * 128:hp * 256 + (w + 1) * 128] = D
    xib = np.zeros((128, NHP * 128), np.float32)
    etb = np.zeros((128, NHP * 128), np.float32)
    g128 = np.zeros((128, NHP), np.float32)
    for hp in range(NHP):
        gp = np.where(p128 >= 64, g[2 * hp + 1], g[2 * hp])
        xib[:, hp * 128:(hp + 1) * 128] = np.power(gp[:, None], (j + 1)[None, :])
        etb[:, hp * 128:(hp + 1) * 128] = np.power(gp[None, :], (C - 1 - j)[:, None])
        g128[:, hp] = np.power(gp, C)
    M = np.zeros((HD, HD), np.float32)
    for i in range(ROT // 2):
        M[2 * i, 2 * i + 1] = 1.0
        M[2 * i + 1, 2 * i] = 1.0
    bdp = np.zeros((128, 128), np.float32)
    bdp[:HD, :HD] = M; bdp[HD:, HD:] = M
    sel_e = np.zeros((32, NE * 128), np.float32)
    for e in range(NE):
        sel_e[e, e * 128:(e + 1) * 128] = 1.0
    sel64 = np.zeros((128, 2), np.float32)
    sel64[:64, 0] = 1.0; sel64[64:, 1] = 1.0
    rep2 = np.zeros((2, 128), np.float32)
    rep2[0, :64] = 1.0; rep2[1, 64:] = 1.0
    return dict(dtb=dtb, xib=xib, etb=etb, g128=g128, bdp=bdp,
                iden=np.eye(128, dtype=np.float32), sel_e=sel_e,
                sel64=sel64, rep2=rep2)


def _prep_shared(inp):
    f = np.float32
    ct = np.ascontiguousarray
    wqkv = np.asarray(inp["L0_attn_wqkv"], f)
    w3 = wqkv.reshape(NH, 3, HD, HID)
    d = {}
    d["wqT"] = ct(w3[:, 0].reshape(NH * HD, HID).T)
    d["wkT"] = ct(w3[:, 1].reshape(NH * HD, HID).T)
    d["wvT"] = ct(w3[:, 2].reshape(NH * HD, HID).T)
    d["wgT"] = ct(np.asarray(inp["L0_attn_wg"], f).T)
    d["woT"] = ct(np.asarray(inp["L0_attn_wo"], f).T)
    d["gwT"] = ct(np.asarray(inp["L0_ffn_gate_w"], f).T)
    d["RgT"] = ct(np.asarray(inp["L0_ffn_Rg"], f).T)
    d["RuT"] = ct(np.asarray(inp["L0_ffn_Ru"], f).T)
    d["RdT"] = ct(np.asarray(inp["L0_ffn_Rd"], f).T)
    for nm in ("Ug", "Cg", "Uu", "Cu", "Ud"):
        d[nm + "T"] = ct(np.asarray(inp["L0_ffn_" + nm], f).transpose(0, 2, 1))
    Cd = np.asarray(inp["L0_ffn_Cd"], f)
    d["CdT"] = ct(Cd.reshape(NE, 16, 128, 128).transpose(1, 0, 3, 2))
    for nm in ("s_Rg", "s_Ru", "s_Ug", "s_Uu", "s_Ud", "s_Cg", "s_Cu", "s_Rd", "s_Cd"):
        d[nm.replace("_", "") + "T"] = ct(np.asarray(inp["L0_ffn_" + nm], f).T)
    d["gnw2"] = ct(np.asarray(inp["L0_attn_gn_weight"], f).reshape(NHP, 2).T)
    d["gnb2"] = ct(np.asarray(inp["L0_attn_gn_bias"], f).reshape(NHP, 2).T)
    d.update(_const_tables())
    return d


def _prep_core(inp, shared, rot, core):
    b, own = core // 2, core % 2
    xb = np.asarray(inp["L0_attn_x"], np.float32)[b]
    perm = np.r_[own * HF:(own + 1) * HF, (1 - own) * HF:(2 - own) * HF]
    m = dict(shared)
    m["x"] = np.ascontiguousarray(xb[perm])
    m["cq"] = np.ascontiguousarray(rot[0][:, perm])
    m["sq"] = np.ascontiguousarray(rot[1][:, perm])
    m["ck"] = np.ascontiguousarray(rot[2][:, perm])
    m["sk"] = np.ascontiguousarray(rot[3][:, perm])
    m["mA"] = np.full((128, 1), float(own), np.float32)
    m["mB"] = np.full((128, 1), float(1 - own), np.float32)
    return m



class Sem:
    __slots__ = ("h", "v")

    def __init__(self, h):
        self.h = h
        self.v = 0


class B:
    def __init__(self):
        nc = bass.Bass("TRN2", target_bir_lowering=False)
        self.nc = nc
        self.EL = {k: [] for k in ("sync", "tensor", "vector", "scalar", "gpsimd")}
        self.bankg = {}
        self.es = contextlib.ExitStack()
        self.es.__enter__()
        self.taps = {}
        self._declare()

    def op(self, eng, fn=None, waits=(), incs=()):
        ws = tuple(w for w in waits if w is not None)
        self.EL[eng].append((fn, ws, tuple(incs)))

    def W(self, s, v=None):
        return (s.h, s.v if v is None else v)

    def I(self, s, n=1):
        s.v += n
        return (s.h, n)

    def mmw(self, keys, extra=()):
        ws = [w for w in extra if w is not None]
        seen = set()
        for k in keys:
            g = self.bankg.get(k)
            if g is not None and g not in seen:
                ws.append(g); seen.add(g)
        return ws

    def setg(self, keys, w):
        for k in keys:
            self.bankg[k] = w

    def _declare(self):
        nc = self.nc
        self.D = {}

        def din(name, shape, dt=F32):
            self.D[name] = nc.dram_tensor(name, shape, dt, kind="ExternalInput").ap()

        din("x", [S, HID])
        for n in ("cq", "sq", "ck", "sk"):
            din(n, [128, S])
        for n in ("wqT", "wkT", "wvT", "wgT", "woT"):
            din(n, [HID, HID])
        din("gwT", [HID, 32])
        din("RgT", [HID, 128]); din("RuT", [HID, 128]); din("RdT", [IFF, 128])
        din("UgT", [NE, 128, 128]); din("CgT", [NE, 128, IFF])
        din("UuT", [NE, 128, 128]); din("CuT", [NE, 128, IFF])
        din("UdT", [NE, 128, 128]); din("CdT", [16, NE, 128, 128])
        din("sRgT", [HID, 128]); din("sRuT", [HID, 128])
        din("sUgT", [128, 128]); din("sUuT", [128, 128]); din("sUdT", [128, 128])
        din("sCgT", [128, SI]); din("sCuT", [128, SI])
        din("sRdT", [SI, 128]); din("sCdT", [128, HID])
        din("gnw2", [2, NHP]); din("gnb2", [2, NHP])
        din("dtb", [128, NHP * 256]); din("xib", [128, NHP * 128])
        din("etb", [128, NHP * 128]); din("g128", [128, NHP])
        din("bdp", [128, 128]); din("iden", [128, 128])
        din("sel_e", [32, NE * 128]); din("sel64", [128, 2]); din("rep2", [2, 128])
        din("mA", [128, 1]); din("mB", [128, 1])
        self.out = nc.dram_tensor("out", [HF, HID], F32, kind="ExternalOutput").ap()
        self.ln1_sp = nc.dram_tensor("ln1_sp", [S, HID], BF16, kind="Internal").ap()
        self.ln2_sp = nc.dram_tensor("ln2_sp", [HF, HID], BF16, kind="Internal").ap()
        self.kv_sp = nc.dram_tensor("kv_sp", [NHP, 2, 128, HF], BF16, kind="Internal").ap()
        self.ret_sp = nc.dram_tensor("ret_sp", [NHP, 128, HF], F32, kind="Internal").ap()
        self.h1_sp = nc.dram_tensor("h1_sp", [HF, HID], F32, kind="Internal").ap()
        self.mish_sp = nc.dram_tensor("mish_sp", [16, 128, HF], BF16, kind="Internal").ap()

        if KDEBUG:
            def tap(name, shape, dt=F32):
                self.taps[name] = nc.dram_tensor("tap_" + name, shape, dt, kind="ExternalOutput").ap()
            if KDEBUG.get("ln1"): tap("ln1", [S, HID], BF16)
            if KDEBUG.get("p0dbg"): tap("p0dbg", [128, 32 * 8])
            if KDEBUG.get("p0dbg"): tap("p0dbg2", [128, 32 * 4])
            if KDEBUG.get("spb"): tap("spb", [128, 1024])
            if KDEBUG.get("ret"): tap("ret", [NHP, 128, HF])
            if KDEBUG.get("gn"): tap("gn", [128, 2 * NHP])
            if KDEBUG.get("h1"): tap("h1", [HF, HID])
            if KDEBUG.get("we"): tap("we", [128, NHP * 32])
            if KDEBUG.get("rg"): tap("rg", [128, HF], BF16)
            if KDEBUG.get("srdu"): tap("srdu", [128, HF], BF16)
            if KDEBUG.get("rdu"): tap("rdu", [128, NE * SPAN], BF16)

        ARN_N = 95936
        self.ARN = self.es.enter_context(nc.sbuf_tensor("b_ARN", [128, ARN_N], BF16))
        reqs = []

        def rq(name, units, ph0, ph1, f32=False, nparts=128):
            reqs.append((name, units, f32, ph0, ph1, nparts))

        rq("dtb", NHP * 256, 0, 3)
        rq("xib", NHP * 128, 0, 3)
        rq("etb", NHP * 128, 0, 3)
        rq("sel_e", NE * 128, 5, 7, nparts=32)
        rq("BIGA", 16 * HF, 1, 7)
        for n in ("cqb", "sqb", "ckb", "skb"):
            rq(n, HF, 1, 3)
        for n in ("qraw", "kraw", "vfm", "vtm", "keta", "xiq"):
            rq(n, HF, 1, 3)
        rq("qp", HF, 1, 4)
        rq("kp", HF, 1, 3)
        rq("t2", HF, 1, 4)
        rq("stagf", 2 * HF, 0, 7, f32=True)
        rq("wq0", HF, 2, 3); rq("wq1", HF, 2, 3)
        rq("wk0", HF, 1, 2); rq("wk1", HF, 1, 2)
        rq("wv0", HF, 1, 2); rq("wv1", HF, 1, 2)
        rq("wgen0", HF, 4, 5); rq("wgen1", HF, 4, 5)
        rq("xtb0", HID, 0, 1); rq("xtb1", HID, 0, 1)
        rq("lnt0", HID, 0, 1); rq("lnt1", HID, 0, 1)
        rq("ln2t0", HID, 4, 4); rq("ln2t1", HID, 4, 4)
        rq("xtf0", 2 * HID, 4, 4, f32=True); rq("xtf1", 2 * HID, 4, 4, f32=True)
        rq("h1t0", 2 * HID, 4, 7, f32=True); rq("h1t1", 2 * HID, 4, 4, f32=True)
        rq("strip0", 2 * HID, 4, 4, f32=True); rq("strip1", 2 * HID, 4, 4, f32=True)
        rq("mstrip0", HID, 4, 4); rq("mstrip1", HID, 4, 4)
        rq("msh0", 1024, 4, 4); rq("msh1", 1024, 4, 4)
        rq("rg_sb", HF, 5, 7); rq("ru_sb", HF, 5, 7)
        rq("srg_sb", HF, 5, 6); rq("sru_sb", HF, 5, 6)
        rq("srdu_sb", HF, 5, 7)
        rq("weT", HF, 5, 7, nparts=32)
        rq("gwsb", NHP * 32, 5, 5)
        rq("rdT_sb", NI * 128, 5, 7)
        rq("sCg_sb", SI, 5, 6); rq("sCu_sb", SI, 5, 6)
        rq("sRd_sb", NSI * 128, 5, 6)
        rq("sU_sb", 3 * 128, 5, 6)
        rq("sCd_sb", HID, 5, 7)
        rq("sgb", HF, 6, 6)
        rq("hsg", HF, 6, 6); rq("hsu", HF, 6, 6)
        rq("lgf", 2 * HF, 5, 5, f32=True, nparts=32)
        rq("lgtm", 2 * NHP * 32, 5, 5, f32=True)
        rq("wetm", 2 * NHP * 32, 5, 5, f32=True)
        rq("smX", 6 * HF, 6, 6)
        rq("hg_sb", SPAN, 7, 7); rq("hu_sb", SPAN, 7, 7)
        rq("rd_sb", SPAN, 7, 7); rq("wb_sb", SPAN, 7, 7)
        rq("gblk", SPAN, 7, 7); rq("gblkB", SPAN, 7, 7); rq("mblk", SPAN, 7, 7)
        for i in range(2):
            rq(f"ug{i}", 128, 7, 7); rq(f"cg{i}", IFF, 7, 7)
            rq(f"uu{i}", 128, 7, 7); rq(f"cu{i}", IFF, 7, 7)
            rq(f"ud{i}", 128, 7, 7)
            rq(f"cdw{i}", NE * 128, 7, 7)
        rq("OASM", 2 * 4 * HID, 7, 7, f32=True)
        rq("BIGF", 2 * 1024, 4, 4, f32=True)
        placed = []
        self.buf = {}
        for name, units, f32, ph0, ph1, nparts in sorted(reqs, key=lambda r: -r[1]):
            cand = 0
            while True:
                lo, hi = cand, cand + units
                clash = None
                for (o, u, a0, a1) in placed:
                    if not (hi <= o or o + u <= lo) and not (ph1 < a0 or a1 < ph0):
                        clash = o + u
                        break
                if clash is None:
                    break
                cand = (clash + 15) // 16 * 16
            assert cand + units <= ARN_N, f"arena overflow at {name}: {cand}+{units}"
            placed.append((cand, units, ph0, ph1))
            ap = self.ARN[0:nparts, cand:cand + units]
            if f32:
                ap = ap.bitcast(F32)
            self.buf[name] = ap
        self._arena_peak = max(o + u for (o, u, _, _) in placed)
        bf = self.buf
        self.dtb, self.xib, self.etb, self.sel_e = bf["dtb"], bf["xib"], bf["etb"], bf["sel_e"]
        self.BIGA = bf["BIGA"]
        self.cqb, self.sqb, self.ckb, self.skb = bf["cqb"], bf["sqb"], bf["ckb"], bf["skb"]
        self.qraw, self.kraw, self.qp, self.kp = bf["qraw"], bf["kraw"], bf["qp"], bf["kp"]
        self.t2, self.vfm, self.vtm, self.keta, self.xiq = bf["t2"], bf["vfm"], bf["vtm"], bf["keta"], bf["xiq"]
        self.stagf = bf["stagf"]
        self.wqs = [bf["wq0"], bf["wq1"]]
        self.wks = [bf["wk0"], bf["wk1"]]
        self.wvs = [bf["wv0"], bf["wv1"]]
        self.wgen = [bf["wgen0"], bf["wgen1"]]
        self.rg_sb, self.ru_sb = bf["rg_sb"], bf["ru_sb"]
        self.srg_sb, self.sru_sb, self.srdu_sb = bf["srg_sb"], bf["sru_sb"], bf["srdu_sb"]
        self.weT, self.gwsb, self.rdT_sb = bf["weT"], bf["gwsb"], bf["rdT_sb"]
        self.sCg_sb, self.sCu_sb, self.sRd_sb = bf["sCg_sb"], bf["sCu_sb"], bf["sRd_sb"]
        self.sU_sb, self.sCd_sb, self.sgb = bf["sU_sb"], bf["sCd_sb"], bf["sgb"]
        self.hsg, self.hsu = bf["hsg"], bf["hsu"]
        self.lgf, self.lgtm, self.wetm = bf["lgf"], bf["lgtm"], bf["wetm"]
        self.smX = bf["smX"]
        self.hg_sb, self.hu_sb, self.rd_sb, self.wb_sb = bf["hg_sb"], bf["hu_sb"], bf["rd_sb"], bf["wb_sb"]
        self.gblk, self.mblk = bf["gblk"], bf["mblk"]
        self.gblk2 = [bf["gblk"], bf["gblkB"]]
        self.eW = [dict(ug=bf[f"ug{i}"], cg=bf[f"cg{i}"], uu=bf[f"uu{i}"], cu=bf[f"cu{i}"], ud=bf[f"ud{i}"]) for i in range(2)]
        self.cdW = [bf["cdw0"], bf["cdw1"]]
        self.OASM = bf["OASM"]
        self.BIGF = bf["BIGF"]
        self.xt = [bf["xtb0"], bf["xtb1"]]
        self.lnt = [bf["lnt0"], bf["lnt1"]]
        self.ln2t = [bf["ln2t0"], bf["ln2t1"]]
        self.xtf = [bf["xtf0"], bf["xtf1"]]
        self.h1t = [bf["h1t0"], bf["h1t1"]]
        self.strip = [bf["strip0"], bf["strip1"]]
        self.mstrip = [bf["mstrip0"], bf["mstrip1"]]
        self.msh = [bf["msh0"], bf["msh1"]]
        sb = lambda name, shape, dt=BF16: self.es.enter_context(nc.sbuf_tensor(name, shape, dt))
        self.g128 = sb("s_g128", [128, NHP], F32)
        self.bdp = sb("s_bdp", [128, 128])
        self.idb = sb("s_idb", [128, 128])
        self.idf = sb("s_idf", [128, 128], F32)
        self.sel64 = sb("s_sel64", [128, 2], F32)
        self.rep2 = sb("s_rep2", [2, 128], F32)
        self.gnw2 = sb("s_gnw2", [2, NHP], F32)
        self.gnb2 = sb("s_gnb2", [2, NHP], F32)
        self.mAb = sb("s_mA", [128, 1], F32)
        self.mBb = sb("s_mB", [128, 1], F32)
        self.S_a = sb("s_Sa", [128, 1024], F32)
        self.S_b = sb("s_Sb", [128, 1024], F32)
        self.S_hist = sb("s_Shist", [128, NCH * 64])
        self.stats = sb("s_stats", [128, NHP * 120], F32)
        self.mv_all = sb("s_mv", [128, 2 * NHP], F32)
        self.st3 = sb("s_st3", [128, 3 * NHP], F32)
        self.fin2 = sb("s_fin2", [2, 4 * NHP], F32)
        self.A_vec = sb("s_Avec", [128, NHP], F32)
        self.C_vec = sb("s_Cvec", [128, NHP], F32)
        self.sm1 = sb("s_sm1", [128, 8], F32)
        self.sm4 = sb("s_sm4", [2, 3 * NHP], F32)
        self.lst = sb("l_lst", [128, 24], F32)
        self.ssep = sb("s_ssep", [128, 2], F32)
        self.lmv = sb("l_lmv", [128, 2], F32)
        self.PF = [self.es.enter_context(nc.psum_tensor(f"p_f{i}", [128, 1024], F32)) for i in range(3)]
        self.PBT = self.es.enter_context(nc.psum_tensor("p_bt", [128, 2048], BF16))
        self.PBTF = self.PBT[:, :].bitcast(F32)
        for n in ("qd", "qg", "te", "ve", "se", "os"):
            setattr(self, n, Sem(self.es.enter_context(nc.semaphore("sem_" + n))))

    def vsep(self):
        self.op("vector", lambda v: v.tensor_copy(self.ssep[:, 0:1], self.ssep[:, 1:2]))

    PK4 = ("pf0a", "pf0b", "pf1a", "pf1b")

    def pf_half(self, sp, bs="A"):
        if bs == "A":
            pf = self.PF[sp // 2]
            return pf[:, (sp % 2) * 512:(sp % 2 + 1) * 512], ("pf0a", "pf0b", "pf1a", "pf1b")[sp]
        src_ = self.PF[2] if sp < 2 else self.PBTF
        return src_[:, (sp % 2) * 512:(sp % 2 + 1) * 512], ("pf2a", "pf2b", "pbt", "pbt")[sp]

    def bs_keys(self, bs):
        return self.PK4 if bs == "A" else ("pf2a", "pf2b", "pbt")

    def bs_buf(self, h, bs):
        return self.PF[h] if bs == "A" else (self.PF[2] if h == 0 else self.PBTF)

    def emit_ln(self, xbuf, outbuf, first_wait, out_wait):
        p, I, W = self.op, self.I, self.W
        ve, se = self.ve, self.se
        sm1 = self.sm1
        p("vector", lambda v, a=xbuf: v.tensor_reduce(sm1[:, 5:6], a[:], mybir.AxisListType.X, AL.add),
          (first_wait,))
        p("scalar", lambda s, a=xbuf: s.activation(self.stagf[:, 0:2048], a[:], AF.Square),
          (first_wait,), [I(se)])
        p("vector", lambda v: v.tensor_reduce(sm1[:, 6:7], self.stagf[:, 0:2048], mybir.AxisListType.X, AL.add),
          (W(se),))
        p("vector", lambda v: v.tensor_scalar(sm1[:, 5:6], sm1[:, 5:6], 1.0 / HID, None, AL.mult))
        p("vector", lambda v: v.tensor_scalar(sm1[:, 6:7], sm1[:, 6:7], 1.0 / HID, None, AL.mult))
        p("vector", lambda v: v.tensor_tensor(sm1[:, 3:4], sm1[:, 5:6], sm1[:, 5:6], AL.mult))
        self.vsep()
        p("vector", lambda v: v.tensor_tensor(sm1[:, 0:1], sm1[:, 6:7], sm1[:, 3:4], AL.subtract))
        self.vsep()
        p("vector", lambda v: v.tensor_scalar(sm1[:, 0:1], sm1[:, 0:1], EPS, None, AL.add), (), [I(ve)])
        p("scalar", lambda s: s.activation(sm1[:, 1:2], sm1[:, 0:1], AF.Sqrt), (W(ve),), [I(se)])
        p("vector", lambda v: v.reciprocal(sm1[:, 2:3], sm1[:, 1:2]), (W(se),), [I(ve)])
        p("scalar", lambda s: s.activation(sm1[:, 7:8], sm1[:, 2:3], AF.Identity), (W(ve),))
        p("scalar", lambda s: s.activation(sm1[:, 3:4], sm1[:, 2:3], AF.Square), (), [I(se)])
        p("vector", lambda v: v.tensor_tensor(sm1[:, 3:4], sm1[:, 3:4], sm1[:, 0:1], AL.mult), (W(se),))
        self.vsep()
        p("vector", lambda v: v.tensor_scalar(sm1[:, 3:4], sm1[:, 3:4], -0.5, 1.5, AL.mult, AL.add))
        self.vsep()
        p("vector", lambda v: v.tensor_tensor(sm1[:, 2:3], sm1[:, 7:8], sm1[:, 3:4], AL.mult))
        self.vsep()
        p("vector", lambda v: v.tensor_tensor(sm1[:, 4:5], sm1[:, 5:6], sm1[:, 2:3], AL.mult))
        self.vsep()
        p("vector", lambda v: v.tensor_scalar(sm1[:, 4:5], sm1[:, 4:5], -1.0, None, AL.mult))
        self.vsep()
        p("vector", lambda v, a=xbuf, o=outbuf: v.tensor_scalar(o[:], a[:], sm1[:, 2:3], sm1[:, 4:5], AL.mult, AL.add),
          (out_wait,), [I(ve)])
        return W(ve)

    def load_lnfm(self, src, rows0, wait):
        p, I, W = self.op, self.I, self.W
        waits = wait if isinstance(wait, tuple) and (not wait or isinstance(wait[0], tuple)) else (wait,)
        for cc in range(16):
            p("sync", lambda s, c=cc: s.dma_start(
                out=self.BIGA[:, c * HF:(c + 1) * HF],
                in_=src[rows0:rows0 + HF, c * 128:(c + 1) * 128], transpose=True),
              waits if cc == 0 else (), [I(self.qd, 16)])
        return W(self.qd)

    def load_rot(self, half0, which, waits=()):
        p, I, W = self.op, self.I, self.W
        for i, (name, buf) in enumerate(which):
            p("gpsimd", lambda g, n=name, bf=buf: g.dma_start(out=bf[:], in_=self.D[n][:, half0:half0 + HF]),
              waits if i == 0 else (), [I(self.qg, 16)])
        return W(self.qg)

    def wslice(self, dst, wname, hp, waits=()):
        p, I, W = self.op, self.I, self.W
        p("gpsimd", lambda g, d=dst, n=wname, h=hp: g.dma_start(
            out=d[:].rearrange("p (c f) -> p c f", f=128),
            in_=self.D[n][:, h * 128:(h + 1) * 128].rearrange("(c p) f -> p c f", p=128)),
          waits, [I(self.qg, 16)])
        return W(self.qg)

    def proj(self, wbuf, waits, bs="A"):
        p, I, W = self.op, self.I, self.W
        ws = self.mmw(self.bs_keys(bs), waits)
        for fi in range(16):
            for sp in range(4):
                tgt, _ = self.pf_half(sp, bs)
                p("tensor", lambda t, tg=tgt, w=wbuf, fi=fi, sp=sp: t.matmul(
                    tg, w[:, fi * 128:(fi + 1) * 128],
                    self.BIGA[:, fi * HF + sp * 512: fi * HF + sp * 512 + 512],
                    start=(fi == 0), stop=(fi == 15)),
                  ws if (fi == 0 and sp == 0) else (),
                  [I(self.te)] if (fi == 15 and sp == 3) else ())
        return W(self.te)

    def evac2(self, eng, dstbuf, wait, bs="A"):
        p, I, W = self.op, self.I, self.W
        sem = self.ve if eng == "vector" else self.se
        for h in range(2):
            pbuf = self.bs_buf(h, bs)
            if eng == "vector":
                p(eng, lambda E, d=dstbuf, h=h, pb=pbuf: E.tensor_copy(d[:, h * 1024:(h + 1) * 1024], pb[:, :] if pb.shape[1] == 1024 else pb[:, 0:1024]),
                  (wait,) if h == 0 else (), [I(sem)] if h == 1 else [])
            else:
                p(eng, lambda E, d=dstbuf, h=h, pb=pbuf: E.copy(d[:, h * 1024:(h + 1) * 1024], pb[:, :] if pb.shape[1] == 1024 else pb[:, 0:1024]),
                  (wait,) if h == 0 else (), [I(sem)] if h == 1 else [])
        w = W(sem)
        self.setg(self.bs_keys(bs), w)
        return w

    def rot_apply(self, rawbuf, outbuf, ctab, stab, wait_raw, bs="A"):
        p, I, W = self.op, self.I, self.W
        ws = self.mmw(self.bs_keys(bs), (wait_raw, self.const_done))
        for sp in range(4):
            tgt, _ = self.pf_half(sp, bs)
            p("tensor", lambda t, tg=tgt, r=rawbuf, sp=sp: t.matmul(
                tg, self.bdp[:], r[:, sp * 512:(sp + 1) * 512], start=True, stop=True),
              ws if sp == 0 else (), [I(self.te)] if sp == 3 else ())
        wte = W(self.te)
        for h in range(2):
            pbuf = self.bs_buf(h, bs)
            p("vector", lambda v, h=h, st=stab, pb=pbuf: v.tensor_tensor(
                self.t2[:, h * 1024:(h + 1) * 1024], pb[:, :] if pb.shape[1] == 1024 else pb[:, 0:1024],
                st[:, h * 1024:(h + 1) * 1024], AL.mult),
              (wte,) if h == 0 else (), [I(self.ve)] if h == 1 else [])
        self.setg(self.bs_keys(bs), W(self.ve))
        p("vector", lambda v, r=rawbuf, o=outbuf, ct=ctab: v.tensor_tensor(o[:], r[:], ct[:], AL.mult))
        p("vector", lambda v, o=outbuf: v.tensor_tensor(o[:], o[:], self.t2[:], AL.add), (), [I(self.ve)])
        return W(self.ve)

    def transposes_to(self, outbuf, srcbuf, wait_src, eta_hp=None):
        p, I, W = self.op, self.I, self.W
        ws = self.mmw(("pbt",), (wait_src, self.const_done))
        for cc in range(16):
            p("tensor", lambda t, c=cc, sb_=srcbuf: t.transpose(
                self.PBT[:, c * 128:(c + 1) * 128], sb_[:, c * 128:(c + 1) * 128], self.idb[:]),
              ws if cc == 0 else (), [I(self.te)] if cc == 15 else ())
        wte = W(self.te)
        if eta_hp is None:
            p("vector", lambda v, o=outbuf: v.tensor_copy(o[:], self.PBT[:, :]), (wte,), [I(self.ve)])
        else:
            p("vector", lambda v, o=outbuf, hp=eta_hp: v.tensor_tensor(
                o[:].rearrange("p (c f) -> p c f", f=128),
                self.PBT[:, :].rearrange("p (c f) -> p c f", f=128),
                self.etb[:, None, hp * 128:(hp + 1) * 128].broadcast_to((128, 16, 128)), AL.mult),
              (wte,), [I(self.ve)])
        self.setg(("pbt",), W(self.ve))
        return W(self.ve)

    def state_mm4(self, waits):
        p, I, W = self.op, self.I, self.W
        ws = self.mmw(("pf2a", "pf2b"), waits)
        for cc in range(NCH):
            for w in range(2):
                p("tensor", lambda t, c=cc, w=w: t.matmul(
                    self.PF[2][w * 64:(w + 1) * 64, c * 64:(c + 1) * 64],
                    self.keta[:, c * 128 + w * 64: c * 128 + w * 64 + 64],
                    self.vtm[:, c * 128 + w * 64: c * 128 + w * 64 + 64],
                    start=True, stop=True, tile_position=(0, w * 64)),
                  ws if (cc == 0 and w == 0) else (),
                  [I(self.te)] if (cc == NCH - 1 and w == 1) else ())
        return W(self.te)

    def state_chain(self, hp, wait_mm4, hist):
        p, I, W = self.op, self.I, self.W
        bufs = [self.S_a, self.S_b]
        for cc in range(NCH):
            src, dst = bufs[cc % 2], bufs[(cc + 1) % 2]
            if hist:
                p("vector", lambda v, c=cc, s=src, hp=hp: v.tensor_copy(
                    self.S_hist[:, c * 64:(c + 1) * 64], s[:, hp * 64:(hp + 1) * 64]))
            p("vector", lambda v, c=cc, s=src, d=dst, hp=hp: v.scalar_tensor_tensor(
                d[:, hp * 64:(hp + 1) * 64], s[:, hp * 64:(hp + 1) * 64], self.g128[:, hp:hp + 1],
                self.PF[2][:, c * 64:(c + 1) * 64], AL.mult, AL.add),
              (wait_mm4,) if cc == 0 else (),
              [I(self.ve)] if cc == NCH - 1 else [])
        self.setg(("pf2a", "pf2b"), W(self.ve))
        return W(self.ve)

    def retention_chunks(self, hp, waits_ready, do_spill, stats_off, spill_wait=None):
        p, I, W = self.op, self.I, self.W
        wv_hist = W(self.ve)

        def emit_mm1(cc, first):
            par = cc % 2
            ak = ("pf0a", "pf0b")[par]
            bk = ("pf1a", "pf1b")[par]
            atA = self.PF[0][:, par * 512: par * 512 + 128]
            atB = self.PF[1][:, par * 512: par * 512 + 128]
            ws = self.mmw((ak,), tuple(waits_ready) + (wv_hist,) if first else ())
            p("tensor", lambda t, c=cc, a=atA: t.matmul(
                a, self.kp[0:64, c * 128:(c + 1) * 128], self.qp[0:64, c * 128:(c + 1) * 128],
                start=True, stop=True), ws, [I(self.te)])
            ws = self.mmw((bk,))
            p("tensor", lambda t, c=cc, a=atB: t.matmul(
                a, self.kp[64:128, c * 128:(c + 1) * 128], self.qp[64:128, c * 128:(c + 1) * 128],
                start=True, stop=True), ws, [I(self.te)])
            return W(self.te)

        w_at = {0: emit_mm1(0, True)}
        for cc in range(NCH):
            par = cc % 2
            ak = ("pf0a", "pf0b")[par]
            bk = ("pf1a", "pf1b")[par]
            gk = ("pf2a", "pf2b")[par]
            atA = self.PF[0][:, par * 512: par * 512 + 128]
            atB = self.PF[1][:, par * 512: par * 512 + 128]
            crB = self.PF[1][:, par * 512 + 128: par * 512 + 256]
            rg = self.PF[2][:, par * 512: par * 512 + 128]
            p("vector", lambda v, c=cc, a=atA, hp=hp: v.tensor_tensor(
                self.t2[:, (c % 4) * 256:(c % 4) * 256 + 128], a, self.dtb[:, hp * 256:hp * 256 + 128], AL.mult),
              (w_at[cc],), [I(self.ve)])
            self.setg((ak,), W(self.ve))
            p("vector", lambda v, c=cc, a=atB, hp=hp: v.tensor_tensor(
                self.t2[:, (c % 4) * 256 + 128:(c % 4) * 256 + 256], a, self.dtb[:, hp * 256 + 128:hp * 256 + 256], AL.mult),
              (), [I(self.ve)])
            w_atd = W(self.ve)
            if cc + 1 < NCH:
                w_at[cc + 1] = emit_mm1(cc + 1, False)
            ws2 = self.mmw((gk,), (w_atd,))
            p("tensor", lambda t, c=cc, r=rg: t.matmul(
                r[0:64, :], self.vtm[:, c * 128: c * 128 + 64],
                self.t2[:, (c % 4) * 256:(c % 4) * 256 + 128],
                start=True, stop=False, tile_position=(0, 0), skip_group_check=True), ws2)
            p("tensor", lambda t, c=cc, r=rg: t.matmul(
                r[64:128, :], self.vtm[:, c * 128 + 64: c * 128 + 128],
                self.t2[:, (c % 4) * 256 + 128:(c % 4) * 256 + 256],
                start=True, stop=True, tile_position=(0, 64), skip_group_check=True))
            p("tensor", lambda t, c=cc, r=rg: t.matmul(
                r[0:64, :], self.S_hist[0:64, c * 64:(c + 1) * 64],
                self.xiq[0:64, c * 128:(c + 1) * 128],
                start=False, stop=True, tile_position=(0, 0), skip_group_check=True),
              (), [I(self.te)])
            p("tensor", lambda t, c=cc, a=crB: t.matmul(
                a[64:128, :], self.S_hist[64:128, c * 64:(c + 1) * 64],
                self.xiq[64:128, c * 128:(c + 1) * 128],
                start=True, stop=True, tile_position=(64, 64), skip_group_check=True),
              (), [I(self.te)])
            w_ret = W(self.te)
            p("vector", lambda v, c=cc, r=rg: v.tensor_copy(self.stagf[:, c * 128:(c + 1) * 128], r),
              (w_ret, spill_wait) if (cc == 0 and do_spill) else (w_ret,), [I(self.ve)])
            self.setg((gk,), W(self.ve))
            p("vector", lambda v, c=cc, a=crB: v.tensor_tensor(
                self.stagf[64:128, c * 128:(c + 1) * 128], a[64:128, :],
                self.stagf[64:128, c * 128:(c + 1) * 128], AL.add), (), [I(self.ve)])
            self.setg((bk,), W(self.ve))
            if not do_spill:
                p("vector", lambda v, c=cc, so=stats_off: v.bn_stats(
                    self.stats[:, so + c * 6: so + (c + 1) * 6], self.stagf[:, c * 128:(c + 1) * 128]),
                  (), [I(self.ve)])
        if do_spill:
            for i in range(4):
                p("vector", lambda v, i=i, so=stats_off: v.bn_stats(
                    self.stats[:, so + i * 6: so + (i + 1) * 6], self.stagf[:, i * 512:(i + 1) * 512]))
            p("vector", lambda v: v.tensor_copy(self.sm1[:, 5:6], self.stagf[:, 0:1]), (), [I(self.ve)])
        return W(self.ve)

    def build(self):
        p, I, W = self.op, self.I, self.W
        qd, qg, te, ve, se, os_ = self.qd, self.qg, self.te, self.ve, self.se, self.os
        D = self.D

        for dst, src in ((self.dtb, D["dtb"]), (self.xib, D["xib"]), (self.etb, D["etb"]),
                         (self.bdp, D["bdp"]), (self.idb, D["iden"]),
                         (self.g128, D["g128"]), (self.idf, D["iden"]), (self.sel64, D["sel64"]),
                         (self.rep2, D["rep2"]), (self.gnw2, D["gnw2"]), (self.gnb2, D["gnb2"]),
                         (self.mAb, D["mA"]), (self.mBb, D["mB"])):
            ndim = len(src.shape)
            p("gpsimd", lambda g, a=dst, b=src: g.dma_start(out=a[:], in_=b[:, :] if len(b.shape) == 2 else b[:, :, :]),
              (), [I(qg, 16)])
        self.const_done = W(qg)

        tile_order = list(range(16, 32)) + list(range(16))
        lnw = {}
        for idx, t in enumerate(tile_order):
            bi = idx % 2
            p("gpsimd", lambda g, b=bi, t=t: g.dma_start(out=self.xt[b][:], in_=D["x"][t * 128:(t + 1) * 128, :]),
              (W(ve),) if idx >= 2 else (), [I(qg, 16)])
            w_apply = self.emit_ln(self.xt[bi], self.lnt[bi], W(qg), W(qg) if idx >= 2 else None)
            p("gpsimd", lambda g, b=bi, t=t: g.dma_start(out=self.ln1_sp[t * 128:(t + 1) * 128, :], in_=self.lnt[b][:]),
              (w_apply,), [I(qg, 16)])
            if KDEBUG.get("p0dbg"):
                p("gpsimd", lambda g, t=t: g.dma_start(out=self.taps["p0dbg"][:, t * 8:(t + 1) * 8], in_=self.sm1[:]),
                  (), [I(qg, 16)])
                p("gpsimd", lambda g, t=t: g.dma_start(out=self.taps["p0dbg2"][:, t * 4:t * 4 + 2], in_=self.lmv[:]),
                  (), [I(qg, 16)])
                p("gpsimd", lambda g, t=t: g.dma_start(out=self.taps["p0dbg2"][:, t * 4 + 2:t * 4 + 4], in_=self.sm1[:, 5:7]),
                  (), [I(qg, 16)])
            if KDEBUG.get("ln1"):
                p("gpsimd", lambda g, b=bi, t=t: g.dma_start(out=self.taps["ln1"][t * 128:(t + 1) * 128, :], in_=self.lnt[b][:]),
                  (), [I(qg, 16)])
            lnw[t] = W(qg)
        ln_pb_done, ln_pa_done = lnw[31], lnw[15]
        if KPHASES[0] < 1:
            return self._finish_early()

        w_lnfm = self.load_lnfm(self.ln1_sp, HF, ln_pb_done)
        w_rot = self.load_rot(HF, (("ck", self.ckb), ("sk", self.skb)))
        p("vector", lambda v: v.memset(self.S_a[:], 0.0), (), [I(ve)])
        for hp in range(NHP):
            bi = hp % 2
            wk_w = self.wslice(self.wks[bi], "wkT", hp, (W(te),) if hp >= 2 else ())
            wv_w = self.wslice(self.wvs[bi], "wvT", hp)
            wte = self.proj(self.wks[bi], (wk_w, w_lnfm if hp == 0 else None,
                                           w_rot if hp == 0 else None,
                                           self.const_done if hp == 0 else None), bs="A")
            wte_v = self.proj(self.wvs[bi], (wv_w,), bs="B")
            w_kraw = self.evac2("scalar", self.kraw, wte, bs="A")
            w_kp = self.rot_apply(self.kraw, self.kp, self.ckb, self.skb, w_kraw, bs="A")
            w_vfm = self.evac2("vector", self.vfm, wte_v, bs="B")
            w_vtm = self.transposes_to(self.vtm, self.vfm, w_vfm)
            w_keta = self.transposes_to(self.keta, self.kp, w_kp, eta_hp=hp)
            p("gpsimd", lambda g, hp=hp: g.dma_start(out=self.kv_sp[hp, 0, :, :], in_=self.kp[:]), (w_kp,), [I(qg, 16)])
            p("gpsimd", lambda g, hp=hp: g.dma_start(out=self.kv_sp[hp, 1, :, :], in_=self.vtm[:]), (w_vtm,), [I(qg, 16)])
            w_mm4 = self.state_mm4((w_keta, w_vtm))
            self.state_chain(hp, w_mm4, hist=False)
        if KDEBUG.get("spb"):
            p("gpsimd", lambda g: g.dma_start(out=self.taps["spb"][:, :], in_=self.S_a[:]), (W(ve),), [I(qg, 16)])
        if KPHASES[0] < 2:
            return self._finish_early()

        p("vector", lambda v: v.tensor_scalar(self.S_a[:], self.S_a[:], self.mAb[:, 0:1], None, AL.mult),
          (self.const_done,), [I(ve)])
        w_lnfm = self.load_lnfm(self.ln1_sp, 0, (ln_pa_done, W(te)))
        w_rot = self.load_rot(0, (("cq", self.cqb), ("sq", self.sqb), ("ck", self.ckb), ("sk", self.skb)),
                              waits=(W(ve),))
        for hp in range(NHP):
            bi = hp % 2
            wq_w = self.wslice(self.wqs[bi], "wqT", hp, (W(te),) if hp >= 2 else (ln_pa_done, W(te)))
            wk_w = self.wslice(self.wks[bi], "wkT", hp)
            wv_w = self.wslice(self.wvs[bi], "wvT", hp)
            wte_q = self.proj(self.wqs[bi], (wq_w, w_lnfm if hp == 0 else None, w_rot if hp == 0 else None), bs="A")
            wte_k = self.proj(self.wks[bi], (wk_w,), bs="B")
            w_qraw = self.evac2("scalar", self.qraw, wte_q, bs="A")
            w_qp = self.rot_apply(self.qraw, self.qp, self.cqb, self.sqb, w_qraw, bs="A")
            p("vector", lambda v, hp=hp: v.tensor_tensor(
                self.xiq[:].rearrange("p (c f) -> p c f", f=128),
                self.qp[:].rearrange("p (c f) -> p c f", f=128),
                self.xib[:, None, hp * 128:(hp + 1) * 128].broadcast_to((128, 16, 128)), AL.mult),
              (w_qp,), [I(ve)])
            w_xiq = W(ve)
            w_kraw = self.evac2("scalar", self.kraw, wte_k, bs="B")
            w_kp = self.rot_apply(self.kraw, self.kp, self.ckb, self.skb, w_kraw, bs="B")
            wte_v = self.proj(self.wvs[bi], (wv_w,), bs="A")
            w_vfm = self.evac2("vector", self.vfm, wte_v, bs="A")
            w_vtm = self.transposes_to(self.vtm, self.vfm, w_vfm)
            w_keta = self.transposes_to(self.keta, self.kp, w_kp, eta_hp=hp)
            w_mm4 = self.state_mm4((w_keta, w_vtm))
            self.state_chain(hp, w_mm4, hist=True)
            w_rc = self.retention_chunks(hp, (w_xiq, w_keta), do_spill=True, stats_off=hp * 120,
                                         spill_wait=W(self.qd) if hp > 0 else None)
            p("sync", lambda s, hp=hp: s.dma_start(out=self.ret_sp[hp, :, :], in_=self.stagf[:]), (w_rc,), [I(qd, 16)])
            if KDEBUG.get("ret"):
                p("gpsimd", lambda g, hp=hp: g.dma_start(out=self.taps["ret"][hp, :, :], in_=self.stagf[:]), (w_rc,), [I(qg, 16)])
        w_ret_spill = W(qd)
        if KPHASES[0] < 3:
            return self._finish_early()

        p("vector", lambda v: v.tensor_scalar(self.S_a[:], self.S_a[:], self.mBb[:, 0:1], None, AL.mult), (), [I(ve)])
        w_lnfm = self.load_lnfm(self.ln1_sp, HF, W(te))
        w_rot = self.load_rot(HF, (("cq", self.cqb), ("sq", self.sqb)), waits=(W(ve),))
        for hp in range(NHP):
            bi = hp % 2
            wq_w = self.wslice(self.wqs[bi], "wqT", hp, (W(te),) if hp >= 2 else ())
            p("sync", lambda s, hp=hp: s.dma_start(out=self.kp[:], in_=self.kv_sp[hp, 0, :, :]),
              (W(te),), [I(qd, 16)])
            p("sync", lambda s, hp=hp: s.dma_start(out=self.vtm[:], in_=self.kv_sp[hp, 1, :, :]), (), [I(qd, 16)])
            w_kv = W(qd)
            wte = self.proj(self.wqs[bi], (wq_w, w_lnfm if hp == 0 else None, w_rot if hp == 0 else None))
            w_qraw = self.evac2("scalar", self.qraw, wte)
            w_qp = self.rot_apply(self.qraw, self.qp, self.cqb, self.sqb, w_qraw)
            p("vector", lambda v, hp=hp: v.tensor_tensor(
                self.xiq[:].rearrange("p (c f) -> p c f", f=128),
                self.qp[:].rearrange("p (c f) -> p c f", f=128),
                self.xib[:, None, hp * 128:(hp + 1) * 128].broadcast_to((128, 16, 128)), AL.mult),
              (w_qp,), [I(ve)])
            w_xiq = W(ve)
            w_keta = self.transposes_to(self.keta, self.kp, w_kv, eta_hp=hp)
            w_mm4 = self.state_mm4((w_keta, w_kv))
            self.state_chain(hp, w_mm4, hist=True)
            self.retention_chunks(hp, (w_xiq, w_keta), do_spill=False, stats_off=hp * 120 + 24)

        if KPHASES[0] < 4:
            return self._finish_early()
        self._stats_finalize()
        w_mish = self._mish(ln_pa_done)
        w_h1_all = self._wo_phase(w_mish, w_ret_spill)
        if KPHASES[0] < 5:
            return self._finish_early()
        self._ffn(w_h1_all)

        return self._emit()

    def _finish_early(self):
        return self._emit()

    def _emit(self):
        p, W = self.op, self.W
        p("sync", None, (W(self.qd), W(self.qg), W(self.te), W(self.ve), W(self.se), W(self.os)))
        blk = self.es.enter_context(self.nc.Block())

        def mk(eng_name):
            def f(eng):
                for fn, waits, incs in self.EL[eng_name]:
                    for (h, v) in waits:
                        eng.wait_ge(h, v)
                    inst = fn(eng) if fn is not None else None
                    if inst is not None:
                        for (h, n) in incs:
                            inst.then_inc(h, n)
            return f

        blk.sync(mk("sync"))
        blk.tensor(mk("tensor"))
        blk.vector(mk("vector"))
        blk.scalar(mk("scalar"))
        blk.gpsimd(mk("gpsimd"))
        return self.nc

    def _stats_finalize(self):
        p, I, W = self.op, self.I, self.W
        te, ve, se = self.te, self.ve, self.se
        st3, fin2, sm4, sm1 = self.st3, self.fin2, self.sm4, self.sm1
        scr = self.stagf
        for hp in range(NHP):
            b0 = hp * 120
            p("vector", lambda v, b0=b0: v.tensor_reduce(sm1[:, 0:1], self.stats[:, b0 + 1:b0 + 24:3], mybir.AxisListType.X, AL.add))
            p("vector", lambda v, b0=b0: v.tensor_reduce(sm1[:, 1:2], self.stats[:, b0 + 25:b0 + 120:3], mybir.AxisListType.X, AL.add))
            p("vector", lambda v, b0=b0: v.tensor_reduce(sm1[:, 2:3], self.stats[:, b0 + 2:b0 + 24:3], mybir.AxisListType.X, AL.add))
            p("vector", lambda v, b0=b0: v.tensor_reduce(sm1[:, 3:4], self.stats[:, b0 + 26:b0 + 120:3], mybir.AxisListType.X, AL.add))
            p("vector", lambda v, b0=b0: v.tensor_tensor(scr[:, 1100:1108], self.stats[:, b0 + 1:b0 + 24:3], self.stats[:, b0 + 1:b0 + 24:3], AL.mult))
            p("vector", lambda v, b0=b0: v.tensor_tensor(scr[:, 1108:1140], self.stats[:, b0 + 25:b0 + 120:3], self.stats[:, b0 + 25:b0 + 120:3], AL.mult))
            p("vector", lambda v: v.tensor_reduce(sm1[:, 4:5], scr[:, 1100:1108], mybir.AxisListType.X, AL.add))
            p("vector", lambda v: v.tensor_reduce(sm1[:, 5:6], scr[:, 1108:1140], mybir.AxisListType.X, AL.add))
            p("vector", lambda v: v.tensor_scalar(sm1[:, 1:2], sm1[:, 1:2], 64.0, None, AL.mult))
            self.vsep()
            p("vector", lambda v: v.scalar_tensor_tensor(sm1[:, 0:1], sm1[:, 0:1], 256.0, sm1[:, 1:2], AL.mult, AL.add))
            self.vsep()
            p("vector", lambda v, hp=hp: v.tensor_scalar(st3[:, hp:hp + 1], sm1[:, 0:1], 1.0 / (S * HD / 64), None, AL.mult))
            p("vector", lambda v: v.tensor_tensor(sm1[:, 2:3], sm1[:, 2:3], sm1[:, 3:4], AL.add))
            self.vsep()
            p("vector", lambda v: v.scalar_tensor_tensor(sm1[:, 2:3], sm1[:, 4:5], 256.0, sm1[:, 2:3], AL.mult, AL.add))
            self.vsep()
            p("vector", lambda v: v.scalar_tensor_tensor(sm1[:, 2:3], sm1[:, 5:6], 64.0, sm1[:, 2:3], AL.mult, AL.add))
            self.vsep()
            p("vector", lambda v: v.tensor_scalar(sm1[:, 2:3], sm1[:, 2:3], 1.0 / (S * HD / 64), None, AL.mult))
            p("vector", lambda v, hp=hp: v.tensor_tensor(scr[:, 1140:1141], st3[:, hp:hp + 1], st3[:, hp:hp + 1], AL.mult))
            self.vsep()
            p("vector", lambda v, hp=hp: v.tensor_tensor(st3[:, NHP + hp:NHP + hp + 1], sm1[:, 2:3], scr[:, 1140:1141], AL.subtract))
            p("vector", lambda v, hp=hp: v.tensor_copy(st3[:, 2 * NHP + hp:2 * NHP + hp + 1], scr[:, 1140:1141]))
        p("vector", lambda v: v.engine_nop() if False else v.tensor_copy(sm1[:, 6:7], st3[:, 0:1]), (), [I(ve)])
        p("tensor", lambda t: t.matmul(self.PF[2][0:2, 0:3 * NHP], self.sel64[:], st3[:], start=True, stop=True),
          self.mmw(("pf2a",), (W(ve), self.const_done)), [I(te)])
        p("vector", lambda v: v.tensor_copy(fin2[:, 0:3 * NHP], self.PF[2][0:2, 0:3 * NHP]), (W(te),), [I(ve)])
        self.setg(("pf2a", "pf2b"), W(ve))
        self.vsep()
        p("vector", lambda v: v.tensor_scalar(fin2[:, 0:3 * NHP], fin2[:, 0:3 * NHP], 1.0 / 64, None, AL.mult))
        self.vsep()
        p("vector", lambda v: v.tensor_tensor(sm4[:, 0:NHP], fin2[:, 0:NHP], fin2[:, 0:NHP], AL.mult))
        p("vector", lambda v: v.tensor_tensor(fin2[:, NHP:2 * NHP], fin2[:, NHP:2 * NHP], fin2[:, 2 * NHP:3 * NHP], AL.add))
        self.vsep()
        p("vector", lambda v: v.tensor_tensor(fin2[:, NHP:2 * NHP], fin2[:, NHP:2 * NHP], sm4[:, 0:NHP], AL.subtract))
        self.vsep()
        p("vector", lambda v: v.tensor_scalar(fin2[:, NHP:2 * NHP], fin2[:, NHP:2 * NHP], EPS, None, AL.add), (), [I(ve)])
        p("scalar", lambda s: s.activation(sm4[:, NHP:2 * NHP], fin2[:, NHP:2 * NHP], AF.Sqrt), (W(ve),), [I(se)])
        p("vector", lambda v: v.reciprocal(sm4[:, 0:NHP], sm4[:, NHP:2 * NHP]), (W(se),), [I(ve)])
        p("scalar", lambda s: s.activation(sm4[:, 2 * NHP:3 * NHP], sm4[:, 0:NHP], AF.Identity), (W(ve),))
        p("scalar", lambda s: s.activation(sm4[:, NHP:2 * NHP], sm4[:, 0:NHP], AF.Square), (), [I(se)])
        p("vector", lambda v: v.tensor_tensor(sm4[:, NHP:2 * NHP], sm4[:, NHP:2 * NHP], fin2[:, NHP:2 * NHP], AL.mult), (W(se),))
        self.vsep()
        p("vector", lambda v: v.tensor_scalar(sm4[:, NHP:2 * NHP], sm4[:, NHP:2 * NHP], -0.5, 1.5, AL.mult, AL.add))
        self.vsep()
        p("vector", lambda v: v.tensor_tensor(sm4[:, 0:NHP], sm4[:, 2 * NHP:3 * NHP], sm4[:, NHP:2 * NHP], AL.mult))
        self.vsep()
        p("vector", lambda v: v.tensor_tensor(fin2[:, 3 * NHP:4 * NHP], self.gnw2[:], sm4[:, 0:NHP], AL.mult))
        self.vsep()
        p("vector", lambda v: v.tensor_tensor(sm4[:, NHP:2 * NHP], fin2[:, 0:NHP], fin2[:, 3 * NHP:4 * NHP], AL.mult))
        self.vsep()
        p("vector", lambda v: v.tensor_tensor(fin2[:, 0:NHP], self.gnb2[:], sm4[:, NHP:2 * NHP], AL.subtract), (), [I(ve)])
        p("tensor", lambda t: t.matmul(self.PF[2][:, 0:NHP], self.rep2[:], fin2[0:2, 3 * NHP:4 * NHP], start=True, stop=True),
          self.mmw(("pf2a",), (W(ve),)))
        p("tensor", lambda t: t.matmul(self.PF[2][:, 512:512 + NHP], self.rep2[:], fin2[0:2, 0:NHP], start=True, stop=True),
          self.mmw(("pf2b",)), [I(te)])
        p("vector", lambda v: v.tensor_copy(self.A_vec[:], self.PF[2][:, 0:NHP]), (W(te),))
        p("vector", lambda v: v.tensor_copy(self.C_vec[:], self.PF[2][:, 512:512 + NHP]), (), [I(ve)])
        self.setg(("pf2a", "pf2b"), W(ve))
        if KDEBUG.get("gn"):
            p("gpsimd", lambda g: g.dma_start(out=self.taps["gn"][:, 0:NHP], in_=self.A_vec[:]), (W(ve),), [I(self.qg, 16)])
            p("gpsimd", lambda g: g.dma_start(out=self.taps["gn"][:, NHP:2 * NHP], in_=self.C_vec[:]), (), [I(self.qg, 16)])

    def _mish(self, ln_pa_done):
        p, I, W = self.op, self.I, self.W
        te, ve, se, qg = self.te, self.ve, self.se, self.qg
        w_lnfm = self.load_lnfm(self.ln1_sp, 0, W(self.te))
        keys3 = (("pf0a", "pf0b"), ("pf1a", "pf1b"), ("pf2a", "pf2b"))
        for fo in range(16):
            bi = fo % 2
            wg_w = self.wslice(self.wgen[bi], "wgT", fo, (W(te),))
            for half in range(2):
                mi = (fo * 2 + half) % 2
                k3 = keys3[(fo * 2 + half) % 3]
                pf = self.PF[(fo * 2 + half) % 3]
                ws = self.mmw(k3, (wg_w, w_lnfm if fo == 0 else None))
                for fi in range(16):
                    for sp in range(2):
                        p("tensor", lambda t, pf=pf, w=self.wgen[bi], fi=fi, sp=sp, half=half: t.matmul(
                            pf[:, sp * 512:(sp + 1) * 512], w[:, fi * 128:(fi + 1) * 128],
                            self.BIGA[:, fi * HF + half * 1024 + sp * 512: fi * HF + half * 1024 + sp * 512 + 512],
                            start=(fi == 0), stop=(fi == 15)),
                          ws if (fi == 0 and sp == 0) else (),
                          [I(te)] if (fi == 15 and sp == 1) else ())
                wte = W(te)
                p("scalar", lambda s, pf=pf: s.activation(self.stagf[:, 0:1024], pf[:, :], AF.Exp), (wte,))
                p("scalar", lambda s: s.activation(self.stagf[:, 0:1024], self.stagf[:, 0:1024], AF.Ln, bias=1.0))
                p("scalar", lambda s: s.activation(self.stagf[:, 0:1024], self.stagf[:, 0:1024], AF.Tanh), (), [I(se)])
                p("vector", lambda v, pf=pf, mi=mi: v.tensor_tensor(
                    self.msh[mi][:], pf[:, :], self.stagf[:, 0:1024], AL.mult),
                  (W(se), W(qg, self.qg.v)), [I(ve)])
                self.setg(k3, W(ve))
                p("gpsimd", lambda g, fo=fo, half=half, mi=mi: g.dma_start(
                    out=self.mish_sp[fo, :, half * 1024:(half + 1) * 1024], in_=self.msh[mi][:]),
                  (W(ve),), [I(qg, 16)])
        return W(qg)

    def _wo_phase(self, w_mish, w_ret_spill):
        p, I, W = self.op, self.I, self.W
        qd, qg, te, ve = self.qd, self.qg, self.te, self.ve
        w_free = W(te)
        for cc in range(16):
            p("gpsimd", lambda g, c=cc: g.dma_start(
                out=self.BIGA[:, c * HF:(c + 1) * HF], in_=self.D["woT"][c * 128:(c + 1) * 128, :]),
              (w_free,) if cc == 0 else (), [I(qg, 16)])
        w_wo = W(qg)
        for tt in range(16):
            bi = tt % 2
            p("sync", lambda s, tt=tt, b=bi: s.dma_start(
                out=self.strip[b][:].rearrange("p (h f) -> p h f", f=128),
                in_=self.ret_sp[:, :, tt * 128:(tt + 1) * 128].rearrange("h p f -> p h f")),
              (W(ve),) if tt >= 2 else (w_ret_spill, W(te), W(ve)), [I(qd, 16)])
            p("sync", lambda s, tt=tt, b=bi: s.dma_start(
                out=self.mstrip[b][:].rearrange("p (h f) -> p h f", f=128),
                in_=self.mish_sp[:, :, tt * 128:(tt + 1) * 128].rearrange("h p f -> p h f")),
              (w_mish, W(te)) if tt <= 1 else (), [I(qd, 16)])
            p("sync", lambda s, tt=tt, b=bi: s.dma_start(out=self.xtf[b][:], in_=self.D["x"][tt * 128:(tt + 1) * 128, :]),
              (W(te), W(ve)) if tt <= 1 else (), [I(qd, 16)])
            w_strip = W(qd)
            for hp in range(NHP):
                p("vector", lambda v, b=bi, hp=hp: v.tensor_scalar(
                    self.stagf[:, hp * 128:(hp + 1) * 128],
                    self.strip[b][:, hp * 128:(hp + 1) * 128],
                    self.A_vec[:, hp:hp + 1], self.C_vec[:, hp:hp + 1], AL.mult, AL.add),
                  (w_strip,) if hp == 0 else ())
            p("vector", lambda v, b=bi: v.tensor_tensor(
                self.qp[:], self.stagf[:, 0:2048], self.mstrip[b][:], AL.mult), (), [I(ve)])
            w_z = W(ve)
            ws = self.mmw(self.PK4, (w_z, w_wo if tt == 0 else None))
            for fi in range(16):
                for sp in range(4):
                    tgt, _ = self.pf_half(sp)
                    p("tensor", lambda t, tg=tgt, fi=fi, sp=sp: t.matmul(
                        tg, self.qp[:, fi * 128:(fi + 1) * 128],
                        self.BIGA[:, fi * HF + sp * 512: fi * HF + sp * 512 + 512],
                        start=(fi == 0), stop=(fi == 15)),
                      ws if (fi == 0 and sp == 0) else (),
                      [I(te)] if (fi == 15 and sp == 3) else ())
            wte = W(te)
            for h in range(2):
                p("vector", lambda v, b=bi, h=h: v.tensor_tensor(
                    self.h1t[b][:, h * 1024:(h + 1) * 1024], self.PF[h][:, :],
                    self.xtf[b][:, h * 1024:(h + 1) * 1024], AL.add),
                  (wte,) if h == 0 else (), [I(ve)] if h == 1 else [])
            self.setg(self.PK4, W(ve))
            w_apply = self.emit_ln(self.h1t[bi], self.ln2t[bi], W(ve), W(qg) if tt >= 2 else None)
            p("gpsimd", lambda g, b=bi, tt=tt: g.dma_start(out=self.h1_sp[tt * 128:(tt + 1) * 128, :], in_=self.h1t[b][:]),
              (w_apply,), [I(qg, 16)])
            p("gpsimd", lambda g, b=bi, tt=tt: g.dma_start(out=self.ln2_sp[tt * 128:(tt + 1) * 128, :], in_=self.ln2t[b][:]),
              (), [I(qg, 16)])
            if KDEBUG.get("h1"):
                p("gpsimd", lambda g, b=bi, tt=tt: g.dma_start(out=self.taps["h1"][tt * 128:(tt + 1) * 128, :], in_=self.h1t[b][:]),
                  (), [I(qg, 16)])
        return W(qg)

    def _ffn(self, w_h1_all):
        p, I, W = self.op, self.I, self.W
        qd, qg, te, ve, se = self.qd, self.qg, self.te, self.ve, self.se
        D = self.D
        w_ln2 = self.load_lnfm(self.ln2_sp, 0, w_h1_all)
        p("gpsimd", lambda g: g.dma_start(
            out=self.gwsb[:].rearrange("p (c f) -> p c f", f=32),
            in_=D["gwT"].rearrange("(c p) f -> p c f", p=128)), (W(te), W(self.ve)), [I(qg, 16)])
        p("gpsimd", lambda g: g.dma_start(
            out=self.rdT_sb[:].rearrange("p (c f) -> p c f", f=128),
            in_=D["RdT"].rearrange("(c p) f -> p c f", p=128)), (), [I(qg, 16)])
        p("gpsimd", lambda g: g.dma_start(out=self.sCg_sb[:], in_=D["sCgT"][:, :]), (), [I(qg, 16)])
        p("gpsimd", lambda g: g.dma_start(out=self.sCu_sb[:], in_=D["sCuT"][:, :]), (), [I(qg, 16)])
        p("gpsimd", lambda g: g.dma_start(
            out=self.sRd_sb[:].rearrange("p (c f) -> p c f", f=128),
            in_=D["sRdT"].rearrange("(c p) f -> p c f", p=128)), (), [I(qg, 16)])
        p("gpsimd", lambda g: g.dma_start(out=self.sU_sb[:, 0:128], in_=D["sUgT"][:, :]), (), [I(qg, 16)])
        p("gpsimd", lambda g: g.dma_start(out=self.sU_sb[:, 128:256], in_=D["sUuT"][:, :]), (), [I(qg, 16)])
        p("gpsimd", lambda g: g.dma_start(out=self.sU_sb[:, 256:384], in_=D["sUdT"][:, :]), (), [I(qg, 16)])
        p("gpsimd", lambda g: g.dma_start(out=self.sCd_sb[:], in_=D["sCdT"][:, :]), (), [I(qg, 16)])
        p("gpsimd", lambda g: g.dma_start(out=self.sel_e[:], in_=D["sel_e"][:, :]), (), [I(qg, 16)])
        w_small = W(qg)

        def projL(dst, wname, extra=()):
            wl = self.wslice(self.wgen[0], wname, 0, (W(te),))
            wte = self.proj(self.wgen[0], (wl,) + tuple(extra))
            return self.evac2("vector", dst, wte)

        w_rg = projL(self.rg_sb, "RgT", (w_ln2,))
        w_ru = projL(self.ru_sb, "RuT")
        w_srg = projL(self.srg_sb, "sRgT")
        w_sru = projL(self.sru_sb, "sRuT")
        if KDEBUG.get("rg"):
            p("gpsimd", lambda g: g.dma_start(out=self.taps["rg"][:, :], in_=self.rg_sb[:]), (w_rg,), [I(qg, 16)])

        for pas in range(2):
            ws = self.mmw(("pf2a", "pf2b"), (w_small,) if pas == 0 else ())
            for fi in range(16):
                for sp in range(2):
                    p("tensor", lambda t, fi=fi, sp=sp, pas=pas: t.matmul(
                        self.PF[2][0:32, sp * 512:sp * 512 + 512],
                        self.gwsb[:, fi * 32:(fi + 1) * 32],
                        self.BIGA[:, fi * HF + pas * 1024 + sp * 512: fi * HF + pas * 1024 + sp * 512 + 512],
                        start=(fi == 0), stop=(fi == 15)),
                      ws if (fi == 0 and sp == 0) else (),
                      [I(te)] if (fi == 15 and sp == 1) else ())
            p("vector", lambda v, pas=pas: v.tensor_copy(self.lgf[:, pas * 1024:(pas + 1) * 1024], self.PF[2][0:32, :]),
              (W(te),), [I(ve)])
            self.setg(("pf2a", "pf2b"), W(ve))
        w_lg = W(ve)

        ws = self.mmw(("pf2a", "pf2b"), (w_lg,))
        for tl in range(16):
            p("tensor", lambda t, tl=tl: t.transpose(
                self.PF[2][:, tl * 32:(tl + 1) * 32],
                self.lgf[:, tl * 128:(tl + 1) * 128], self.idf[0:32, 0:32]),
              ws if tl == 0 else (), [I(te)] if tl == 15 else ())
        p("vector", lambda v: v.tensor_copy(self.lgtm[:], self.PF[2][:, 0:512]), (W(te),), [I(ve)])
        self.setg(("pf2a", "pf2b"), W(ve))

        sm1 = self.sm1
        for tl in range(16):
            lg = self.lgtm[:, tl * 32:(tl + 1) * 32]
            wetile = self.wetm[:, tl * 32:(tl + 1) * 32]
            msk = self.stagf[:, 64:96]
            p("vector", lambda v, lg=lg: v.tensor_reduce(sm1[:, 0:1], lg, mybir.AxisListType.X, AL.max))
            p("vector", lambda v, lg=lg, we=wetile: v.tensor_copy(we, lg))
            for rnd in range(3):
                mcol = sm1[:, 0:1] if rnd == 0 else sm1[:, 1:2]
                self.vsep()
                p("vector", lambda v, we=wetile, mc=mcol, mk=msk: v.tensor_scalar(mk, we, mc[:, 0:1], None, AL.is_ge))
                self.vsep()
                p("vector", lambda v, we=wetile, mk=msk: v.scalar_tensor_tensor(we, mk, -1e30, we, AL.mult, AL.add))
                self.vsep()
                p("vector", lambda v, we=wetile: v.tensor_reduce(sm1[:, 1:2], we, mybir.AxisListType.X, AL.max))
            self.vsep()
            p("vector", lambda v: v.tensor_scalar(sm1[:, 2:3], sm1[:, 0:1], -1.0, None, AL.mult))
            self.vsep()
            p("vector", lambda v, lg=lg: v.tensor_scalar(self.stagf[:, 0:32], lg, sm1[:, 2:3], None, AL.add), (), [I(ve)])
            p("scalar", lambda s: s.activation(self.stagf[:, 32:64], self.stagf[:, 0:32], AF.Exp), (W(ve),), [I(se)])
            p("vector", lambda v: v.tensor_reduce(sm1[:, 3:4], self.stagf[:, 32:64], mybir.AxisListType.X, AL.add), (W(se),))
            p("vector", lambda v, lg=lg, mk=msk: v.tensor_scalar(mk, lg, sm1[:, 1:2], None, AL.is_ge))
            self.vsep()
            p("vector", lambda v, mk=msk: v.tensor_tensor(self.stagf[:, 32:64], self.stagf[:, 32:64], mk, AL.mult))
            self.vsep()
            p("vector", lambda v: v.tensor_reduce(sm1[:, 4:5], self.stagf[:, 32:64], mybir.AxisListType.X, AL.add))
            self.vsep()
            p("vector", lambda v: v.scalar_tensor_tensor(sm1[:, 4:5], sm1[:, 3:4], 1e-20, sm1[:, 4:5], AL.mult, AL.add))
            p("vector", lambda v: v.reciprocal(sm1[:, 5:6], sm1[:, 4:5]), (), [I(ve)])
            p("scalar", lambda s: s.activation(sm1[:, 6:7], sm1[:, 5:6], AF.Identity), (W(ve),), [I(se)])
            p("vector", lambda v, we=wetile: v.tensor_scalar(we, self.stagf[:, 32:64], sm1[:, 6:7], None, AL.mult), (W(se),), [I(ve)])
        w_we = W(ve)
        if KDEBUG.get("we"):
            p("gpsimd", lambda g: g.dma_start(out=self.taps["we"][:, :], in_=self.wetm[:]), (w_we,), [I(qg, 16)])

        for grp in range(4):
            ws = self.mmw(("pf2a", "pf2b"), (w_we,) if grp == 0 else ())
            for q in range(4):
                tl = grp * 4 + q
                p("tensor", lambda t, tl=tl, q=q: t.transpose(
                    self.PF[2][0:32, q * 128:(q + 1) * 128],
                    self.wetm[:, tl * 32:(tl + 1) * 32], self.idf[:]),
                  ws if q == 0 else (), [I(te)] if q == 3 else ())
            p("vector", lambda v, grp=grp: v.tensor_copy(self.weT[:, grp * 512:(grp + 1) * 512], self.PF[2][0:32, 0:512]),
              (W(te),), [I(ve)])
            self.setg(("pf2a", "pf2b"), W(ve))
        w_weT = W(ve)

        def k128(dst, lhsT, rhsbuf, waits):
            ws = self.mmw(self.PK4, waits)
            for sp in range(4):
                tgt, _ = self.pf_half(sp)
                p("tensor", lambda t, tg=tgt, l=lhsT, r=rhsbuf, sp=sp: t.matmul(
                    tg, l, r[:, sp * 512:(sp + 1) * 512], start=True, stop=True),
                  ws if sp == 0 else (), [I(te)] if sp == 3 else ())
            return self.evac2("vector", dst, W(te))

        w_hsg = k128(self.hsg, self.sU_sb[:, 0:128], self.srg_sb, (w_srg, w_small))
        w_hsu = k128(self.hsu, self.sU_sb[:, 128:256], self.sru_sb, (w_sru,))

        def smblk(ft):
            return (self.BIGA[:, ft * HF:(ft + 1) * HF] if ft < 16
                    else self.smX[:, (ft - 16) * HF:(ft - 15) * HF])

        for ft in range(NSI):
            ws = self.mmw(self.PK4, (w_hsg, w_lg) if ft == 0 else ())
            for sp in range(4):
                tgt, _ = self.pf_half(sp, "A")
                p("tensor", lambda t, tg=tgt, ft=ft, sp=sp: t.matmul(
                    tg, self.sCg_sb[:, ft * 128:(ft + 1) * 128],
                    self.hsg[:, sp * 512:(sp + 1) * 512], start=True, stop=True),
                  ws if sp == 0 else (), [I(te)] if sp == 3 else ())
            wte = W(te)
            ws = self.mmw(self.bs_keys("B"), (w_hsu,) if ft == 0 else ())
            for sp in range(4):
                tgt, _ = self.pf_half(sp, "B")
                p("tensor", lambda t, tg=tgt, ft=ft, sp=sp: t.matmul(
                    tg, self.sCu_sb[:, ft * 128:(ft + 1) * 128],
                    self.hsu[:, sp * 512:(sp + 1) * 512], start=True, stop=True),
                  ws if sp == 0 else (), [I(te)] if sp == 3 else ())
            wte_u = W(te)
            for h in range(2):
                p("scalar", lambda s, h=h: s.activation(self.sgb[:, h * 1024:(h + 1) * 1024], self.PF[h][:, :], AF.Silu),
                  (wte,) if h == 0 else (), [I(se)] if h == 1 else [])
            self.setg(self.PK4, W(se))
            w_sg = W(se)
            for h in range(2):
                pbuf = self.bs_buf(h, "B")
                p("vector", lambda v, ft=ft, h=h, pb=pbuf: v.tensor_tensor(
                    smblk(ft)[:, h * 1024:(h + 1) * 1024], pb[:, 0:1024],
                    self.sgb[:, h * 1024:(h + 1) * 1024], AL.mult),
                  (wte_u, w_sg) if h == 0 else (), [I(ve)] if h == 1 else [])
            self.setg(self.bs_keys("B"), W(ve))
        ws = self.mmw(self.PK4, (W(ve),))
        for ki in range(NSI):
            for sp in range(4):
                tgt, _ = self.pf_half(sp)
                p("tensor", lambda t, tg=tgt, ki=ki, sp=sp: t.matmul(
                    tg, self.sRd_sb[:, ki * 128:(ki + 1) * 128],
                    smblk(ki)[:, sp * 512:(sp + 1) * 512],
                    start=(ki == 0), stop=(ki == NSI - 1)),
                  ws if (ki == 0 and sp == 0) else (),
                  [I(te)] if (ki == NSI - 1 and sp == 3) else ())
        w_srd = self.evac2("vector", self.sru_sb, W(te))
        w_srdu = k128(self.srdu_sb, self.sU_sb[:, 256:384], self.sru_sb, (w_srd,))
        if KDEBUG.get("srdu"):
            p("gpsimd", lambda g: g.dma_start(out=self.taps["srdu"][:, :], in_=self.srdu_sb[:]), (w_srdu,), [I(qg, 16)])

        for sp_i in range(NSPAN):
            self._moe_span(sp_i, w_weT, w_rg, w_ru)
            for sub in range(SPAN // 512):
                self._out_span(sp_i, sub)

    def _moe_span(self, sp_i, w_weT, w_rg, w_ru):
        p, I, W = self.op, self.I, self.W
        qg, te, ve, se = self.qg, self.te, self.ve, self.se
        D = self.D
        t0 = sp_i * SPAN
        NQ = SPAN // 512
        for e in range(NE):
            bi = e % 2
            ew = self.eW[bi]
            wld = (W(te), W(ve))
            p("gpsimd", lambda g, ew=ew, e=e: g.dma_start(out=ew["ug"][:], in_=D["UgT"][e, :, :]), wld, [I(qg, 16)])
            p("gpsimd", lambda g, ew=ew, e=e: g.dma_start(out=ew["cg"][:], in_=D["CgT"][e, :, :]), (), [I(qg, 16)])
            p("gpsimd", lambda g, ew=ew, e=e: g.dma_start(out=ew["uu"][:], in_=D["UuT"][e, :, :]), (), [I(qg, 16)])
            p("gpsimd", lambda g, ew=ew, e=e: g.dma_start(out=ew["cu"][:], in_=D["CuT"][e, :, :]), (), [I(qg, 16)])
            p("gpsimd", lambda g, ew=ew, e=e: g.dma_start(out=ew["ud"][:], in_=D["UdT"][e, :, :]), (), [I(qg, 16)])
            w_ew = W(qg)
            ws2 = self.mmw(("pf2a", "pf2b"), (w_weT,) if (e == 0 and sp_i == 0) else ())
            for q in range(NQ):
                p("tensor", lambda t, e=e, q=q: t.matmul(
                    self.PF[2][:, q * 512:q * 512 + 512], self.sel_e[:, e * 128:(e + 1) * 128],
                    self.weT[:, t0 + q * 512:t0 + q * 512 + 512], start=True, stop=True),
                  ws2 if q == 0 else (), [I(te)] if q == NQ - 1 else ())
            w_wbm = W(te)
            ws0 = self.mmw(("pf0a", "pf0b"), (w_ew, w_rg) if (e == 0 and sp_i == 0) else (w_ew,))
            for q in range(NQ):
                p("tensor", lambda t, ew=ew, q=q: t.matmul(
                    self.PF[0][:, q * 512:q * 512 + 512], ew["ug"][:],
                    self.rg_sb[:, t0 + q * 512:t0 + q * 512 + 512], start=True, stop=True),
                  ws0 if q == 0 else (), [I(te)] if q == NQ - 1 else ())
            w_a1g = W(te)
            ws1 = self.mmw(("pf1a", "pf1b"), (w_ru,) if (e == 0 and sp_i == 0) else ())
            for q in range(NQ):
                p("tensor", lambda t, ew=ew, q=q: t.matmul(
                    self.PF[1][:, q * 512:q * 512 + 512], ew["uu"][:],
                    self.ru_sb[:, t0 + q * 512:t0 + q * 512 + 512], start=True, stop=True),
                  ws1 if q == 0 else (), [I(te)] if q == NQ - 1 else ())
            w_a1u = W(te)
            p("scalar", lambda s: s.copy(self.wb_sb[:], self.PF[2][:, 0:SPAN]), (w_wbm,))
            p("scalar", lambda s: s.copy(self.hg_sb[:], self.PF[0][:, 0:SPAN]), (w_a1g,), [I(se)])
            self.setg(("pf2a", "pf2b", "pf0a", "pf0b"), W(se))
            w_hg = W(se)
            p("vector", lambda v: v.tensor_tensor(self.hu_sb[:], self.PF[1][:, 0:SPAN], self.wb_sb[:], AL.mult),
              (w_a1u, w_hg), [I(ve)])
            self.setg(("pf1a", "pf1b"), W(ve))
            w_hu = W(ve)
            w_sil = {}
            for ft in range(NI + 1):
                if ft < NI:
                    ws = self.mmw(("pf0a", "pf0b"), (w_hg,) if ft == 0 else ())
                    for q in range(NQ):
                        p("tensor", lambda t, ew=ew, ft=ft, q=q: t.matmul(
                            self.PF[0][:, q * 512:(q + 1) * 512], ew["cg"][:, ft * 128:(ft + 1) * 128],
                            self.hg_sb[:, q * 512:(q + 1) * 512], start=True, stop=True),
                          ws if q == 0 else (), [I(te)] if q == NQ - 1 else ())
                    wcg = W(te)
                    p("scalar", lambda s, ft=ft: s.activation(self.gblk2[ft % 2][:], self.PF[0][:, 0:SPAN], AF.Silu),
                      (wcg, W(ve)), [I(se)])
                    self.setg(("pf0a", "pf0b"), W(se))
                    w_sil[ft] = W(se)
                if ft >= 1:
                    fu = ft - 1
                    ws = self.mmw(("pf1a", "pf1b"), (w_hu,) if fu == 0 else ())
                    for q in range(NQ):
                        p("tensor", lambda t, ew=ew, fu=fu, q=q: t.matmul(
                            self.PF[1][:, q * 512:(q + 1) * 512], ew["cu"][:, fu * 128:(fu + 1) * 128],
                            self.hu_sb[:, q * 512:(q + 1) * 512], start=True, stop=True),
                          ws if q == 0 else (), [I(te)] if q == NQ - 1 else ())
                    wcu = W(te)
                    p("vector", lambda v, fu=fu: v.tensor_tensor(self.mblk[:], self.PF[1][:, 0:SPAN], self.gblk2[fu % 2][:], AL.mult),
                      (wcu, w_sil[fu], W(te)), [I(ve)])
                    self.setg(("pf1a", "pf1b"), W(ve))
                    w_m = W(ve)
                    ws3 = self.mmw(("pf2a", "pf2b"), (w_m,)) if fu == 0 else (w_m,)
                    for q in range(NQ):
                        p("tensor", lambda t, fu=fu, q=q: t.matmul(
                            self.PF[2][:, q * 512:(q + 1) * 512], self.rdT_sb[:, fu * 128:(fu + 1) * 128],
                            self.mblk[:, q * 512:(q + 1) * 512],
                            start=(fu == 0), stop=(fu == NI - 1)),
                          ws3 if q == 0 else (), [I(te)] if (fu == NI - 1 and q == NQ - 1) else ())
                if ft < NI and ft >= 1:
                    pass
            p("scalar", lambda s: s.copy(self.rd_sb[:], self.PF[2][:, 0:SPAN]), (W(te),), [I(se)])
            self.setg(("pf2a", "pf2b"), W(se))
            w_rd = W(se)
            ws = self.mmw(("pf0a", "pf0b"), (w_rd,))
            for q in range(NQ):
                p("tensor", lambda t, ew=ew, q=q: t.matmul(
                    self.PF[0][:, q * 512:(q + 1) * 512], ew["ud"][:],
                    self.rd_sb[:, q * 512:(q + 1) * 512], start=True, stop=True),
                  ws if q == 0 else (), [I(te)] if q == NQ - 1 else ())
            p("vector", lambda v, e=e: v.tensor_copy(self.BIGA[:, e * SPAN:(e + 1) * SPAN], self.PF[0][:, 0:SPAN]),
              (W(te),), [I(ve)])
            self.setg(("pf0a", "pf0b"), W(ve))
        if KDEBUG.get("rdu") and sp_i == 0:
            p("gpsimd", lambda g: g.dma_start(out=self.taps["rdu"][:, :], in_=self.BIGA[:, 0:NE * SPAN]), (W(ve),), [I(qg, 16)])

    def _out_span(self, sp_i, sub):
        p, I, W = self.op, self.I, self.W
        qd, qg, te, ve, os_ = self.qd, self.qg, self.te, self.ve, self.os
        D = self.D
        w_rdu = W(self.ve)
        w_oasm_free = W(os_)
        soff = sub * 512
        for fo in range(16):
            bi = fo % 2
            p("gpsimd", lambda g, fo=fo, b=bi: g.dma_start(
                out=self.cdW[b][:].rearrange("p (e f) -> p e f", f=128),
                in_=D["CdT"][fo].rearrange("e p f -> p e f")),
              (W(te), W(ve)), [I(qg, 16)])
            w_cd = W(qg)
            pk = ("pf0a", "pf0b", "pf1a", "pf1b")[fo % 4]
            pf = self.PF[(fo % 4) // 2]
            reg = pf[:, (fo % 2) * 512:(fo % 2) * 512 + 512]
            ws = self.mmw((pk,), (w_cd, w_rdu) if fo == 0 else (w_cd,))
            for e in range(NE + 1):
                if e < NE:
                    lhsT = self.cdW[bi][:, e * 128:(e + 1) * 128]
                    rhs = self.BIGA[:, e * SPAN + soff: e * SPAN + soff + 512]
                else:
                    lhsT = self.sCd_sb[:, fo * 128:(fo + 1) * 128]
                    rhs = self.srdu_sb[:, sp_i * SPAN + soff: sp_i * SPAN + soff + 512]
                p("tensor", lambda t, l=lhsT, r=rhs, e=e, reg=reg: t.matmul(
                    reg, l, r, start=(e == 0), stop=(e == NE)),
                  ws if e == 0 else (), [I(te)] if e == NE else ())
            wte = W(te)
            st = self.stagf[:, (fo % 2) * 512:(fo % 2) * 512 + 512]
            p("vector", lambda v, st=st, reg=reg: v.tensor_copy(st, reg), (wte,), [I(ve)])
            self.setg((pk,), W(ve))
            w_st = W(ve)
            tk = "pf2a" if fo % 2 == 0 else "pf2b"
            treg = self.PF[2][:, 0:512] if fo % 2 == 0 else self.PF[2][:, 512:1024]
            ws2 = self.mmw((tk,), (w_st,))
            for q in range(4):
                p("tensor", lambda t, st=st, q=q, treg=treg: t.transpose(
                    treg[:, q * 128:(q + 1) * 128], st[:, q * 128:(q + 1) * 128], self.idf[:]),
                  ws2 if q == 0 else (), [I(te)] if q == 3 else ())
            wtt = W(te)
            for q in range(4):
                p("vector", lambda v, q=q, fo=fo, treg=treg: v.tensor_copy(
                    self.OASM[:, q * HID + fo * 128: q * HID + (fo + 1) * 128],
                    treg[:, q * 128:(q + 1) * 128]),
                  (wtt, w_oasm_free) if (q == 0 and fo == 0) else ((wtt,) if q == 0 else ()),
                  [I(ve)] if q == 3 else [])
            self.setg((tk,), W(ve))
        w_asm = W(ve)
        for q in range(4):
            row = sp_i * SPAN + soff + q * 128
            p("sync", lambda s, row=row: s.dma_start(out=self.h1t[0][:], in_=self.h1_sp[row:row + 128, :]),
              (W(ve),), [I(qd, 16)])
            p("vector", lambda v, q=q: v.tensor_tensor(
                self.OASM[:, q * HID:(q + 1) * HID], self.OASM[:, q * HID:(q + 1) * HID],
                self.h1t[0][:], AL.add), (W(qd), w_asm), [I(ve)])
            p("sync", lambda s, row=row, q=q: s.dma_start(out=self.out[row:row + 128, :],
                                                          in_=self.OASM[:, q * HID:(q + 1) * HID]),
              (W(ve),), [I(os_, 16)])


def _build_cached():
    key = "nc"
    if key not in _CACHE:
        b = B()
        nc = b.build()
        _CACHE[key] = (nc, b.taps, b)
    return _CACHE[key]


def kernel(**inputs):
    nc, taps, _b = _build_cached()
    shared = _prep_shared(inputs)
    rot = _rot_tables()
    in_maps = [_prep_core(inputs, shared, rot, c) for c in range(8)]
    res = run_bass_kernel_spmd(nc, in_maps, core_ids=list(range(8)))
    out = np.empty((4, S, HID), np.float32)
    for c in range(8):
        bb, own = c // 2, c % 2
        out[bb, own * HF:(own + 1) * HF] = res.results[c]["out"]
    if KDEBUG:
        kernel.last_taps = [{k: res.results[c]["tap_" + k] for k in taps} for c in range(8)]
    return out


# version 04 (final)
# speedup vs baseline: 1.2123x; 1.0076x over previous
import contextlib
import numpy as np
import concourse.bass as bass
import concourse.mybir as mybir
from concourse.bass_utils import run_bass_kernel_spmd
from concourse.alu_op_type import AluOpType as AL

F32 = mybir.dt.float32
BF16 = mybir.dt.bfloat16
AF = mybir.ActivationFunctionType

HID = 2048
HD = 64
NH = 32
NHP = 16
ROT = 32
S = 4096
HF = 2048
C = 128
NCH = HF // C
EPS = 1e-5
NE = 32
IFF = 1408
NI = IFF // 128
SI = 2816
NSI = SI // 128
SPAN = 1024
NSPAN = HF // SPAN
THETA = 10000.0
SCALE_BASE = 512.0

KDEBUG = {}
KPHASES = [99]
KSUB = [99]
_CACHE = {}



def _gamma():
    return 1.0 - 2.0 ** (-5.0 - np.arange(NH, dtype=np.float64))


def _rot_tables():
    t = np.arange(S, dtype=np.float64)
    r = np.arange(0, ROT, 2, dtype=np.float64)
    inv_freq = 1.0 / THETA ** (r / ROT)
    freqs = np.repeat(t[:, None] * inv_freq[None, :], 2, axis=-1)
    base_scale = (r + 0.4 * ROT) / (1.4 * ROT)
    power = (t - S // 2) / SCALE_BASE
    scale = np.concatenate([base_scale[None, :] ** power[:, None]] * 2, axis=-1)
    cos = np.cos(freqs); sin = np.sin(freqs)
    sgn = np.where(np.arange(ROT) % 2 == 0, -1.0, 1.0)
    Cq = np.ones((HD, S)); Sq = np.zeros((HD, S))
    Ck = np.ones((HD, S)); Sk = np.zeros((HD, S))
    Cq[:ROT] = (cos * scale).T
    Ck[:ROT] = (cos / scale).T
    Sq[:ROT] = (sin * scale).T * sgn[:, None]
    Sk[:ROT] = (sin / scale).T * sgn[:, None]
    return [np.concatenate([a, a], 0).astype(np.float32) for a in (Cq, Sq, Ck, Sk)]


def _const_tables():
    g = _gamma()
    j = np.arange(C)
    d = j[None, :] - j[:, None]
    p128 = np.arange(128)
    dtb = np.zeros((128, NHP * 256), np.float32)
    for hp in range(NHP):
        for w in range(2):
            D = np.where(d >= 0, np.power(g[2 * hp + w], np.maximum(d, 0)), 0.0)
            dtb[:, hp * 256 + w * 128:hp * 256 + (w + 1) * 128] = D
    xib = np.zeros((128, NHP * 128), np.float32)
    etb = np.zeros((128, NHP * 128), np.float32)
    g128 = np.zeros((128, NHP), np.float32)
    for hp in range(NHP):
        gp = np.where(p128 >= 64, g[2 * hp + 1], g[2 * hp])
        xib[:, hp * 128:(hp + 1) * 128] = np.power(gp[:, None], (j + 1)[None, :])
        etb[:, hp * 128:(hp + 1) * 128] = np.power(gp[None, :], (C - 1 - j)[:, None])
        g128[:, hp] = np.power(gp, C)
    M = np.zeros((HD, HD), np.float32)
    for i in range(ROT // 2):
        M[2 * i, 2 * i + 1] = 1.0
        M[2 * i + 1, 2 * i] = 1.0
    bdp = np.zeros((128, 128), np.float32)
    bdp[:HD, :HD] = M; bdp[HD:, HD:] = M
    sel_e = np.zeros((32, NE * 128), np.float32)
    for e in range(NE):
        sel_e[e, e * 128:(e + 1) * 128] = 1.0
    sel64 = np.zeros((128, 2), np.float32)
    sel64[:64, 0] = 1.0; sel64[64:, 1] = 1.0
    rep2 = np.zeros((2, 128), np.float32)
    rep2[0, :64] = 1.0; rep2[1, 64:] = 1.0
    return dict(dtb=dtb, xib=xib, etb=etb, g128=g128, bdp=bdp,
                iden=np.eye(128, dtype=np.float32), sel_e=sel_e,
                sel64=sel64, rep2=rep2)


def _prep_shared(inp):
    f = np.float32
    ct = np.ascontiguousarray
    wqkv = np.asarray(inp["L0_attn_wqkv"], f)
    w3 = wqkv.reshape(NH, 3, HD, HID)
    d = {}
    d["wqT"] = ct(w3[:, 0].reshape(NH * HD, HID).T)
    d["wkT"] = ct(w3[:, 1].reshape(NH * HD, HID).T)
    d["wvT"] = ct(w3[:, 2].reshape(NH * HD, HID).T)
    d["wgT"] = ct(np.asarray(inp["L0_attn_wg"], f).T)
    d["woT"] = ct(np.asarray(inp["L0_attn_wo"], f).T)
    d["gwT"] = ct(np.asarray(inp["L0_ffn_gate_w"], f).T)
    d["RgT"] = ct(np.asarray(inp["L0_ffn_Rg"], f).T)
    d["RuT"] = ct(np.asarray(inp["L0_ffn_Ru"], f).T)
    d["RdT"] = ct(np.asarray(inp["L0_ffn_Rd"], f).T)
    for nm in ("Ug", "Cg", "Uu", "Cu", "Ud"):
        d[nm + "T"] = ct(np.asarray(inp["L0_ffn_" + nm], f).transpose(0, 2, 1))
    Cd = np.asarray(inp["L0_ffn_Cd"], f)
    d["CdT"] = ct(Cd.reshape(NE, 16, 128, 128).transpose(1, 0, 3, 2))
    for nm in ("s_Rg", "s_Ru", "s_Ug", "s_Uu", "s_Ud", "s_Cg", "s_Cu", "s_Rd", "s_Cd"):
        d[nm.replace("_", "") + "T"] = ct(np.asarray(inp["L0_ffn_" + nm], f).T)
    d["gnw2"] = ct(np.asarray(inp["L0_attn_gn_weight"], f).reshape(NHP, 2).T)
    d["gnb2"] = ct(np.asarray(inp["L0_attn_gn_bias"], f).reshape(NHP, 2).T)
    d.update(_const_tables())
    return d


def _prep_core(inp, shared, rot, core):
    b, own = core // 2, core % 2
    xb = np.asarray(inp["L0_attn_x"], np.float32)[b]
    perm = np.r_[own * HF:(own + 1) * HF, (1 - own) * HF:(2 - own) * HF]
    m = dict(shared)
    m["x"] = np.ascontiguousarray(xb[perm])
    m["cq"] = np.ascontiguousarray(rot[0][:, perm])
    m["sq"] = np.ascontiguousarray(rot[1][:, perm])
    m["ck"] = np.ascontiguousarray(rot[2][:, perm])
    m["sk"] = np.ascontiguousarray(rot[3][:, perm])
    m["mA"] = np.full((128, 1), float(own), np.float32)
    m["mB"] = np.full((128, 1), float(1 - own), np.float32)
    return m



class Sem:
    __slots__ = ("h", "v")

    def __init__(self, h):
        self.h = h
        self.v = 0


class B:
    def __init__(self):
        nc = bass.Bass("TRN2", target_bir_lowering=False)
        self.nc = nc
        self.EL = {k: [] for k in ("sync", "tensor", "vector", "scalar", "gpsimd")}
        self.bankg = {}
        self.es = contextlib.ExitStack()
        self.es.__enter__()
        self.taps = {}
        self._declare()

    def op(self, eng, fn=None, waits=(), incs=()):
        ws = tuple(w for w in waits if w is not None)
        self.EL[eng].append((fn, ws, tuple(incs)))

    def W(self, s, v=None):
        return (s.h, s.v if v is None else v)

    def I(self, s, n=1):
        s.v += n
        return (s.h, n)

    def mmw(self, keys, extra=()):
        ws = [w for w in extra if w is not None]
        seen = set()
        for k in keys:
            g = self.bankg.get(k)
            if g is not None and g not in seen:
                ws.append(g); seen.add(g)
        return ws

    def setg(self, keys, w):
        for k in keys:
            self.bankg[k] = w

    def _declare(self):
        nc = self.nc
        self.D = {}

        def din(name, shape, dt=F32):
            self.D[name] = nc.dram_tensor(name, shape, dt, kind="ExternalInput").ap()

        din("x", [S, HID])
        for n in ("cq", "sq", "ck", "sk"):
            din(n, [128, S])
        for n in ("wqT", "wkT", "wvT", "wgT", "woT"):
            din(n, [HID, HID])
        din("gwT", [HID, 32])
        din("RgT", [HID, 128]); din("RuT", [HID, 128]); din("RdT", [IFF, 128])
        din("UgT", [NE, 128, 128]); din("CgT", [NE, 128, IFF])
        din("UuT", [NE, 128, 128]); din("CuT", [NE, 128, IFF])
        din("UdT", [NE, 128, 128]); din("CdT", [16, NE, 128, 128])
        din("sRgT", [HID, 128]); din("sRuT", [HID, 128])
        din("sUgT", [128, 128]); din("sUuT", [128, 128]); din("sUdT", [128, 128])
        din("sCgT", [128, SI]); din("sCuT", [128, SI])
        din("sRdT", [SI, 128]); din("sCdT", [128, HID])
        din("gnw2", [2, NHP]); din("gnb2", [2, NHP])
        din("dtb", [128, NHP * 256]); din("xib", [128, NHP * 128])
        din("etb", [128, NHP * 128]); din("g128", [128, NHP])
        din("bdp", [128, 128]); din("iden", [128, 128])
        din("sel_e", [32, NE * 128]); din("sel64", [128, 2]); din("rep2", [2, 128])
        din("mA", [128, 1]); din("mB", [128, 1])
        self.out = nc.dram_tensor("out", [HF, HID], F32, kind="ExternalOutput").ap()
        self.ln1_sp = nc.dram_tensor("ln1_sp", [S, HID], BF16, kind="Internal").ap()
        self.ln2_sp = nc.dram_tensor("ln2_sp", [HF, HID], BF16, kind="Internal").ap()
        self.kv_sp = nc.dram_tensor("kv_sp", [NHP, 2, 128, HF], BF16, kind="Internal").ap()
        self.ret_sp = nc.dram_tensor("ret_sp", [NHP, 128, HF], F32, kind="Internal").ap()
        self.h1_sp = nc.dram_tensor("h1_sp", [HF, HID], F32, kind="Internal").ap()
        self.mish_sp = nc.dram_tensor("mish_sp", [16, 128, HF], BF16, kind="Internal").ap()

        if KDEBUG:
            def tap(name, shape, dt=F32):
                self.taps[name] = nc.dram_tensor("tap_" + name, shape, dt, kind="ExternalOutput").ap()
            if KDEBUG.get("ln1"): tap("ln1", [S, HID], BF16)
            if KDEBUG.get("p0dbg"): tap("p0dbg", [128, 32 * 8])
            if KDEBUG.get("p0dbg"): tap("p0dbg2", [128, 32 * 4])
            if KDEBUG.get("spb"): tap("spb", [128, 1024])
            if KDEBUG.get("ret"): tap("ret", [NHP, 128, HF])
            if KDEBUG.get("gn"): tap("gn", [128, 2 * NHP])
            if KDEBUG.get("h1"): tap("h1", [HF, HID])
            if KDEBUG.get("we"): tap("we", [128, NHP * 32])
            if KDEBUG.get("rg"): tap("rg", [128, HF], BF16)
            if KDEBUG.get("srdu"): tap("srdu", [128, HF], BF16)
            if KDEBUG.get("rdu"): tap("rdu", [128, NE * SPAN], BF16)

        ARN_N = 95936
        self.ARN = self.es.enter_context(nc.sbuf_tensor("b_ARN", [128, ARN_N], BF16))
        reqs = []

        def rq(name, units, ph0, ph1, f32=False, nparts=128):
            reqs.append((name, units, f32, ph0, ph1, nparts))

        rq("dtb", NHP * 256, 0, 3)
        rq("xib", NHP * 128, 0, 3)
        rq("etb", NHP * 128, 0, 3)
        rq("sel_e", NE * 128, 5, 7, nparts=32)
        rq("BIGA", 16 * HF, 1, 7)
        for n in ("cqb", "sqb", "ckb", "skb"):
            rq(n, HF, 1, 3)
        for n in ("qraw", "kraw", "vfm", "vtm", "keta", "xiq"):
            rq(n, HF, 1, 3)
        rq("qp", HF, 1, 4)
        rq("kp", HF, 1, 3)
        rq("t2", HF, 1, 4)
        rq("stagf", 2 * HF, 0, 7, f32=True)
        rq("wq0", HF, 2, 3); rq("wq1", HF, 2, 3)
        rq("wk0", HF, 1, 2); rq("wk1", HF, 1, 2)
        rq("wv0", HF, 1, 2); rq("wv1", HF, 1, 2)
        rq("wgen0", HF, 4, 5); rq("wgen1", HF, 4, 5)
        rq("xtb0", HID, 0, 1); rq("xtb1", HID, 0, 1)
        rq("lnt0", HID, 0, 1); rq("lnt1", HID, 0, 1)
        rq("ln2t0", HID, 4, 4); rq("ln2t1", HID, 4, 4)
        rq("xtf0", 2 * HID, 4, 4, f32=True); rq("xtf1", 2 * HID, 4, 4, f32=True)
        rq("h1t0", 2 * HID, 4, 7, f32=True); rq("h1t1", 2 * HID, 4, 4, f32=True)
        rq("strip0", 2 * HID, 4, 4, f32=True); rq("strip1", 2 * HID, 4, 4, f32=True)
        rq("mstrip0", HID, 4, 4); rq("mstrip1", HID, 4, 4)
        rq("msh0", 1024, 4, 4); rq("msh1", 1024, 4, 4)
        rq("rg_sb", HF, 5, 7); rq("ru_sb", HF, 5, 7)
        rq("srg_sb", HF, 5, 6); rq("sru_sb", HF, 5, 6)
        rq("srdu_sb", HF, 5, 7)
        rq("weT", HF, 5, 7, nparts=32)
        rq("gwsb", NHP * 32, 5, 5)
        rq("rdT_sb", NI * 128, 5, 7)
        rq("sCg_sb", SI, 5, 6); rq("sCu_sb", SI, 5, 6)
        rq("sRd_sb", NSI * 128, 5, 6)
        rq("sU_sb", 3 * 128, 5, 6)
        rq("sCd_sb", HID, 5, 7)
        rq("sgb", HF, 6, 6)
        rq("hsg", HF, 6, 6); rq("hsu", HF, 6, 6)
        rq("lgf", 2 * HF, 5, 5, f32=True, nparts=32)
        rq("lgtm", 2 * NHP * 32, 5, 5, f32=True)
        rq("wetm", 2 * NHP * 32, 5, 5, f32=True)
        rq("smX", 6 * HF, 6, 6)
        rq("hg_sb", SPAN, 7, 7); rq("hu_sb", SPAN, 7, 7)
        rq("rd_sb", SPAN, 7, 7); rq("wb_sb", SPAN, 7, 7)
        rq("gblk", SPAN, 7, 7); rq("gblkB", SPAN, 7, 7); rq("mblk", SPAN, 7, 7)
        for i in range(2):
            rq(f"ug{i}", 128, 7, 7); rq(f"cg{i}", IFF, 7, 7)
            rq(f"uu{i}", 128, 7, 7); rq(f"cu{i}", IFF, 7, 7)
            rq(f"ud{i}", 128, 7, 7)
            rq(f"cdw{i}", NE * 128, 7, 7)
        rq("OASM", 2 * 4 * HID, 7, 7, f32=True)
        rq("BIGF", 2 * 1024, 4, 4, f32=True)
        placed = []
        self.buf = {}
        for name, units, f32, ph0, ph1, nparts in sorted(reqs, key=lambda r: -r[1]):
            cand = 0
            while True:
                lo, hi = cand, cand + units
                clash = None
                for (o, u, a0, a1) in placed:
                    if not (hi <= o or o + u <= lo) and not (ph1 < a0 or a1 < ph0):
                        clash = o + u
                        break
                if clash is None:
                    break
                cand = (clash + 15) // 16 * 16
            assert cand + units <= ARN_N, f"arena overflow at {name}: {cand}+{units}"
            placed.append((cand, units, ph0, ph1))
            ap = self.ARN[0:nparts, cand:cand + units]
            if f32:
                ap = ap.bitcast(F32)
            self.buf[name] = ap
        self._arena_peak = max(o + u for (o, u, _, _) in placed)
        bf = self.buf
        self.dtb, self.xib, self.etb, self.sel_e = bf["dtb"], bf["xib"], bf["etb"], bf["sel_e"]
        self.BIGA = bf["BIGA"]
        self.cqb, self.sqb, self.ckb, self.skb = bf["cqb"], bf["sqb"], bf["ckb"], bf["skb"]
        self.qraw, self.kraw, self.qp, self.kp = bf["qraw"], bf["kraw"], bf["qp"], bf["kp"]
        self.t2, self.vfm, self.vtm, self.keta, self.xiq = bf["t2"], bf["vfm"], bf["vtm"], bf["keta"], bf["xiq"]
        self.stagf = bf["stagf"]
        self.wqs = [bf["wq0"], bf["wq1"]]
        self.wks = [bf["wk0"], bf["wk1"]]
        self.wvs = [bf["wv0"], bf["wv1"]]
        self.wgen = [bf["wgen0"], bf["wgen1"]]
        self.rg_sb, self.ru_sb = bf["rg_sb"], bf["ru_sb"]
        self.srg_sb, self.sru_sb, self.srdu_sb = bf["srg_sb"], bf["sru_sb"], bf["srdu_sb"]
        self.weT, self.gwsb, self.rdT_sb = bf["weT"], bf["gwsb"], bf["rdT_sb"]
        self.sCg_sb, self.sCu_sb, self.sRd_sb = bf["sCg_sb"], bf["sCu_sb"], bf["sRd_sb"]
        self.sU_sb, self.sCd_sb, self.sgb = bf["sU_sb"], bf["sCd_sb"], bf["sgb"]
        self.hsg, self.hsu = bf["hsg"], bf["hsu"]
        self.lgf, self.lgtm, self.wetm = bf["lgf"], bf["lgtm"], bf["wetm"]
        self.smX = bf["smX"]
        self.hg_sb, self.hu_sb, self.rd_sb, self.wb_sb = bf["hg_sb"], bf["hu_sb"], bf["rd_sb"], bf["wb_sb"]
        self.gblk, self.mblk = bf["gblk"], bf["mblk"]
        self.gblk2 = [bf["gblk"], bf["gblkB"]]
        self.eW = [dict(ug=bf[f"ug{i}"], cg=bf[f"cg{i}"], uu=bf[f"uu{i}"], cu=bf[f"cu{i}"], ud=bf[f"ud{i}"]) for i in range(2)]
        self.cdW = [bf["cdw0"], bf["cdw1"]]
        self.OASM = bf["OASM"]
        self.BIGF = bf["BIGF"]
        self.xt = [bf["xtb0"], bf["xtb1"]]
        self.lnt = [bf["lnt0"], bf["lnt1"]]
        self.ln2t = [bf["ln2t0"], bf["ln2t1"]]
        self.xtf = [bf["xtf0"], bf["xtf1"]]
        self.h1t = [bf["h1t0"], bf["h1t1"]]
        self.strip = [bf["strip0"], bf["strip1"]]
        self.mstrip = [bf["mstrip0"], bf["mstrip1"]]
        self.msh = [bf["msh0"], bf["msh1"]]
        sb = lambda name, shape, dt=BF16: self.es.enter_context(nc.sbuf_tensor(name, shape, dt))
        self.g128 = sb("s_g128", [128, NHP], F32)
        self.bdp = sb("s_bdp", [128, 128])
        self.idb = sb("s_idb", [128, 128])
        self.idf = sb("s_idf", [128, 128], F32)
        self.sel64 = sb("s_sel64", [128, 2], F32)
        self.rep2 = sb("s_rep2", [2, 128], F32)
        self.gnw2 = sb("s_gnw2", [2, NHP], F32)
        self.gnb2 = sb("s_gnb2", [2, NHP], F32)
        self.mAb = sb("s_mA", [128, 1], F32)
        self.mBb = sb("s_mB", [128, 1], F32)
        self.S_a = sb("s_Sa", [128, 1024], F32)
        self.S_b = sb("s_Sb", [128, 1024], F32)
        self.S_hist = sb("s_Shist", [128, NCH * 64])
        self.stats = sb("s_stats", [128, NHP * 120], F32)
        self.mv_all = sb("s_mv", [128, 2 * NHP], F32)
        self.st3 = sb("s_st3", [128, 3 * NHP], F32)
        self.fin2 = sb("s_fin2", [2, 4 * NHP], F32)
        self.A_vec = sb("s_Avec", [128, NHP], F32)
        self.C_vec = sb("s_Cvec", [128, NHP], F32)
        self.sm1 = sb("s_sm1", [128, 8], F32)
        self.sm4 = sb("s_sm4", [2, 3 * NHP], F32)
        self.lst = sb("l_lst", [128, 24], F32)
        self.ssep = sb("s_ssep", [128, 2], F32)
        self.lmv = sb("l_lmv", [128, 2], F32)
        self.PF = [self.es.enter_context(nc.psum_tensor(f"p_f{i}", [128, 1024], F32)) for i in range(3)]
        self.PBT = self.es.enter_context(nc.psum_tensor("p_bt", [128, 2048], BF16))
        self.PBTF = self.PBT[:, :].bitcast(F32)
        for n in ("qd", "qg", "te", "ve", "se", "os"):
            setattr(self, n, Sem(self.es.enter_context(nc.semaphore("sem_" + n))))

    def vsep(self):
        self.op("vector", lambda v: v.tensor_copy(self.ssep[:, 0:1], self.ssep[:, 1:2]))

    PK4 = ("pf0a", "pf0b", "pf1a", "pf1b")

    def pf_half(self, sp, bs="A"):
        if bs == "A":
            pf = self.PF[sp // 2]
            return pf[:, (sp % 2) * 512:(sp % 2 + 1) * 512], ("pf0a", "pf0b", "pf1a", "pf1b")[sp]
        src_ = self.PF[2] if sp < 2 else self.PBTF
        return src_[:, (sp % 2) * 512:(sp % 2 + 1) * 512], ("pf2a", "pf2b", "pbt", "pbt")[sp]

    def bs_keys(self, bs):
        return self.PK4 if bs == "A" else ("pf2a", "pf2b", "pbt")

    def bs_buf(self, h, bs):
        return self.PF[h] if bs == "A" else (self.PF[2] if h == 0 else self.PBTF)

    def emit_ln(self, xbuf, outbuf, first_wait, out_wait):
        p, I, W = self.op, self.I, self.W
        ve, se = self.ve, self.se
        sm1 = self.sm1
        p("vector", lambda v, a=xbuf: v.tensor_reduce(sm1[:, 5:6], a[:], mybir.AxisListType.X, AL.add),
          (first_wait,))
        p("scalar", lambda s, a=xbuf: s.activation(self.stagf[:, 0:2048], a[:], AF.Square),
          (first_wait,), [I(se)])
        p("vector", lambda v: v.tensor_reduce(sm1[:, 6:7], self.stagf[:, 0:2048], mybir.AxisListType.X, AL.add),
          (W(se),))
        p("vector", lambda v: v.tensor_scalar(sm1[:, 5:6], sm1[:, 5:6], 1.0 / HID, None, AL.mult))
        p("vector", lambda v: v.tensor_scalar(sm1[:, 6:7], sm1[:, 6:7], 1.0 / HID, None, AL.mult))
        p("vector", lambda v: v.tensor_tensor(sm1[:, 3:4], sm1[:, 5:6], sm1[:, 5:6], AL.mult))
        self.vsep()
        p("vector", lambda v: v.tensor_tensor(sm1[:, 0:1], sm1[:, 6:7], sm1[:, 3:4], AL.subtract))
        self.vsep()
        p("vector", lambda v: v.tensor_scalar(sm1[:, 0:1], sm1[:, 0:1], EPS, None, AL.add), (), [I(ve)])
        p("scalar", lambda s: s.activation(sm1[:, 1:2], sm1[:, 0:1], AF.Sqrt), (W(ve),), [I(se)])
        p("vector", lambda v: v.reciprocal(sm1[:, 2:3], sm1[:, 1:2]), (W(se),), [I(ve)])
        p("scalar", lambda s: s.activation(sm1[:, 7:8], sm1[:, 2:3], AF.Identity), (W(ve),))
        p("scalar", lambda s: s.activation(sm1[:, 3:4], sm1[:, 2:3], AF.Square), (), [I(se)])
        p("vector", lambda v: v.tensor_tensor(sm1[:, 3:4], sm1[:, 3:4], sm1[:, 0:1], AL.mult), (W(se),))
        self.vsep()
        p("vector", lambda v: v.tensor_scalar(sm1[:, 3:4], sm1[:, 3:4], -0.5, 1.5, AL.mult, AL.add))
        self.vsep()
        p("vector", lambda v: v.tensor_tensor(sm1[:, 2:3], sm1[:, 7:8], sm1[:, 3:4], AL.mult))
        self.vsep()
        p("vector", lambda v: v.tensor_tensor(sm1[:, 4:5], sm1[:, 5:6], sm1[:, 2:3], AL.mult))
        self.vsep()
        p("vector", lambda v: v.tensor_scalar(sm1[:, 4:5], sm1[:, 4:5], -1.0, None, AL.mult))
        self.vsep()
        p("vector", lambda v, a=xbuf, o=outbuf: v.tensor_scalar(o[:], a[:], sm1[:, 2:3], sm1[:, 4:5], AL.mult, AL.add),
          (out_wait,), [I(ve)])
        return W(ve)

    def load_lnfm(self, src, rows0, wait):
        p, I, W = self.op, self.I, self.W
        waits = wait if isinstance(wait, tuple) and (not wait or isinstance(wait[0], tuple)) else (wait,)
        for cc in range(16):
            p("sync", lambda s, c=cc: s.dma_start(
                out=self.BIGA[:, c * HF:(c + 1) * HF],
                in_=src[rows0:rows0 + HF, c * 128:(c + 1) * 128], transpose=True),
              waits if cc == 0 else (), [I(self.qd, 16)])
        return W(self.qd)

    def load_rot(self, half0, which, waits=()):
        p, I, W = self.op, self.I, self.W
        for i, (name, buf) in enumerate(which):
            p("gpsimd", lambda g, n=name, bf=buf: g.dma_start(out=bf[:], in_=self.D[n][:, half0:half0 + HF]),
              waits if i == 0 else (), [I(self.qg, 16)])
        return W(self.qg)

    def wslice(self, dst, wname, hp, waits=()):
        p, I, W = self.op, self.I, self.W
        p("gpsimd", lambda g, d=dst, n=wname, h=hp: g.dma_start(
            out=d[:].rearrange("p (c f) -> p c f", f=128),
            in_=self.D[n][:, h * 128:(h + 1) * 128].rearrange("(c p) f -> p c f", p=128)),
          waits, [I(self.qg, 16)])
        return W(self.qg)

    def proj(self, wbuf, waits, bs="A"):
        p, I, W = self.op, self.I, self.W
        ws = self.mmw(self.bs_keys(bs), waits)
        for fi in range(16):
            for sp in range(4):
                tgt, _ = self.pf_half(sp, bs)
                p("tensor", lambda t, tg=tgt, w=wbuf, fi=fi, sp=sp: t.matmul(
                    tg, w[:, fi * 128:(fi + 1) * 128],
                    self.BIGA[:, fi * HF + sp * 512: fi * HF + sp * 512 + 512],
                    start=(fi == 0), stop=(fi == 15)),
                  ws if (fi == 0 and sp == 0) else (),
                  [I(self.te)] if (fi == 15 and sp == 3) else ())
        return W(self.te)

    def evac2(self, eng, dstbuf, wait, bs="A"):
        p, I, W = self.op, self.I, self.W
        sem = self.ve if eng == "vector" else self.se
        for h in range(2):
            pbuf = self.bs_buf(h, bs)
            if eng == "vector":
                p(eng, lambda E, d=dstbuf, h=h, pb=pbuf: E.tensor_copy(d[:, h * 1024:(h + 1) * 1024], pb[:, :] if pb.shape[1] == 1024 else pb[:, 0:1024]),
                  (wait,) if h == 0 else (), [I(sem)] if h == 1 else [])
            else:
                p(eng, lambda E, d=dstbuf, h=h, pb=pbuf: E.copy(d[:, h * 1024:(h + 1) * 1024], pb[:, :] if pb.shape[1] == 1024 else pb[:, 0:1024]),
                  (wait,) if h == 0 else (), [I(sem)] if h == 1 else [])
        w = W(sem)
        self.setg(self.bs_keys(bs), w)
        return w

    def rot_apply(self, rawbuf, outbuf, ctab, stab, wait_raw, bs="A"):
        p, I, W = self.op, self.I, self.W
        ws = self.mmw(self.bs_keys(bs), (wait_raw, self.const_done))
        for sp in range(4):
            tgt, _ = self.pf_half(sp, bs)
            p("tensor", lambda t, tg=tgt, r=rawbuf, sp=sp: t.matmul(
                tg, self.bdp[:], r[:, sp * 512:(sp + 1) * 512], start=True, stop=True),
              ws if sp == 0 else (), [I(self.te)] if sp == 3 else ())
        wte = W(self.te)
        for h in range(2):
            pbuf = self.bs_buf(h, bs)
            p("vector", lambda v, h=h, st=stab, pb=pbuf: v.tensor_tensor(
                self.t2[:, h * 1024:(h + 1) * 1024], pb[:, :] if pb.shape[1] == 1024 else pb[:, 0:1024],
                st[:, h * 1024:(h + 1) * 1024], AL.mult),
              (wte,) if h == 0 else (), [I(self.ve)] if h == 1 else [])
        self.setg(self.bs_keys(bs), W(self.ve))
        p("vector", lambda v, r=rawbuf, o=outbuf, ct=ctab: v.tensor_tensor(o[:], r[:], ct[:], AL.mult))
        p("vector", lambda v, o=outbuf: v.tensor_tensor(o[:], o[:], self.t2[:], AL.add), (), [I(self.ve)])
        return W(self.ve)

    def transposes_to(self, outbuf, srcbuf, wait_src, eta_hp=None):
        p, I, W = self.op, self.I, self.W
        ws = self.mmw(("pbt",), (wait_src, self.const_done))
        for cc in range(16):
            p("tensor", lambda t, c=cc, sb_=srcbuf: t.transpose(
                self.PBT[:, c * 128:(c + 1) * 128], sb_[:, c * 128:(c + 1) * 128], self.idb[:]),
              ws if cc == 0 else (), [I(self.te)] if cc == 15 else ())
        wte = W(self.te)
        if eta_hp is None:
            p("vector", lambda v, o=outbuf: v.tensor_copy(o[:], self.PBT[:, :]), (wte,), [I(self.ve)])
        else:
            p("vector", lambda v, o=outbuf, hp=eta_hp: v.tensor_tensor(
                o[:].rearrange("p (c f) -> p c f", f=128),
                self.PBT[:, :].rearrange("p (c f) -> p c f", f=128),
                self.etb[:, None, hp * 128:(hp + 1) * 128].broadcast_to((128, 16, 128)), AL.mult),
              (wte,), [I(self.ve)])
        self.setg(("pbt",), W(self.ve))
        return W(self.ve)

    def state_mm4(self, waits):
        p, I, W = self.op, self.I, self.W
        ws = self.mmw(("pf2a", "pf2b"), waits)
        for cc in range(NCH):
            for w in range(2):
                p("tensor", lambda t, c=cc, w=w: t.matmul(
                    self.PF[2][w * 64:(w + 1) * 64, c * 64:(c + 1) * 64],
                    self.keta[:, c * 128 + w * 64: c * 128 + w * 64 + 64],
                    self.vtm[:, c * 128 + w * 64: c * 128 + w * 64 + 64],
                    start=True, stop=True, tile_position=(0, w * 64)),
                  ws if (cc == 0 and w == 0) else (),
                  [I(self.te)] if (cc == NCH - 1 and w == 1) else ())
        return W(self.te)

    def state_chain(self, hp, wait_mm4, hist):
        p, I, W = self.op, self.I, self.W
        bufs = [self.S_a, self.S_b]
        for cc in range(NCH):
            src, dst = bufs[cc % 2], bufs[(cc + 1) % 2]
            if hist:
                p("vector", lambda v, c=cc, s=src, hp=hp: v.tensor_copy(
                    self.S_hist[:, c * 64:(c + 1) * 64], s[:, hp * 64:(hp + 1) * 64]))
            p("vector", lambda v, c=cc, s=src, d=dst, hp=hp: v.scalar_tensor_tensor(
                d[:, hp * 64:(hp + 1) * 64], s[:, hp * 64:(hp + 1) * 64], self.g128[:, hp:hp + 1],
                self.PF[2][:, c * 64:(c + 1) * 64], AL.mult, AL.add),
              (wait_mm4,) if cc == 0 else (),
              [I(self.ve)] if cc == NCH - 1 else [])
        self.setg(("pf2a", "pf2b"), W(self.ve))
        return W(self.ve)

    def retention_chunks(self, hp, waits_ready, do_spill, stats_off, spill_wait=None):
        p, I, W = self.op, self.I, self.W
        wv_hist = W(self.ve)

        def emit_mm1(cc, first):
            par = cc % 2
            ak = ("pf0a", "pf0b")[par]
            bk = ("pf1a", "pf1b")[par]
            atA = self.PF[0][:, par * 512: par * 512 + 128]
            atB = self.PF[1][:, par * 512: par * 512 + 128]
            ws = self.mmw((ak,), tuple(waits_ready) + (wv_hist,) if first else ())
            p("tensor", lambda t, c=cc, a=atA: t.matmul(
                a, self.kp[0:64, c * 128:(c + 1) * 128], self.qp[0:64, c * 128:(c + 1) * 128],
                start=True, stop=True), ws, [I(self.te)])
            ws = self.mmw((bk,))
            p("tensor", lambda t, c=cc, a=atB: t.matmul(
                a, self.kp[64:128, c * 128:(c + 1) * 128], self.qp[64:128, c * 128:(c + 1) * 128],
                start=True, stop=True), ws, [I(self.te)])
            return W(self.te)

        w_at = {0: emit_mm1(0, True)}
        for cc in range(NCH):
            par = cc % 2
            ak = ("pf0a", "pf0b")[par]
            bk = ("pf1a", "pf1b")[par]
            gk = ("pf2a", "pf2b")[par]
            atA = self.PF[0][:, par * 512: par * 512 + 128]
            atB = self.PF[1][:, par * 512: par * 512 + 128]
            crB = self.PF[1][:, par * 512 + 128: par * 512 + 256]
            rg = self.PF[2][:, par * 512: par * 512 + 128]
            p("vector", lambda v, c=cc, a=atA, hp=hp: v.tensor_tensor(
                self.t2[:, (c % 4) * 256:(c % 4) * 256 + 128], a, self.dtb[:, hp * 256:hp * 256 + 128], AL.mult),
              (w_at[cc],), [I(self.ve)])
            self.setg((ak,), W(self.ve))
            p("vector", lambda v, c=cc, a=atB, hp=hp: v.tensor_tensor(
                self.t2[:, (c % 4) * 256 + 128:(c % 4) * 256 + 256], a, self.dtb[:, hp * 256 + 128:hp * 256 + 256], AL.mult),
              (), [I(self.ve)])
            w_atd = W(self.ve)
            if cc + 1 < NCH:
                w_at[cc + 1] = emit_mm1(cc + 1, False)
            ws2 = self.mmw((gk,), (w_atd,))
            p("tensor", lambda t, c=cc, r=rg: t.matmul(
                r[0:64, :], self.vtm[:, c * 128: c * 128 + 64],
                self.t2[:, (c % 4) * 256:(c % 4) * 256 + 128],
                start=True, stop=False, tile_position=(0, 0), skip_group_check=True), ws2)
            p("tensor", lambda t, c=cc, r=rg: t.matmul(
                r[64:128, :], self.vtm[:, c * 128 + 64: c * 128 + 128],
                self.t2[:, (c % 4) * 256 + 128:(c % 4) * 256 + 256],
                start=True, stop=True, tile_position=(0, 64), skip_group_check=True))
            p("tensor", lambda t, c=cc, r=rg: t.matmul(
                r[0:64, :], self.S_hist[0:64, c * 64:(c + 1) * 64],
                self.xiq[0:64, c * 128:(c + 1) * 128],
                start=False, stop=True, tile_position=(0, 0), skip_group_check=True),
              (), [I(self.te)])
            p("tensor", lambda t, c=cc, a=crB: t.matmul(
                a[64:128, :], self.S_hist[64:128, c * 64:(c + 1) * 64],
                self.xiq[64:128, c * 128:(c + 1) * 128],
                start=True, stop=True, tile_position=(64, 64), skip_group_check=True),
              (), [I(self.te)])
            w_ret = W(self.te)
            p("vector", lambda v, c=cc, r=rg: v.tensor_copy(self.stagf[:, c * 128:(c + 1) * 128], r),
              (w_ret, spill_wait) if (cc == 0 and do_spill) else (w_ret,), [I(self.ve)])
            self.setg((gk,), W(self.ve))
            p("vector", lambda v, c=cc, a=crB: v.tensor_tensor(
                self.stagf[64:128, c * 128:(c + 1) * 128], a[64:128, :],
                self.stagf[64:128, c * 128:(c + 1) * 128], AL.add), (), [I(self.ve)])
            self.setg((bk,), W(self.ve))
            if not do_spill:
                p("vector", lambda v, c=cc, so=stats_off: v.bn_stats(
                    self.stats[:, so + c * 6: so + (c + 1) * 6], self.stagf[:, c * 128:(c + 1) * 128]),
                  (), [I(self.ve)])
        if do_spill:
            for i in range(4):
                p("vector", lambda v, i=i, so=stats_off: v.bn_stats(
                    self.stats[:, so + i * 6: so + (i + 1) * 6], self.stagf[:, i * 512:(i + 1) * 512]))
            p("vector", lambda v: v.tensor_copy(self.sm1[:, 5:6], self.stagf[:, 0:1]), (), [I(self.ve)])
        return W(self.ve)

    def build(self):
        p, I, W = self.op, self.I, self.W
        qd, qg, te, ve, se, os_ = self.qd, self.qg, self.te, self.ve, self.se, self.os
        D = self.D

        for dst, src in ((self.dtb, D["dtb"]), (self.xib, D["xib"]), (self.etb, D["etb"]),
                         (self.bdp, D["bdp"]), (self.idb, D["iden"]),
                         (self.g128, D["g128"]), (self.idf, D["iden"]), (self.sel64, D["sel64"]),
                         (self.rep2, D["rep2"]), (self.gnw2, D["gnw2"]), (self.gnb2, D["gnb2"]),
                         (self.mAb, D["mA"]), (self.mBb, D["mB"])):
            ndim = len(src.shape)
            p("gpsimd", lambda g, a=dst, b=src: g.dma_start(out=a[:], in_=b[:, :] if len(b.shape) == 2 else b[:, :, :]),
              (), [I(qg, 16)])
        self.const_done = W(qg)

        tile_order = list(range(16, 32)) + list(range(16))
        lnw = {}
        for idx, t in enumerate(tile_order):
            bi = idx % 2
            p("gpsimd", lambda g, b=bi, t=t: g.dma_start(out=self.xt[b][:], in_=D["x"][t * 128:(t + 1) * 128, :]),
              (W(ve),) if idx >= 2 else (), [I(qg, 16)])
            w_apply = self.emit_ln(self.xt[bi], self.lnt[bi], W(qg), W(qg) if idx >= 2 else None)
            p("gpsimd", lambda g, b=bi, t=t: g.dma_start(out=self.ln1_sp[t * 128:(t + 1) * 128, :], in_=self.lnt[b][:]),
              (w_apply,), [I(qg, 16)])
            if KDEBUG.get("p0dbg"):
                p("gpsimd", lambda g, t=t: g.dma_start(out=self.taps["p0dbg"][:, t * 8:(t + 1) * 8], in_=self.sm1[:]),
                  (), [I(qg, 16)])
                p("gpsimd", lambda g, t=t: g.dma_start(out=self.taps["p0dbg2"][:, t * 4:t * 4 + 2], in_=self.lmv[:]),
                  (), [I(qg, 16)])
                p("gpsimd", lambda g, t=t: g.dma_start(out=self.taps["p0dbg2"][:, t * 4 + 2:t * 4 + 4], in_=self.sm1[:, 5:7]),
                  (), [I(qg, 16)])
            if KDEBUG.get("ln1"):
                p("gpsimd", lambda g, b=bi, t=t: g.dma_start(out=self.taps["ln1"][t * 128:(t + 1) * 128, :], in_=self.lnt[b][:]),
                  (), [I(qg, 16)])
            lnw[t] = W(qg)
        ln_pb_done, ln_pa_done = lnw[31], lnw[15]
        if KPHASES[0] < 1:
            return self._finish_early()

        w_lnfm = self.load_lnfm(self.ln1_sp, HF, ln_pb_done)
        w_rot = self.load_rot(HF, (("ck", self.ckb), ("sk", self.skb)))
        p("vector", lambda v: v.memset(self.S_a[:], 0.0), (), [I(ve)])
        for hp in range(NHP):
            bi = hp % 2
            wk_w = self.wslice(self.wks[bi], "wkT", hp, (W(te),) if hp >= 2 else ())
            wv_w = self.wslice(self.wvs[bi], "wvT", hp)
            wte = self.proj(self.wks[bi], (wk_w, w_lnfm if hp == 0 else None,
                                           w_rot if hp == 0 else None,
                                           self.const_done if hp == 0 else None), bs="A")
            wte_v = self.proj(self.wvs[bi], (wv_w,), bs="B")
            w_kraw = self.evac2("scalar", self.kraw, wte, bs="A")
            w_kp = self.rot_apply(self.kraw, self.kp, self.ckb, self.skb, w_kraw, bs="A")
            w_vfm = self.evac2("vector", self.vfm, wte_v, bs="B")
            w_vtm = self.transposes_to(self.vtm, self.vfm, w_vfm)
            w_keta = self.transposes_to(self.keta, self.kp, w_kp, eta_hp=hp)
            p("gpsimd", lambda g, hp=hp: g.dma_start(out=self.kv_sp[hp, 0, :, :], in_=self.kp[:]), (w_kp,), [I(qg, 16)])
            p("gpsimd", lambda g, hp=hp: g.dma_start(out=self.kv_sp[hp, 1, :, :], in_=self.vtm[:]), (w_vtm,), [I(qg, 16)])
            w_mm4 = self.state_mm4((w_keta, w_vtm))
            self.state_chain(hp, w_mm4, hist=False)
        if KDEBUG.get("spb"):
            p("gpsimd", lambda g: g.dma_start(out=self.taps["spb"][:, :], in_=self.S_a[:]), (W(ve),), [I(qg, 16)])
        if KPHASES[0] < 2:
            return self._finish_early()

        p("vector", lambda v: v.tensor_scalar(self.S_a[:], self.S_a[:], self.mAb[:, 0:1], None, AL.mult),
          (self.const_done,), [I(ve)])
        w_lnfm = self.load_lnfm(self.ln1_sp, 0, (ln_pa_done, W(te)))
        w_rot = self.load_rot(0, (("cq", self.cqb), ("sq", self.sqb), ("ck", self.ckb), ("sk", self.skb)),
                              waits=(W(ve),))
        for hp in range(NHP):
            bi = hp % 2
            wq_w = self.wslice(self.wqs[bi], "wqT", hp, (W(te),) if hp >= 2 else (ln_pa_done, W(te)))
            wk_w = self.wslice(self.wks[bi], "wkT", hp)
            wv_w = self.wslice(self.wvs[bi], "wvT", hp)
            wte_q = self.proj(self.wqs[bi], (wq_w, w_lnfm if hp == 0 else None, w_rot if hp == 0 else None), bs="A")
            wte_k = self.proj(self.wks[bi], (wk_w,), bs="B")
            w_qraw = self.evac2("scalar", self.qraw, wte_q, bs="A")
            w_qp = self.rot_apply(self.qraw, self.qp, self.cqb, self.sqb, w_qraw, bs="A")
            p("vector", lambda v, hp=hp: v.tensor_tensor(
                self.xiq[:].rearrange("p (c f) -> p c f", f=128),
                self.qp[:].rearrange("p (c f) -> p c f", f=128),
                self.xib[:, None, hp * 128:(hp + 1) * 128].broadcast_to((128, 16, 128)), AL.mult),
              (w_qp,), [I(ve)])
            w_xiq = W(ve)
            w_kraw = self.evac2("scalar", self.kraw, wte_k, bs="B")
            w_kp = self.rot_apply(self.kraw, self.kp, self.ckb, self.skb, w_kraw, bs="B")
            wte_v = self.proj(self.wvs[bi], (wv_w,), bs="A")
            w_vfm = self.evac2("vector", self.vfm, wte_v, bs="A")
            w_vtm = self.transposes_to(self.vtm, self.vfm, w_vfm)
            w_keta = self.transposes_to(self.keta, self.kp, w_kp, eta_hp=hp)
            w_mm4 = self.state_mm4((w_keta, w_vtm))
            self.state_chain(hp, w_mm4, hist=True)
            w_rc = self.retention_chunks(hp, (w_xiq, w_keta), do_spill=True, stats_off=hp * 120,
                                         spill_wait=W(self.qd) if hp > 0 else None)
            p("sync", lambda s, hp=hp: s.dma_start(out=self.ret_sp[hp, :, :], in_=self.stagf[:]), (w_rc,), [I(qd, 16)])
            if KDEBUG.get("ret"):
                p("gpsimd", lambda g, hp=hp: g.dma_start(out=self.taps["ret"][hp, :, :], in_=self.stagf[:]), (w_rc,), [I(qg, 16)])
        w_ret_spill = W(qd)
        if KPHASES[0] < 3:
            return self._finish_early()

        p("vector", lambda v: v.tensor_scalar(self.S_a[:], self.S_a[:], self.mBb[:, 0:1], None, AL.mult), (), [I(ve)])
        w_lnfm = self.load_lnfm(self.ln1_sp, HF, W(te))
        w_rot = self.load_rot(HF, (("cq", self.cqb), ("sq", self.sqb)), waits=(W(ve),))
        for hp in range(NHP):
            bi = hp % 2
            wq_w = self.wslice(self.wqs[bi], "wqT", hp, (W(te),) if hp >= 2 else ())
            p("sync", lambda s, hp=hp: s.dma_start(out=self.kp[:], in_=self.kv_sp[hp, 0, :, :]),
              (W(te),), [I(qd, 16)])
            p("sync", lambda s, hp=hp: s.dma_start(out=self.vtm[:], in_=self.kv_sp[hp, 1, :, :]), (), [I(qd, 16)])
            w_kv = W(qd)
            wte = self.proj(self.wqs[bi], (wq_w, w_lnfm if hp == 0 else None, w_rot if hp == 0 else None))
            w_qraw = self.evac2("scalar", self.qraw, wte)
            w_qp = self.rot_apply(self.qraw, self.qp, self.cqb, self.sqb, w_qraw)
            p("vector", lambda v, hp=hp: v.tensor_tensor(
                self.xiq[:].rearrange("p (c f) -> p c f", f=128),
                self.qp[:].rearrange("p (c f) -> p c f", f=128),
                self.xib[:, None, hp * 128:(hp + 1) * 128].broadcast_to((128, 16, 128)), AL.mult),
              (w_qp,), [I(ve)])
            w_xiq = W(ve)
            w_keta = self.transposes_to(self.keta, self.kp, w_kv, eta_hp=hp)
            w_mm4 = self.state_mm4((w_keta, w_kv))
            self.state_chain(hp, w_mm4, hist=True)
            self.retention_chunks(hp, (w_xiq, w_keta), do_spill=False, stats_off=hp * 120 + 24)

        if KPHASES[0] < 4:
            return self._finish_early()
        self._stats_finalize()
        w_mish = self._mish(ln_pa_done)
        w_h1_all = self._wo_phase(w_mish, w_ret_spill)
        if KPHASES[0] < 5:
            return self._finish_early()
        self._ffn(w_h1_all)

        return self._emit()

    def _finish_early(self):
        return self._emit()

    def _emit(self):
        p, W = self.op, self.W
        p("sync", None, (W(self.qd), W(self.qg), W(self.te), W(self.ve), W(self.se), W(self.os)))
        blk = self.es.enter_context(self.nc.Block())

        def mk(eng_name):
            def f(eng):
                for fn, waits, incs in self.EL[eng_name]:
                    for (h, v) in waits:
                        eng.wait_ge(h, v)
                    inst = fn(eng) if fn is not None else None
                    if inst is not None:
                        for (h, n) in incs:
                            inst.then_inc(h, n)
            return f

        blk.sync(mk("sync"))
        blk.tensor(mk("tensor"))
        blk.vector(mk("vector"))
        blk.scalar(mk("scalar"))
        blk.gpsimd(mk("gpsimd"))
        return self.nc

    def _stats_finalize(self):
        p, I, W = self.op, self.I, self.W
        te, ve, se = self.te, self.ve, self.se
        st3, fin2, sm4, sm1 = self.st3, self.fin2, self.sm4, self.sm1
        scr = self.stagf
        for hp in range(NHP):
            b0 = hp * 120
            p("vector", lambda v, b0=b0: v.tensor_reduce(sm1[:, 0:1], self.stats[:, b0 + 1:b0 + 24:3], mybir.AxisListType.X, AL.add))
            p("vector", lambda v, b0=b0: v.tensor_reduce(sm1[:, 1:2], self.stats[:, b0 + 25:b0 + 120:3], mybir.AxisListType.X, AL.add))
            p("vector", lambda v, b0=b0: v.tensor_reduce(sm1[:, 2:3], self.stats[:, b0 + 2:b0 + 24:3], mybir.AxisListType.X, AL.add))
            p("vector", lambda v, b0=b0: v.tensor_reduce(sm1[:, 3:4], self.stats[:, b0 + 26:b0 + 120:3], mybir.AxisListType.X, AL.add))
            p("vector", lambda v, b0=b0: v.tensor_tensor(scr[:, 1100:1108], self.stats[:, b0 + 1:b0 + 24:3], self.stats[:, b0 + 1:b0 + 24:3], AL.mult))
            p("vector", lambda v, b0=b0: v.tensor_tensor(scr[:, 1108:1140], self.stats[:, b0 + 25:b0 + 120:3], self.stats[:, b0 + 25:b0 + 120:3], AL.mult))
            p("vector", lambda v: v.tensor_reduce(sm1[:, 4:5], scr[:, 1100:1108], mybir.AxisListType.X, AL.add))
            p("vector", lambda v: v.tensor_reduce(sm1[:, 5:6], scr[:, 1108:1140], mybir.AxisListType.X, AL.add))
            p("vector", lambda v: v.tensor_scalar(sm1[:, 1:2], sm1[:, 1:2], 64.0, None, AL.mult))
            self.vsep()
            p("vector", lambda v: v.scalar_tensor_tensor(sm1[:, 0:1], sm1[:, 0:1], 256.0, sm1[:, 1:2], AL.mult, AL.add))
            self.vsep()
            p("vector", lambda v, hp=hp: v.tensor_scalar(st3[:, hp:hp + 1], sm1[:, 0:1], 1.0 / (S * HD / 64), None, AL.mult))
            p("vector", lambda v: v.tensor_tensor(sm1[:, 2:3], sm1[:, 2:3], sm1[:, 3:4], AL.add))
            self.vsep()
            p("vector", lambda v: v.scalar_tensor_tensor(sm1[:, 2:3], sm1[:, 4:5], 256.0, sm1[:, 2:3], AL.mult, AL.add))
            self.vsep()
            p("vector", lambda v: v.scalar_tensor_tensor(sm1[:, 2:3], sm1[:, 5:6], 64.0, sm1[:, 2:3], AL.mult, AL.add))
            self.vsep()
            p("vector", lambda v: v.tensor_scalar(sm1[:, 2:3], sm1[:, 2:3], 1.0 / (S * HD / 64), None, AL.mult))
            p("vector", lambda v, hp=hp: v.tensor_tensor(scr[:, 1140:1141], st3[:, hp:hp + 1], st3[:, hp:hp + 1], AL.mult))
            self.vsep()
            p("vector", lambda v, hp=hp: v.tensor_tensor(st3[:, NHP + hp:NHP + hp + 1], sm1[:, 2:3], scr[:, 1140:1141], AL.subtract))
            p("vector", lambda v, hp=hp: v.tensor_copy(st3[:, 2 * NHP + hp:2 * NHP + hp + 1], scr[:, 1140:1141]))
        p("vector", lambda v: v.engine_nop() if False else v.tensor_copy(sm1[:, 6:7], st3[:, 0:1]), (), [I(ve)])
        p("tensor", lambda t: t.matmul(self.PF[2][0:2, 0:3 * NHP], self.sel64[:], st3[:], start=True, stop=True),
          self.mmw(("pf2a",), (W(ve), self.const_done)), [I(te)])
        p("vector", lambda v: v.tensor_copy(fin2[:, 0:3 * NHP], self.PF[2][0:2, 0:3 * NHP]), (W(te),), [I(ve)])
        self.setg(("pf2a", "pf2b"), W(ve))
        self.vsep()
        p("vector", lambda v: v.tensor_scalar(fin2[:, 0:3 * NHP], fin2[:, 0:3 * NHP], 1.0 / 64, None, AL.mult))
        self.vsep()
        p("vector", lambda v: v.tensor_tensor(sm4[:, 0:NHP], fin2[:, 0:NHP], fin2[:, 0:NHP], AL.mult))
        p("vector", lambda v: v.tensor_tensor(fin2[:, NHP:2 * NHP], fin2[:, NHP:2 * NHP], fin2[:, 2 * NHP:3 * NHP], AL.add))
        self.vsep()
        p("vector", lambda v: v.tensor_tensor(fin2[:, NHP:2 * NHP], fin2[:, NHP:2 * NHP], sm4[:, 0:NHP], AL.subtract))
        self.vsep()
        p("vector", lambda v: v.tensor_scalar(fin2[:, NHP:2 * NHP], fin2[:, NHP:2 * NHP], EPS, None, AL.add), (), [I(ve)])
        p("scalar", lambda s: s.activation(sm4[:, NHP:2 * NHP], fin2[:, NHP:2 * NHP], AF.Sqrt), (W(ve),), [I(se)])
        p("vector", lambda v: v.reciprocal(sm4[:, 0:NHP], sm4[:, NHP:2 * NHP]), (W(se),), [I(ve)])
        p("scalar", lambda s: s.activation(sm4[:, 2 * NHP:3 * NHP], sm4[:, 0:NHP], AF.Identity), (W(ve),))
        p("scalar", lambda s: s.activation(sm4[:, NHP:2 * NHP], sm4[:, 0:NHP], AF.Square), (), [I(se)])
        p("vector", lambda v: v.tensor_tensor(sm4[:, NHP:2 * NHP], sm4[:, NHP:2 * NHP], fin2[:, NHP:2 * NHP], AL.mult), (W(se),))
        self.vsep()
        p("vector", lambda v: v.tensor_scalar(sm4[:, NHP:2 * NHP], sm4[:, NHP:2 * NHP], -0.5, 1.5, AL.mult, AL.add))
        self.vsep()
        p("vector", lambda v: v.tensor_tensor(sm4[:, 0:NHP], sm4[:, 2 * NHP:3 * NHP], sm4[:, NHP:2 * NHP], AL.mult))
        self.vsep()
        p("vector", lambda v: v.tensor_tensor(fin2[:, 3 * NHP:4 * NHP], self.gnw2[:], sm4[:, 0:NHP], AL.mult))
        self.vsep()
        p("vector", lambda v: v.tensor_tensor(sm4[:, NHP:2 * NHP], fin2[:, 0:NHP], fin2[:, 3 * NHP:4 * NHP], AL.mult))
        self.vsep()
        p("vector", lambda v: v.tensor_tensor(fin2[:, 0:NHP], self.gnb2[:], sm4[:, NHP:2 * NHP], AL.subtract), (), [I(ve)])
        p("tensor", lambda t: t.matmul(self.PF[2][:, 0:NHP], self.rep2[:], fin2[0:2, 3 * NHP:4 * NHP], start=True, stop=True),
          self.mmw(("pf2a",), (W(ve),)))
        p("tensor", lambda t: t.matmul(self.PF[2][:, 512:512 + NHP], self.rep2[:], fin2[0:2, 0:NHP], start=True, stop=True),
          self.mmw(("pf2b",)), [I(te)])
        p("vector", lambda v: v.tensor_copy(self.A_vec[:], self.PF[2][:, 0:NHP]), (W(te),))
        p("vector", lambda v: v.tensor_copy(self.C_vec[:], self.PF[2][:, 512:512 + NHP]), (), [I(ve)])
        self.setg(("pf2a", "pf2b"), W(ve))
        if KDEBUG.get("gn"):
            p("gpsimd", lambda g: g.dma_start(out=self.taps["gn"][:, 0:NHP], in_=self.A_vec[:]), (W(ve),), [I(self.qg, 16)])
            p("gpsimd", lambda g: g.dma_start(out=self.taps["gn"][:, NHP:2 * NHP], in_=self.C_vec[:]), (), [I(self.qg, 16)])

    def _mish(self, ln_pa_done):
        p, I, W = self.op, self.I, self.W
        te, ve, se, qg = self.te, self.ve, self.se, self.qg
        w_lnfm = self.load_lnfm(self.ln1_sp, 0, W(self.te))
        keys3 = (("pf0a", "pf0b"), ("pf1a", "pf1b"), ("pf2a", "pf2b"))
        for fo in range(16):
            bi = fo % 2
            wg_w = self.wslice(self.wgen[bi], "wgT", fo, (W(te),))
            for half in range(2):
                mi = (fo * 2 + half) % 2
                k3 = keys3[(fo * 2 + half) % 3]
                pf = self.PF[(fo * 2 + half) % 3]
                ws = self.mmw(k3, (wg_w, w_lnfm if fo == 0 else None))
                for fi in range(16):
                    for sp in range(2):
                        p("tensor", lambda t, pf=pf, w=self.wgen[bi], fi=fi, sp=sp, half=half: t.matmul(
                            pf[:, sp * 512:(sp + 1) * 512], w[:, fi * 128:(fi + 1) * 128],
                            self.BIGA[:, fi * HF + half * 1024 + sp * 512: fi * HF + half * 1024 + sp * 512 + 512],
                            start=(fi == 0), stop=(fi == 15)),
                          ws if (fi == 0 and sp == 0) else (),
                          [I(te)] if (fi == 15 and sp == 1) else ())
                wte = W(te)
                p("scalar", lambda s, pf=pf: s.activation(self.stagf[:, 0:1024], pf[:, :], AF.Exp), (wte,))
                p("scalar", lambda s: s.activation(self.stagf[:, 0:1024], self.stagf[:, 0:1024], AF.Ln, bias=1.0))
                p("scalar", lambda s: s.activation(self.stagf[:, 0:1024], self.stagf[:, 0:1024], AF.Tanh), (), [I(se)])
                p("vector", lambda v, pf=pf, mi=mi: v.tensor_tensor(
                    self.msh[mi][:], pf[:, :], self.stagf[:, 0:1024], AL.mult),
                  (W(se), W(qg, self.qg.v)), [I(ve)])
                self.setg(k3, W(ve))
                p("gpsimd", lambda g, fo=fo, half=half, mi=mi: g.dma_start(
                    out=self.mish_sp[fo, :, half * 1024:(half + 1) * 1024], in_=self.msh[mi][:]),
                  (W(ve),), [I(qg, 16)])
        return W(qg)

    def _wo_phase(self, w_mish, w_ret_spill):
        p, I, W = self.op, self.I, self.W
        qd, qg, te, ve = self.qd, self.qg, self.te, self.ve
        w_free = W(te)
        for cc in range(16):
            p("gpsimd", lambda g, c=cc: g.dma_start(
                out=self.BIGA[:, c * HF:(c + 1) * HF], in_=self.D["woT"][c * 128:(c + 1) * 128, :]),
              (w_free,) if cc == 0 else (), [I(qg, 16)])
        w_wo = W(qg)
        for tt in range(16):
            bi = tt % 2
            p("sync", lambda s, tt=tt, b=bi: s.dma_start(
                out=self.strip[b][:].rearrange("p (h f) -> p h f", f=128),
                in_=self.ret_sp[:, :, tt * 128:(tt + 1) * 128].rearrange("h p f -> p h f")),
              (W(ve),) if tt >= 2 else (w_ret_spill, W(te), W(ve)), [I(qd, 16)])
            p("sync", lambda s, tt=tt, b=bi: s.dma_start(
                out=self.mstrip[b][:].rearrange("p (h f) -> p h f", f=128),
                in_=self.mish_sp[:, :, tt * 128:(tt + 1) * 128].rearrange("h p f -> p h f")),
              (w_mish, W(te)) if tt <= 1 else (), [I(qd, 16)])
            p("sync", lambda s, tt=tt, b=bi: s.dma_start(out=self.xtf[b][:], in_=self.D["x"][tt * 128:(tt + 1) * 128, :]),
              (W(te), W(ve)) if tt <= 1 else (), [I(qd, 16)])
            w_strip = W(qd)
            for hp in range(NHP):
                p("vector", lambda v, b=bi, hp=hp: v.tensor_scalar(
                    self.stagf[:, hp * 128:(hp + 1) * 128],
                    self.strip[b][:, hp * 128:(hp + 1) * 128],
                    self.A_vec[:, hp:hp + 1], self.C_vec[:, hp:hp + 1], AL.mult, AL.add),
                  (w_strip,) if hp == 0 else ())
            p("vector", lambda v, b=bi: v.tensor_tensor(
                self.qp[:], self.stagf[:, 0:2048], self.mstrip[b][:], AL.mult), (), [I(ve)])
            w_z = W(ve)
            bs = "A" if tt % 2 == 0 else "B"
            ws = self.mmw(self.bs_keys(bs), (w_z, w_wo if tt == 0 else None))
            for fi in range(16):
                for sp in range(4):
                    tgt, _ = self.pf_half(sp, bs)
                    p("tensor", lambda t, tg=tgt, fi=fi, sp=sp: t.matmul(
                        tg, self.qp[:, fi * 128:(fi + 1) * 128],
                        self.BIGA[:, fi * HF + sp * 512: fi * HF + sp * 512 + 512],
                        start=(fi == 0), stop=(fi == 15)),
                      ws if (fi == 0 and sp == 0) else (),
                      [I(te)] if (fi == 15 and sp == 3) else ())
            wte = W(te)
            for h in range(2):
                pbuf = self.bs_buf(h, bs)
                p("vector", lambda v, b=bi, h=h, pb=pbuf: v.tensor_tensor(
                    self.h1t[b][:, h * 1024:(h + 1) * 1024], pb[:, 0:1024],
                    self.xtf[b][:, h * 1024:(h + 1) * 1024], AL.add),
                  (wte,) if h == 0 else (), [I(ve)] if h == 1 else [])
            self.setg(self.bs_keys(bs), W(ve))
            w_apply = self.emit_ln(self.h1t[bi], self.ln2t[bi], W(ve), W(qg) if tt >= 2 else None)
            p("gpsimd", lambda g, b=bi, tt=tt: g.dma_start(out=self.h1_sp[tt * 128:(tt + 1) * 128, :], in_=self.h1t[b][:]),
              (w_apply,), [I(qg, 16)])
            p("gpsimd", lambda g, b=bi, tt=tt: g.dma_start(out=self.ln2_sp[tt * 128:(tt + 1) * 128, :], in_=self.ln2t[b][:]),
              (), [I(qg, 16)])
            if KDEBUG.get("h1"):
                p("gpsimd", lambda g, b=bi, tt=tt: g.dma_start(out=self.taps["h1"][tt * 128:(tt + 1) * 128, :], in_=self.h1t[b][:]),
                  (), [I(qg, 16)])
        return W(qg)

    def _ffn(self, w_h1_all):
        p, I, W = self.op, self.I, self.W
        qd, qg, te, ve, se = self.qd, self.qg, self.te, self.ve, self.se
        D = self.D
        w_ln2 = self.load_lnfm(self.ln2_sp, 0, w_h1_all)
        p("gpsimd", lambda g: g.dma_start(
            out=self.gwsb[:].rearrange("p (c f) -> p c f", f=32),
            in_=D["gwT"].rearrange("(c p) f -> p c f", p=128)), (W(te), W(self.ve)), [I(qg, 16)])
        p("gpsimd", lambda g: g.dma_start(
            out=self.rdT_sb[:].rearrange("p (c f) -> p c f", f=128),
            in_=D["RdT"].rearrange("(c p) f -> p c f", p=128)), (), [I(qg, 16)])
        p("gpsimd", lambda g: g.dma_start(out=self.sCg_sb[:], in_=D["sCgT"][:, :]), (), [I(qg, 16)])
        p("gpsimd", lambda g: g.dma_start(out=self.sCu_sb[:], in_=D["sCuT"][:, :]), (), [I(qg, 16)])
        p("gpsimd", lambda g: g.dma_start(
            out=self.sRd_sb[:].rearrange("p (c f) -> p c f", f=128),
            in_=D["sRdT"].rearrange("(c p) f -> p c f", p=128)), (), [I(qg, 16)])
        p("gpsimd", lambda g: g.dma_start(out=self.sU_sb[:, 0:128], in_=D["sUgT"][:, :]), (), [I(qg, 16)])
        p("gpsimd", lambda g: g.dma_start(out=self.sU_sb[:, 128:256], in_=D["sUuT"][:, :]), (), [I(qg, 16)])
        p("gpsimd", lambda g: g.dma_start(out=self.sU_sb[:, 256:384], in_=D["sUdT"][:, :]), (), [I(qg, 16)])
        p("gpsimd", lambda g: g.dma_start(out=self.sCd_sb[:], in_=D["sCdT"][:, :]), (), [I(qg, 16)])
        p("gpsimd", lambda g: g.dma_start(out=self.sel_e[:], in_=D["sel_e"][:, :]), (), [I(qg, 16)])
        w_small = W(qg)

        self._plx = 0

        def projL(dst, wname, extra=()):
            bi2 = self._plx % 2
            bs = "A" if bi2 == 0 else "B"
            self._plx += 1
            wl = self.wslice(self.wgen[bi2], wname, 0, (W(te),))
            wte = self.proj(self.wgen[bi2], (wl,) + tuple(extra), bs=bs)
            return self.evac2("vector", dst, wte, bs=bs)

        w_rg = projL(self.rg_sb, "RgT", (w_ln2,))
        w_ru = projL(self.ru_sb, "RuT")
        w_srg = projL(self.srg_sb, "sRgT")
        w_sru = projL(self.sru_sb, "sRuT")
        if KDEBUG.get("rg"):
            p("gpsimd", lambda g: g.dma_start(out=self.taps["rg"][:, :], in_=self.rg_sb[:]), (w_rg,), [I(qg, 16)])

        for pas in range(2):
            ws = self.mmw(("pf2a", "pf2b"), (w_small,) if pas == 0 else ())
            for fi in range(16):
                for sp in range(2):
                    p("tensor", lambda t, fi=fi, sp=sp, pas=pas: t.matmul(
                        self.PF[2][0:32, sp * 512:sp * 512 + 512],
                        self.gwsb[:, fi * 32:(fi + 1) * 32],
                        self.BIGA[:, fi * HF + pas * 1024 + sp * 512: fi * HF + pas * 1024 + sp * 512 + 512],
                        start=(fi == 0), stop=(fi == 15)),
                      ws if (fi == 0 and sp == 0) else (),
                      [I(te)] if (fi == 15 and sp == 1) else ())
            p("vector", lambda v, pas=pas: v.tensor_copy(self.lgf[:, pas * 1024:(pas + 1) * 1024], self.PF[2][0:32, :]),
              (W(te),), [I(ve)])
            self.setg(("pf2a", "pf2b"), W(ve))
        w_lg = W(ve)

        ws = self.mmw(("pf2a", "pf2b"), (w_lg,))
        for tl in range(16):
            p("tensor", lambda t, tl=tl: t.transpose(
                self.PF[2][:, tl * 32:(tl + 1) * 32],
                self.lgf[:, tl * 128:(tl + 1) * 128], self.idf[0:32, 0:32]),
              ws if tl == 0 else (), [I(te)] if tl == 15 else ())
        p("vector", lambda v: v.tensor_copy(self.lgtm[:], self.PF[2][:, 0:512]), (W(te),), [I(ve)])
        self.setg(("pf2a", "pf2b"), W(ve))

        sm1 = self.sm1
        for tl in range(16):
            lg = self.lgtm[:, tl * 32:(tl + 1) * 32]
            wetile = self.wetm[:, tl * 32:(tl + 1) * 32]
            msk = self.stagf[:, 64:96]
            p("vector", lambda v, lg=lg: v.tensor_reduce(sm1[:, 0:1], lg, mybir.AxisListType.X, AL.max))
            p("vector", lambda v, lg=lg, we=wetile: v.tensor_copy(we, lg))
            for rnd in range(3):
                mcol = sm1[:, 0:1] if rnd == 0 else sm1[:, 1:2]
                self.vsep()
                p("vector", lambda v, we=wetile, mc=mcol, mk=msk: v.tensor_scalar(mk, we, mc[:, 0:1], None, AL.is_ge))
                self.vsep()
                p("vector", lambda v, we=wetile, mk=msk: v.scalar_tensor_tensor(we, mk, -1e30, we, AL.mult, AL.add))
                self.vsep()
                p("vector", lambda v, we=wetile: v.tensor_reduce(sm1[:, 1:2], we, mybir.AxisListType.X, AL.max))
            self.vsep()
            p("vector", lambda v: v.tensor_scalar(sm1[:, 2:3], sm1[:, 0:1], -1.0, None, AL.mult))
            self.vsep()
            p("vector", lambda v, lg=lg: v.tensor_scalar(self.stagf[:, 0:32], lg, sm1[:, 2:3], None, AL.add), (), [I(ve)])
            p("scalar", lambda s: s.activation(self.stagf[:, 32:64], self.stagf[:, 0:32], AF.Exp), (W(ve),), [I(se)])
            p("vector", lambda v: v.tensor_reduce(sm1[:, 3:4], self.stagf[:, 32:64], mybir.AxisListType.X, AL.add), (W(se),))
            p("vector", lambda v, lg=lg, mk=msk: v.tensor_scalar(mk, lg, sm1[:, 1:2], None, AL.is_ge))
            self.vsep()
            p("vector", lambda v, mk=msk: v.tensor_tensor(self.stagf[:, 32:64], self.stagf[:, 32:64], mk, AL.mult))
            self.vsep()
            p("vector", lambda v: v.tensor_reduce(sm1[:, 4:5], self.stagf[:, 32:64], mybir.AxisListType.X, AL.add))
            self.vsep()
            p("vector", lambda v: v.scalar_tensor_tensor(sm1[:, 4:5], sm1[:, 3:4], 1e-20, sm1[:, 4:5], AL.mult, AL.add))
            p("vector", lambda v: v.reciprocal(sm1[:, 5:6], sm1[:, 4:5]), (), [I(ve)])
            p("scalar", lambda s: s.activation(sm1[:, 6:7], sm1[:, 5:6], AF.Identity), (W(ve),), [I(se)])
            p("vector", lambda v, we=wetile: v.tensor_scalar(we, self.stagf[:, 32:64], sm1[:, 6:7], None, AL.mult), (W(se),), [I(ve)])
        w_we = W(ve)
        if KDEBUG.get("we"):
            p("gpsimd", lambda g: g.dma_start(out=self.taps["we"][:, :], in_=self.wetm[:]), (w_we,), [I(qg, 16)])

        for grp in range(4):
            ws = self.mmw(("pf2a", "pf2b"), (w_we,) if grp == 0 else ())
            for q in range(4):
                tl = grp * 4 + q
                p("tensor", lambda t, tl=tl, q=q: t.transpose(
                    self.PF[2][0:32, q * 128:(q + 1) * 128],
                    self.wetm[:, tl * 32:(tl + 1) * 32], self.idf[:]),
                  ws if q == 0 else (), [I(te)] if q == 3 else ())
            p("vector", lambda v, grp=grp: v.tensor_copy(self.weT[:, grp * 512:(grp + 1) * 512], self.PF[2][0:32, 0:512]),
              (W(te),), [I(ve)])
            self.setg(("pf2a", "pf2b"), W(ve))
        w_weT = W(ve)

        def k128(dst, lhsT, rhsbuf, waits):
            ws = self.mmw(self.PK4, waits)
            for sp in range(4):
                tgt, _ = self.pf_half(sp)
                p("tensor", lambda t, tg=tgt, l=lhsT, r=rhsbuf, sp=sp: t.matmul(
                    tg, l, r[:, sp * 512:(sp + 1) * 512], start=True, stop=True),
                  ws if sp == 0 else (), [I(te)] if sp == 3 else ())
            return self.evac2("vector", dst, W(te))

        w_hsg = k128(self.hsg, self.sU_sb[:, 0:128], self.srg_sb, (w_srg, w_small))
        w_hsu = k128(self.hsu, self.sU_sb[:, 128:256], self.sru_sb, (w_sru,))

        def smblk(ft):
            return (self.BIGA[:, ft * HF:(ft + 1) * HF] if ft < 16
                    else self.smX[:, (ft - 16) * HF:(ft - 15) * HF])

        for ft in range(NSI):
            ws = self.mmw(self.PK4, (w_hsg, w_lg) if ft == 0 else ())
            for sp in range(4):
                tgt, _ = self.pf_half(sp, "A")
                p("tensor", lambda t, tg=tgt, ft=ft, sp=sp: t.matmul(
                    tg, self.sCg_sb[:, ft * 128:(ft + 1) * 128],
                    self.hsg[:, sp * 512:(sp + 1) * 512], start=True, stop=True),
                  ws if sp == 0 else (), [I(te)] if sp == 3 else ())
            wte = W(te)
            ws = self.mmw(self.bs_keys("B"), (w_hsu,) if ft == 0 else ())
            for sp in range(4):
                tgt, _ = self.pf_half(sp, "B")
                p("tensor", lambda t, tg=tgt, ft=ft, sp=sp: t.matmul(
                    tg, self.sCu_sb[:, ft * 128:(ft + 1) * 128],
                    self.hsu[:, sp * 512:(sp + 1) * 512], start=True, stop=True),
                  ws if sp == 0 else (), [I(te)] if sp == 3 else ())
            wte_u = W(te)
            for h in range(2):
                p("scalar", lambda s, h=h: s.activation(self.sgb[:, h * 1024:(h + 1) * 1024], self.PF[h][:, :], AF.Silu),
                  (wte,) if h == 0 else (), [I(se)] if h == 1 else [])
            self.setg(self.PK4, W(se))
            w_sg = W(se)
            for h in range(2):
                pbuf = self.bs_buf(h, "B")
                p("vector", lambda v, ft=ft, h=h, pb=pbuf: v.tensor_tensor(
                    smblk(ft)[:, h * 1024:(h + 1) * 1024], pb[:, 0:1024],
                    self.sgb[:, h * 1024:(h + 1) * 1024], AL.mult),
                  (wte_u, w_sg) if h == 0 else (), [I(ve)] if h == 1 else [])
            self.setg(self.bs_keys("B"), W(ve))
        ws = self.mmw(self.PK4, (W(ve),))
        for ki in range(NSI):
            for sp in range(4):
                tgt, _ = self.pf_half(sp)
                p("tensor", lambda t, tg=tgt, ki=ki, sp=sp: t.matmul(
                    tg, self.sRd_sb[:, ki * 128:(ki + 1) * 128],
                    smblk(ki)[:, sp * 512:(sp + 1) * 512],
                    start=(ki == 0), stop=(ki == NSI - 1)),
                  ws if (ki == 0 and sp == 0) else (),
                  [I(te)] if (ki == NSI - 1 and sp == 3) else ())
        w_srd = self.evac2("vector", self.sru_sb, W(te))
        w_srdu = k128(self.srdu_sb, self.sU_sb[:, 256:384], self.sru_sb, (w_srd,))
        if KDEBUG.get("srdu"):
            p("gpsimd", lambda g: g.dma_start(out=self.taps["srdu"][:, :], in_=self.srdu_sb[:]), (w_srdu,), [I(qg, 16)])

        for sp_i in range(NSPAN):
            self._moe_span(sp_i, w_weT, w_rg, w_ru)
            for sub in range(SPAN // 512):
                self._out_span(sp_i, sub)

    def _moe_span(self, sp_i, w_weT, w_rg, w_ru):
        p, I, W = self.op, self.I, self.W
        qg, te, ve, se = self.qg, self.te, self.ve, self.se
        D = self.D
        t0 = sp_i * SPAN
        NQ = SPAN // 512
        for e in range(NE):
            bi = e % 2
            ew = self.eW[bi]
            wld = (W(te), W(ve))
            p("gpsimd", lambda g, ew=ew, e=e: g.dma_start(out=ew["ug"][:], in_=D["UgT"][e, :, :]), wld, [I(qg, 16)])
            p("gpsimd", lambda g, ew=ew, e=e: g.dma_start(out=ew["cg"][:], in_=D["CgT"][e, :, :]), (), [I(qg, 16)])
            p("gpsimd", lambda g, ew=ew, e=e: g.dma_start(out=ew["uu"][:], in_=D["UuT"][e, :, :]), (), [I(qg, 16)])
            p("gpsimd", lambda g, ew=ew, e=e: g.dma_start(out=ew["cu"][:], in_=D["CuT"][e, :, :]), (), [I(qg, 16)])
            p("gpsimd", lambda g, ew=ew, e=e: g.dma_start(out=ew["ud"][:], in_=D["UdT"][e, :, :]), (), [I(qg, 16)])
            w_ew = W(qg)
            ws2 = self.mmw(("pf2a", "pf2b"), (w_weT,) if (e == 0 and sp_i == 0) else ())
            for q in range(NQ):
                p("tensor", lambda t, e=e, q=q: t.matmul(
                    self.PF[2][:, q * 512:q * 512 + 512], self.sel_e[:, e * 128:(e + 1) * 128],
                    self.weT[:, t0 + q * 512:t0 + q * 512 + 512], start=True, stop=True),
                  ws2 if q == 0 else (), [I(te)] if q == NQ - 1 else ())
            w_wbm = W(te)
            ws0 = self.mmw(("pf0a", "pf0b"), (w_ew, w_rg) if (e == 0 and sp_i == 0) else (w_ew,))
            for q in range(NQ):
                p("tensor", lambda t, ew=ew, q=q: t.matmul(
                    self.PF[0][:, q * 512:q * 512 + 512], ew["ug"][:],
                    self.rg_sb[:, t0 + q * 512:t0 + q * 512 + 512], start=True, stop=True),
                  ws0 if q == 0 else (), [I(te)] if q == NQ - 1 else ())
            w_a1g = W(te)
            ws1 = self.mmw(("pf1a", "pf1b"), (w_ru,) if (e == 0 and sp_i == 0) else ())
            for q in range(NQ):
                p("tensor", lambda t, ew=ew, q=q: t.matmul(
                    self.PF[1][:, q * 512:q * 512 + 512], ew["uu"][:],
                    self.ru_sb[:, t0 + q * 512:t0 + q * 512 + 512], start=True, stop=True),
                  ws1 if q == 0 else (), [I(te)] if q == NQ - 1 else ())
            w_a1u = W(te)
            p("scalar", lambda s: s.copy(self.wb_sb[:], self.PF[2][:, 0:SPAN]), (w_wbm,))
            p("scalar", lambda s: s.copy(self.hg_sb[:], self.PF[0][:, 0:SPAN]), (w_a1g,), [I(se)])
            self.setg(("pf2a", "pf2b", "pf0a", "pf0b"), W(se))
            w_hg = W(se)
            p("vector", lambda v: v.tensor_tensor(self.hu_sb[:], self.PF[1][:, 0:SPAN], self.wb_sb[:], AL.mult),
              (w_a1u, w_hg), [I(ve)])
            self.setg(("pf1a", "pf1b"), W(ve))
            w_hu = W(ve)
            w_sil = {}
            for ft in range(NI + 1):
                if ft < NI:
                    ws = self.mmw(("pf0a", "pf0b"), (w_hg,) if ft == 0 else ())
                    for q in range(NQ):
                        p("tensor", lambda t, ew=ew, ft=ft, q=q: t.matmul(
                            self.PF[0][:, q * 512:(q + 1) * 512], ew["cg"][:, ft * 128:(ft + 1) * 128],
                            self.hg_sb[:, q * 512:(q + 1) * 512], start=True, stop=True),
                          ws if q == 0 else (), [I(te)] if q == NQ - 1 else ())
                    wcg = W(te)
                    p("scalar", lambda s, ft=ft: s.activation(self.gblk2[ft % 2][:], self.PF[0][:, 0:SPAN], AF.Silu),
                      (wcg, W(ve)), [I(se)])
                    self.setg(("pf0a", "pf0b"), W(se))
                    w_sil[ft] = W(se)
                if ft >= 1:
                    fu = ft - 1
                    ws = self.mmw(("pf1a", "pf1b"), (w_hu,) if fu == 0 else ())
                    for q in range(NQ):
                        p("tensor", lambda t, ew=ew, fu=fu, q=q: t.matmul(
                            self.PF[1][:, q * 512:(q + 1) * 512], ew["cu"][:, fu * 128:(fu + 1) * 128],
                            self.hu_sb[:, q * 512:(q + 1) * 512], start=True, stop=True),
                          ws if q == 0 else (), [I(te)] if q == NQ - 1 else ())
                    wcu = W(te)
                    p("vector", lambda v, fu=fu: v.tensor_tensor(self.mblk[:], self.PF[1][:, 0:SPAN], self.gblk2[fu % 2][:], AL.mult),
                      (wcu, w_sil[fu], W(te)), [I(ve)])
                    self.setg(("pf1a", "pf1b"), W(ve))
                    w_m = W(ve)
                    ws3 = self.mmw(("pf2a", "pf2b"), (w_m,)) if fu == 0 else (w_m,)
                    for q in range(NQ):
                        p("tensor", lambda t, fu=fu, q=q: t.matmul(
                            self.PF[2][:, q * 512:(q + 1) * 512], self.rdT_sb[:, fu * 128:(fu + 1) * 128],
                            self.mblk[:, q * 512:(q + 1) * 512],
                            start=(fu == 0), stop=(fu == NI - 1)),
                          ws3 if q == 0 else (), [I(te)] if (fu == NI - 1 and q == NQ - 1) else ())
                if ft < NI and ft >= 1:
                    pass
            p("scalar", lambda s: s.copy(self.rd_sb[:], self.PF[2][:, 0:SPAN]), (W(te),), [I(se)])
            self.setg(("pf2a", "pf2b"), W(se))
            w_rd = W(se)
            ws = self.mmw(("pf0a", "pf0b"), (w_rd,))
            for q in range(NQ):
                p("tensor", lambda t, ew=ew, q=q: t.matmul(
                    self.PF[0][:, q * 512:(q + 1) * 512], ew["ud"][:],
                    self.rd_sb[:, q * 512:(q + 1) * 512], start=True, stop=True),
                  ws if q == 0 else (), [I(te)] if q == NQ - 1 else ())
            p("vector", lambda v, e=e: v.tensor_copy(self.BIGA[:, e * SPAN:(e + 1) * SPAN], self.PF[0][:, 0:SPAN]),
              (W(te),), [I(ve)])
            self.setg(("pf0a", "pf0b"), W(ve))
        if KDEBUG.get("rdu") and sp_i == 0:
            p("gpsimd", lambda g: g.dma_start(out=self.taps["rdu"][:, :], in_=self.BIGA[:, 0:NE * SPAN]), (W(ve),), [I(qg, 16)])

    def _out_span(self, sp_i, sub):
        p, I, W = self.op, self.I, self.W
        qd, qg, te, ve, os_ = self.qd, self.qg, self.te, self.ve, self.os
        D = self.D
        w_rdu = W(self.ve)
        w_oasm_free = W(os_)
        soff = sub * 512
        for fo in range(16):
            bi = fo % 2
            p("gpsimd", lambda g, fo=fo, b=bi: g.dma_start(
                out=self.cdW[b][:].rearrange("p (e f) -> p e f", f=128),
                in_=D["CdT"][fo].rearrange("e p f -> p e f")),
              (W(te), W(ve)), [I(qg, 16)])
            w_cd = W(qg)
            pk = ("pf0a", "pf0b", "pf1a", "pf1b")[fo % 4]
            pf = self.PF[(fo % 4) // 2]
            reg = pf[:, (fo % 2) * 512:(fo % 2) * 512 + 512]
            ws = self.mmw((pk,), (w_cd, w_rdu) if fo == 0 else (w_cd,))
            for e in range(NE + 1):
                if e < NE:
                    lhsT = self.cdW[bi][:, e * 128:(e + 1) * 128]
                    rhs = self.BIGA[:, e * SPAN + soff: e * SPAN + soff + 512]
                else:
                    lhsT = self.sCd_sb[:, fo * 128:(fo + 1) * 128]
                    rhs = self.srdu_sb[:, sp_i * SPAN + soff: sp_i * SPAN + soff + 512]
                p("tensor", lambda t, l=lhsT, r=rhs, e=e, reg=reg: t.matmul(
                    reg, l, r, start=(e == 0), stop=(e == NE)),
                  ws if e == 0 else (), [I(te)] if e == NE else ())
            wte = W(te)
            st = self.stagf[:, (fo % 2) * 512:(fo % 2) * 512 + 512]
            p("vector", lambda v, st=st, reg=reg: v.tensor_copy(st, reg), (wte,), [I(ve)])
            self.setg((pk,), W(ve))
            w_st = W(ve)
            tk = "pf2a" if fo % 2 == 0 else "pf2b"
            treg = self.PF[2][:, 0:512] if fo % 2 == 0 else self.PF[2][:, 512:1024]
            ws2 = self.mmw((tk,), (w_st,))
            for q in range(4):
                p("tensor", lambda t, st=st, q=q, treg=treg: t.transpose(
                    treg[:, q * 128:(q + 1) * 128], st[:, q * 128:(q + 1) * 128], self.idf[:]),
                  ws2 if q == 0 else (), [I(te)] if q == 3 else ())
            wtt = W(te)
            for q in range(4):
                p("vector", lambda v, q=q, fo=fo, treg=treg: v.tensor_copy(
                    self.OASM[:, q * HID + fo * 128: q * HID + (fo + 1) * 128],
                    treg[:, q * 128:(q + 1) * 128]),
                  (wtt, w_oasm_free) if (q == 0 and fo == 0) else ((wtt,) if q == 0 else ()),
                  [I(ve)] if q == 3 else [])
            self.setg((tk,), W(ve))
        w_asm = W(ve)
        for q in range(4):
            row = sp_i * SPAN + soff + q * 128
            p("sync", lambda s, row=row: s.dma_start(out=self.h1t[0][:], in_=self.h1_sp[row:row + 128, :]),
              (W(ve),), [I(qd, 16)])
            p("vector", lambda v, q=q: v.tensor_tensor(
                self.OASM[:, q * HID:(q + 1) * HID], self.OASM[:, q * HID:(q + 1) * HID],
                self.h1t[0][:], AL.add), (W(qd), w_asm), [I(ve)])
            p("sync", lambda s, row=row, q=q: s.dma_start(out=self.out[row:row + 128, :],
                                                          in_=self.OASM[:, q * HID:(q + 1) * HID]),
              (W(ve),), [I(os_, 16)])


def _build_cached():
    key = "nc"
    if key not in _CACHE:
        b = B()
        nc = b.build()
        _CACHE[key] = (nc, b.taps, b)
    return _CACHE[key]


def kernel(**inputs):
    nc, taps, _b = _build_cached()
    shared = _prep_shared(inputs)
    rot = _rot_tables()
    in_maps = [_prep_core(inputs, shared, rot, c) for c in range(8)]
    res = run_bass_kernel_spmd(nc, in_maps, core_ids=list(range(8)))
    out = np.empty((4, S, HID), np.float32)
    for c in range(8):
        bb, own = c // 2, c % 2
        out[bb, own * HF:(own + 1) * HF] = res.results[c]["out"]
    if KDEBUG:
        kernel.last_taps = [{k: res.results[c]["tap_" + k] for k in taps} for c in range(8)]
    return out


# version 05
# speedup vs baseline: 1.3068x; 1.0779x over previous
import contextlib
import numpy as np
import concourse.bass as bass
import concourse.mybir as mybir
from concourse.bass_utils import run_bass_kernel_spmd
from concourse.alu_op_type import AluOpType as AL

F32 = mybir.dt.float32
BF16 = mybir.dt.bfloat16
AF = mybir.ActivationFunctionType

HID = 2048
HD = 64
NH = 32
NHP = 16
ROT = 32
S = 4096
HF = 2048
C = 128
NCH = HF // C
EPS = 1e-5
NE = 32
IFF = 1408
NI = IFF // 128
SI = 2816
NSI = SI // 128
SPAN = 1024
NSPAN = HF // SPAN
THETA = 10000.0
SCALE_BASE = 512.0

KDEBUG = {}
KPHASES = [99]
KSUB = [99]
_CACHE = {}



def _gamma():
    return 1.0 - 2.0 ** (-5.0 - np.arange(NH, dtype=np.float64))


def _rot_tables():
    t = np.arange(S, dtype=np.float64)
    r = np.arange(0, ROT, 2, dtype=np.float64)
    inv_freq = 1.0 / THETA ** (r / ROT)
    freqs = np.repeat(t[:, None] * inv_freq[None, :], 2, axis=-1)
    base_scale = (r + 0.4 * ROT) / (1.4 * ROT)
    power = (t - S // 2) / SCALE_BASE
    scale = np.concatenate([base_scale[None, :] ** power[:, None]] * 2, axis=-1)
    cos = np.cos(freqs); sin = np.sin(freqs)
    sgn = np.where(np.arange(ROT) % 2 == 0, -1.0, 1.0)
    Cq = np.ones((HD, S)); Sq = np.zeros((HD, S))
    Ck = np.ones((HD, S)); Sk = np.zeros((HD, S))
    Cq[:ROT] = (cos * scale).T
    Ck[:ROT] = (cos / scale).T
    Sq[:ROT] = (sin * scale).T * sgn[:, None]
    Sk[:ROT] = (sin / scale).T * sgn[:, None]
    return [np.concatenate([a, a], 0).astype(np.float32) for a in (Cq, Sq, Ck, Sk)]


def _const_tables():
    g = _gamma()
    j = np.arange(C)
    d = j[None, :] - j[:, None]
    p128 = np.arange(128)
    dtb = np.zeros((128, NHP * 256), np.float32)
    for hp in range(NHP):
        for w in range(2):
            D = np.where(d >= 0, np.power(g[2 * hp + w], np.maximum(d, 0)), 0.0)
            dtb[:, hp * 256 + w * 128:hp * 256 + (w + 1) * 128] = D
    xib = np.zeros((128, NHP * 128), np.float32)
    etb = np.zeros((128, NHP * 128), np.float32)
    g128 = np.zeros((128, NHP), np.float32)
    for hp in range(NHP):
        gp = np.where(p128 >= 64, g[2 * hp + 1], g[2 * hp])
        xib[:, hp * 128:(hp + 1) * 128] = np.power(gp[:, None], (j + 1)[None, :])
        etb[:, hp * 128:(hp + 1) * 128] = np.power(gp[None, :], (C - 1 - j)[:, None])
        g128[:, hp] = np.power(gp, C)
    M = np.zeros((HD, HD), np.float32)
    for i in range(ROT // 2):
        M[2 * i, 2 * i + 1] = 1.0
        M[2 * i + 1, 2 * i] = 1.0
    bdp = np.zeros((128, 128), np.float32)
    bdp[:HD, :HD] = M; bdp[HD:, HD:] = M
    sel_e = np.zeros((32, NE * 128), np.float32)
    for e in range(NE):
        sel_e[e, e * 128:(e + 1) * 128] = 1.0
    sel64 = np.zeros((128, 2), np.float32)
    sel64[:64, 0] = 1.0; sel64[64:, 1] = 1.0
    rep2 = np.zeros((2, 128), np.float32)
    rep2[0, :64] = 1.0; rep2[1, 64:] = 1.0
    return dict(dtb=dtb, xib=xib, etb=etb, g128=g128, bdp=bdp,
                iden=np.eye(128, dtype=np.float32), sel_e=sel_e,
                sel64=sel64, rep2=rep2)


def _prep_shared(inp):
    f = np.float32
    ct = np.ascontiguousarray
    wqkv = np.asarray(inp["L0_attn_wqkv"], f)
    w3 = wqkv.reshape(NH, 3, HD, HID)
    d = {}
    d["wqT"] = ct(w3[:, 0].reshape(NH * HD, HID).T)
    d["wkT"] = ct(w3[:, 1].reshape(NH * HD, HID).T)
    d["wvT"] = ct(w3[:, 2].reshape(NH * HD, HID).T)
    d["wgT"] = ct(np.asarray(inp["L0_attn_wg"], f).T)
    d["woT"] = ct(np.asarray(inp["L0_attn_wo"], f).T)
    d["gwT"] = ct(np.asarray(inp["L0_ffn_gate_w"], f).T)
    d["RgT"] = ct(np.asarray(inp["L0_ffn_Rg"], f).T)
    d["RuT"] = ct(np.asarray(inp["L0_ffn_Ru"], f).T)
    d["RdT"] = ct(np.asarray(inp["L0_ffn_Rd"], f).T)
    for nm in ("Ug", "Cg", "Uu", "Cu", "Ud"):
        d[nm + "T"] = ct(np.asarray(inp["L0_ffn_" + nm], f).transpose(0, 2, 1))
    Cd = np.asarray(inp["L0_ffn_Cd"], f)
    d["CdT"] = ct(Cd.reshape(NE, 16, 128, 128).transpose(1, 0, 3, 2))
    for nm in ("s_Rg", "s_Ru", "s_Ug", "s_Uu", "s_Ud", "s_Cg", "s_Cu", "s_Rd", "s_Cd"):
        d[nm.replace("_", "") + "T"] = ct(np.asarray(inp["L0_ffn_" + nm], f).T)
    d["gnw2"] = ct(np.asarray(inp["L0_attn_gn_weight"], f).reshape(NHP, 2).T)
    d["gnb2"] = ct(np.asarray(inp["L0_attn_gn_bias"], f).reshape(NHP, 2).T)
    d.update(_const_tables())
    return d


def _prep_core(inp, shared, rot, core):
    b, own = core // 2, core % 2
    xb = np.asarray(inp["L0_attn_x"], np.float32)[b]
    perm = np.r_[own * HF:(own + 1) * HF, (1 - own) * HF:(2 - own) * HF]
    m = dict(shared)
    m["x"] = np.ascontiguousarray(xb[perm])
    m["cq"] = np.ascontiguousarray(rot[0][:, perm])
    m["sq"] = np.ascontiguousarray(rot[1][:, perm])
    m["ck"] = np.ascontiguousarray(rot[2][:, perm])
    m["sk"] = np.ascontiguousarray(rot[3][:, perm])
    m["mA"] = np.full((128, 1), float(own), np.float32)
    m["mB"] = np.full((128, 1), float(1 - own), np.float32)
    return m



class Sem:
    __slots__ = ("h", "v")

    def __init__(self, h):
        self.h = h
        self.v = 0


class B:
    def __init__(self):
        nc = bass.Bass("TRN2", target_bir_lowering=False)
        self.nc = nc
        self.EL = {k: [] for k in ("sync", "tensor", "vector", "scalar", "gpsimd")}
        self.bankg = {}
        self.es = contextlib.ExitStack()
        self.es.__enter__()
        self.taps = {}
        self._declare()

    def op(self, eng, fn=None, waits=(), incs=()):
        ws = tuple(w for w in waits if w is not None)
        self.EL[eng].append((fn, ws, tuple(incs)))

    def W(self, s, v=None):
        return (s.h, s.v if v is None else v)

    def I(self, s, n=1):
        s.v += n
        return (s.h, n)

    def mmw(self, keys, extra=()):
        ws = [w for w in extra if w is not None]
        seen = set()
        for k in keys:
            g = self.bankg.get(k)
            if g is not None and g not in seen:
                ws.append(g); seen.add(g)
        return ws

    def setg(self, keys, w):
        for k in keys:
            self.bankg[k] = w

    def _declare(self):
        nc = self.nc
        self.D = {}

        def din(name, shape, dt=F32):
            self.D[name] = nc.dram_tensor(name, shape, dt, kind="ExternalInput").ap()

        din("x", [S, HID])
        for n in ("cq", "sq", "ck", "sk"):
            din(n, [128, S])
        for n in ("wqT", "wkT", "wvT", "wgT", "woT"):
            din(n, [HID, HID])
        din("gwT", [HID, 32])
        din("RgT", [HID, 128]); din("RuT", [HID, 128]); din("RdT", [IFF, 128])
        din("UgT", [NE, 128, 128]); din("CgT", [NE, 128, IFF])
        din("UuT", [NE, 128, 128]); din("CuT", [NE, 128, IFF])
        din("UdT", [NE, 128, 128]); din("CdT", [16, NE, 128, 128])
        din("sRgT", [HID, 128]); din("sRuT", [HID, 128])
        din("sUgT", [128, 128]); din("sUuT", [128, 128]); din("sUdT", [128, 128])
        din("sCgT", [128, SI]); din("sCuT", [128, SI])
        din("sRdT", [SI, 128]); din("sCdT", [128, HID])
        din("gnw2", [2, NHP]); din("gnb2", [2, NHP])
        din("dtb", [128, NHP * 256]); din("xib", [128, NHP * 128])
        din("etb", [128, NHP * 128]); din("g128", [128, NHP])
        din("bdp", [128, 128]); din("iden", [128, 128])
        din("sel_e", [32, NE * 128]); din("sel64", [128, 2]); din("rep2", [2, 128])
        din("mA", [128, 1]); din("mB", [128, 1])
        self.out = nc.dram_tensor("out", [HF, HID], F32, kind="ExternalOutput").ap()
        self.ln1_sp = nc.dram_tensor("ln1_sp", [S, HID], BF16, kind="Internal").ap()
        self.ln2_sp = nc.dram_tensor("ln2_sp", [HF, HID], BF16, kind="Internal").ap()
        self.kv_sp = nc.dram_tensor("kv_sp", [NHP, 2, 128, HF], BF16, kind="Internal").ap()
        self.ret_sp = nc.dram_tensor("ret_sp", [NHP, 128, HF], F32, kind="Internal").ap()
        self.h1_sp = nc.dram_tensor("h1_sp", [HF, HID], F32, kind="Internal").ap()
        self.mish_sp = nc.dram_tensor("mish_sp", [16, 128, HF], BF16, kind="Internal").ap()

        if KDEBUG:
            def tap(name, shape, dt=F32):
                self.taps[name] = nc.dram_tensor("tap_" + name, shape, dt, kind="ExternalOutput").ap()
            if KDEBUG.get("ln1"): tap("ln1", [S, HID], BF16)
            if KDEBUG.get("p0dbg"): tap("p0dbg", [128, 32 * 8])
            if KDEBUG.get("p0dbg"): tap("p0dbg2", [128, 32 * 4])
            if KDEBUG.get("spb"): tap("spb", [128, 1024])
            if KDEBUG.get("ret"): tap("ret", [NHP, 128, HF])
            if KDEBUG.get("gn"): tap("gn", [128, 2 * NHP])
            if KDEBUG.get("h1"): tap("h1", [HF, HID])
            if KDEBUG.get("we"): tap("we", [128, NHP * 32])
            if KDEBUG.get("rg"): tap("rg", [128, HF], BF16)
            if KDEBUG.get("srdu"): tap("srdu", [128, HF], BF16)
            if KDEBUG.get("rdu"): tap("rdu", [128, NE * SPAN], BF16)

        ARN_N = 95936
        self.ARN = self.es.enter_context(nc.sbuf_tensor("b_ARN", [128, ARN_N], BF16))
        reqs = []

        def rq(name, units, ph0, ph1, f32=False, nparts=128):
            reqs.append((name, units, f32, ph0, ph1, nparts))

        rq("dtb", NHP * 256, 0, 3)
        rq("xib", NHP * 128, 0, 3)
        rq("etb", NHP * 128, 0, 3)
        rq("sel_e", NE * 128, 5, 7, nparts=32)
        rq("BIGA", 16 * HF, 1, 7)
        for n in ("cqb", "sqb", "ckb", "skb"):
            rq(n, HF, 1, 3)
        for n in ("qraw", "kraw", "vfm", "vtm", "keta", "xiq"):
            rq(n, HF, 1, 3)
        rq("qp", HF, 1, 4)
        rq("kp", HF, 1, 3)
        rq("t2", HF, 1, 4)
        rq("stagf", 2 * HF, 0, 7, f32=True)
        rq("wq0", HF, 2, 3); rq("wq1", HF, 2, 3)
        rq("wk0", HF, 1, 2); rq("wk1", HF, 1, 2)
        rq("wv0", HF, 1, 2); rq("wv1", HF, 1, 2)
        rq("wgen0", HF, 4, 5); rq("wgen1", HF, 4, 5)
        rq("xtb0", HID, 0, 1); rq("xtb1", HID, 0, 1)
        rq("lnt0", HID, 0, 1); rq("lnt1", HID, 0, 1)
        rq("ln2t0", HID, 4, 4); rq("ln2t1", HID, 4, 4)
        rq("xtf0", 2 * HID, 4, 4, f32=True); rq("xtf1", 2 * HID, 4, 4, f32=True)
        rq("h1t0", 2 * HID, 4, 7, f32=True); rq("h1t1", 2 * HID, 4, 4, f32=True)
        rq("strip0", 2 * HID, 4, 4, f32=True); rq("strip1", 2 * HID, 4, 4, f32=True)
        rq("mstrip0", HID, 4, 4); rq("mstrip1", HID, 4, 4)
        rq("msh0", 1024, 4, 4); rq("msh1", 1024, 4, 4)
        rq("rg_sb", HF, 5, 7); rq("ru_sb", HF, 5, 7)
        rq("srg_sb", HF, 5, 6); rq("sru_sb", HF, 5, 6)
        rq("srdu_sb", HF, 5, 7)
        rq("weT", HF, 5, 7, nparts=32)
        rq("gwsb", NHP * 32, 5, 5)
        rq("rdT_sb", NI * 128, 5, 7)
        rq("sCg_sb", SI, 5, 6); rq("sCu_sb", SI, 5, 6)
        rq("sRd_sb", NSI * 128, 5, 6)
        rq("sU_sb", 3 * 128, 5, 6)
        rq("sCd_sb", HID, 5, 7)
        rq("sgb", HF, 6, 6)
        rq("hsg", HF, 6, 6); rq("hsu", HF, 6, 6)
        rq("lgf", 2 * HF, 5, 5, f32=True, nparts=32)
        rq("lgtm", 2 * NHP * 32, 5, 5, f32=True)
        rq("wetm", 2 * NHP * 32, 5, 5, f32=True)
        rq("smX", 6 * HF, 6, 6)
        rq("hg_sb", SPAN, 7, 7); rq("hu_sb", SPAN, 7, 7)
        rq("rd_sb", SPAN, 7, 7); rq("wb_sb", SPAN, 7, 7)
        rq("gblk", SPAN, 7, 7); rq("gblkB", SPAN, 7, 7); rq("mblk", SPAN, 7, 7)
        for i in range(2):
            rq(f"ug{i}", 128, 7, 7); rq(f"cg{i}", IFF, 7, 7)
            rq(f"uu{i}", 128, 7, 7); rq(f"cu{i}", IFF, 7, 7)
            rq(f"ud{i}", 128, 7, 7)
            rq(f"cdw{i}", NE * 128, 7, 7)
        rq("OASM", 2 * 4 * HID, 7, 7, f32=True)
        rq("BIGF", 2 * 1024, 4, 4, f32=True)
        placed = []
        self.buf = {}
        for name, units, f32, ph0, ph1, nparts in sorted(reqs, key=lambda r: -r[1]):
            cand = 0
            while True:
                lo, hi = cand, cand + units
                clash = None
                for (o, u, a0, a1) in placed:
                    if not (hi <= o or o + u <= lo) and not (ph1 < a0 or a1 < ph0):
                        clash = o + u
                        break
                if clash is None:
                    break
                cand = (clash + 15) // 16 * 16
            assert cand + units <= ARN_N, f"arena overflow at {name}: {cand}+{units}"
            placed.append((cand, units, ph0, ph1))
            ap = self.ARN[0:nparts, cand:cand + units]
            if f32:
                ap = ap.bitcast(F32)
            self.buf[name] = ap
        self._arena_peak = max(o + u for (o, u, _, _) in placed)
        bf = self.buf
        self.dtb, self.xib, self.etb, self.sel_e = bf["dtb"], bf["xib"], bf["etb"], bf["sel_e"]
        self.BIGA = bf["BIGA"]
        self.cqb, self.sqb, self.ckb, self.skb = bf["cqb"], bf["sqb"], bf["ckb"], bf["skb"]
        self.qraw, self.kraw, self.qp, self.kp = bf["qraw"], bf["kraw"], bf["qp"], bf["kp"]
        self.t2, self.vfm, self.vtm, self.keta, self.xiq = bf["t2"], bf["vfm"], bf["vtm"], bf["keta"], bf["xiq"]
        self.stagf = bf["stagf"]
        self.wqs = [bf["wq0"], bf["wq1"]]
        self.wks = [bf["wk0"], bf["wk1"]]
        self.wvs = [bf["wv0"], bf["wv1"]]
        self.wgen = [bf["wgen0"], bf["wgen1"]]
        self.rg_sb, self.ru_sb = bf["rg_sb"], bf["ru_sb"]
        self.srg_sb, self.sru_sb, self.srdu_sb = bf["srg_sb"], bf["sru_sb"], bf["srdu_sb"]
        self.weT, self.gwsb, self.rdT_sb = bf["weT"], bf["gwsb"], bf["rdT_sb"]
        self.sCg_sb, self.sCu_sb, self.sRd_sb = bf["sCg_sb"], bf["sCu_sb"], bf["sRd_sb"]
        self.sU_sb, self.sCd_sb, self.sgb = bf["sU_sb"], bf["sCd_sb"], bf["sgb"]
        self.hsg, self.hsu = bf["hsg"], bf["hsu"]
        self.lgf, self.lgtm, self.wetm = bf["lgf"], bf["lgtm"], bf["wetm"]
        self.smX = bf["smX"]
        self.hg_sb, self.hu_sb, self.rd_sb, self.wb_sb = bf["hg_sb"], bf["hu_sb"], bf["rd_sb"], bf["wb_sb"]
        self.gblk, self.mblk = bf["gblk"], bf["mblk"]
        self.gblk2 = [bf["gblk"], bf["gblkB"]]
        self.eW = [dict(ug=bf[f"ug{i}"], cg=bf[f"cg{i}"], uu=bf[f"uu{i}"], cu=bf[f"cu{i}"], ud=bf[f"ud{i}"]) for i in range(2)]
        self.cdW = [bf["cdw0"], bf["cdw1"]]
        self.OASM = bf["OASM"]
        self.BIGF = bf["BIGF"]
        self.xt = [bf["xtb0"], bf["xtb1"]]
        self.lnt = [bf["lnt0"], bf["lnt1"]]
        self.ln2t = [bf["ln2t0"], bf["ln2t1"]]
        self.xtf = [bf["xtf0"], bf["xtf1"]]
        self.h1t = [bf["h1t0"], bf["h1t1"]]
        self.strip = [bf["strip0"], bf["strip1"]]
        self.mstrip = [bf["mstrip0"], bf["mstrip1"]]
        self.msh = [bf["msh0"], bf["msh1"]]
        sb = lambda name, shape, dt=BF16: self.es.enter_context(nc.sbuf_tensor(name, shape, dt))
        self.g128 = sb("s_g128", [128, NHP], F32)
        self.bdp = sb("s_bdp", [128, 128])
        self.idb = sb("s_idb", [128, 128])
        self.idf = sb("s_idf", [128, 128], F32)
        self.sel64 = sb("s_sel64", [128, 2], F32)
        self.rep2 = sb("s_rep2", [2, 128], F32)
        self.gnw2 = sb("s_gnw2", [2, NHP], F32)
        self.gnb2 = sb("s_gnb2", [2, NHP], F32)
        self.mAb = sb("s_mA", [128, 1], F32)
        self.mBb = sb("s_mB", [128, 1], F32)
        self.S_a = sb("s_Sa", [128, 1024], F32)
        self.S_b = sb("s_Sb", [128, 1024], F32)
        self.S_hist = sb("s_Shist", [128, NCH * 64])
        self.stats = sb("s_stats", [128, NHP * 120], F32)
        self.mv_all = sb("s_mv", [128, 2 * NHP], F32)
        self.st3 = sb("s_st3", [128, 3 * NHP], F32)
        self.fin2 = sb("s_fin2", [2, 4 * NHP], F32)
        self.A_vec = sb("s_Avec", [128, NHP], F32)
        self.C_vec = sb("s_Cvec", [128, NHP], F32)
        self.sm1 = sb("s_sm1", [128, 8], F32)
        self.sm4 = sb("s_sm4", [2, 3 * NHP], F32)
        self.lst = sb("l_lst", [128, 24], F32)
        self.ssep = sb("s_ssep", [128, 2], F32)
        self.lmv = sb("l_lmv", [128, 2], F32)
        self.PF = [self.es.enter_context(nc.psum_tensor(f"p_f{i}", [128, 1024], F32)) for i in range(3)]
        self.PBT = self.es.enter_context(nc.psum_tensor("p_bt", [128, 2048], BF16))
        self.PBTF = self.PBT[:, :].bitcast(F32)
        for n in ("qd", "qg", "te", "ve", "se", "os"):
            setattr(self, n, Sem(self.es.enter_context(nc.semaphore("sem_" + n))))

    def vsep(self):
        self.op("vector", lambda v: v.tensor_copy(self.ssep[:, 0:1], self.ssep[:, 1:2]))

    PK4 = ("pf0a", "pf0b", "pf1a", "pf1b")

    def pf_half(self, sp, bs="A"):
        if bs == "A":
            pf = self.PF[sp // 2]
            return pf[:, (sp % 2) * 512:(sp % 2 + 1) * 512], ("pf0a", "pf0b", "pf1a", "pf1b")[sp]
        src_ = self.PF[2] if sp < 2 else self.PBTF
        return src_[:, (sp % 2) * 512:(sp % 2 + 1) * 512], ("pf2a", "pf2b", "pbt", "pbt")[sp]

    def bs_keys(self, bs):
        return self.PK4 if bs == "A" else ("pf2a", "pf2b", "pbt")

    def bs_buf(self, h, bs):
        return self.PF[h] if bs == "A" else (self.PF[2] if h == 0 else self.PBTF)

    def emit_ln(self, xbuf, outbuf, first_wait, out_wait):
        p, I, W = self.op, self.I, self.W
        ve, se = self.ve, self.se
        sm1 = self.sm1
        p("vector", lambda v, a=xbuf: v.tensor_reduce(sm1[:, 5:6], a[:], mybir.AxisListType.X, AL.add),
          (first_wait,))
        p("scalar", lambda s, a=xbuf: s.activation(self.stagf[:, 0:2048], a[:], AF.Square),
          (first_wait,), [I(se)])
        p("vector", lambda v: v.tensor_reduce(sm1[:, 6:7], self.stagf[:, 0:2048], mybir.AxisListType.X, AL.add),
          (W(se),))
        p("vector", lambda v: v.tensor_scalar(sm1[:, 5:6], sm1[:, 5:6], 1.0 / HID, None, AL.mult))
        p("vector", lambda v: v.tensor_scalar(sm1[:, 6:7], sm1[:, 6:7], 1.0 / HID, None, AL.mult))
        p("vector", lambda v: v.tensor_tensor(sm1[:, 3:4], sm1[:, 5:6], sm1[:, 5:6], AL.mult))
        self.vsep()
        p("vector", lambda v: v.tensor_tensor(sm1[:, 0:1], sm1[:, 6:7], sm1[:, 3:4], AL.subtract))
        self.vsep()
        p("vector", lambda v: v.tensor_scalar(sm1[:, 0:1], sm1[:, 0:1], EPS, None, AL.add), (), [I(ve)])
        p("scalar", lambda s: s.activation(sm1[:, 1:2], sm1[:, 0:1], AF.Sqrt), (W(ve),), [I(se)])
        p("vector", lambda v: v.reciprocal(sm1[:, 2:3], sm1[:, 1:2]), (W(se),), [I(ve)])
        p("scalar", lambda s: s.activation(sm1[:, 7:8], sm1[:, 2:3], AF.Identity), (W(ve),))
        p("scalar", lambda s: s.activation(sm1[:, 3:4], sm1[:, 2:3], AF.Square), (), [I(se)])
        p("vector", lambda v: v.tensor_tensor(sm1[:, 3:4], sm1[:, 3:4], sm1[:, 0:1], AL.mult), (W(se),))
        self.vsep()
        p("vector", lambda v: v.tensor_scalar(sm1[:, 3:4], sm1[:, 3:4], -0.5, 1.5, AL.mult, AL.add))
        self.vsep()
        p("vector", lambda v: v.tensor_tensor(sm1[:, 2:3], sm1[:, 7:8], sm1[:, 3:4], AL.mult))
        self.vsep()
        p("vector", lambda v: v.tensor_tensor(sm1[:, 4:5], sm1[:, 5:6], sm1[:, 2:3], AL.mult))
        self.vsep()
        p("vector", lambda v: v.tensor_scalar(sm1[:, 4:5], sm1[:, 4:5], -1.0, None, AL.mult))
        self.vsep()
        p("vector", lambda v, a=xbuf, o=outbuf: v.tensor_scalar(o[:], a[:], sm1[:, 2:3], sm1[:, 4:5], AL.mult, AL.add),
          (out_wait,), [I(ve)])
        return W(ve)

    def load_lnfm(self, src, rows0, wait):
        p, I, W = self.op, self.I, self.W
        waits = wait if isinstance(wait, tuple) and (not wait or isinstance(wait[0], tuple)) else (wait,)
        for cc in range(16):
            p("sync", lambda s, c=cc: s.dma_start(
                out=self.BIGA[:, c * HF:(c + 1) * HF],
                in_=src[rows0:rows0 + HF, c * 128:(c + 1) * 128], transpose=True),
              waits if cc == 0 else (), [I(self.qd, 16)])
        return W(self.qd)

    def load_rot(self, half0, which, waits=()):
        p, I, W = self.op, self.I, self.W
        for i, (name, buf) in enumerate(which):
            p("gpsimd", lambda g, n=name, bf=buf: g.dma_start(out=bf[:], in_=self.D[n][:, half0:half0 + HF]),
              waits if i == 0 else (), [I(self.qg, 16)])
        return W(self.qg)

    def wslice(self, dst, wname, hp, waits=()):
        p, I, W = self.op, self.I, self.W
        p("gpsimd", lambda g, d=dst, n=wname, h=hp: g.dma_start(
            out=d[:].rearrange("p (c f) -> p c f", f=128),
            in_=self.D[n][:, h * 128:(h + 1) * 128].rearrange("(c p) f -> p c f", p=128)),
          waits, [I(self.qg, 16)])
        return W(self.qg)

    def proj(self, wbuf, waits, bs="A"):
        p, I, W = self.op, self.I, self.W
        ws = self.mmw(self.bs_keys(bs), waits)
        for fi in range(16):
            for sp in range(4):
                tgt, _ = self.pf_half(sp, bs)
                p("tensor", lambda t, tg=tgt, w=wbuf, fi=fi, sp=sp: t.matmul(
                    tg, w[:, fi * 128:(fi + 1) * 128],
                    self.BIGA[:, fi * HF + sp * 512: fi * HF + sp * 512 + 512],
                    start=(fi == 0), stop=(fi == 15)),
                  ws if (fi == 0 and sp == 0) else (),
                  [I(self.te)] if (fi == 15 and sp == 3) else ())
        return W(self.te)

    def evac2(self, eng, dstbuf, wait, bs="A"):
        p, I, W = self.op, self.I, self.W
        sem = self.ve if eng == "vector" else self.se
        for h in range(2):
            pbuf = self.bs_buf(h, bs)
            if eng == "vector":
                p(eng, lambda E, d=dstbuf, h=h, pb=pbuf: E.tensor_copy(d[:, h * 1024:(h + 1) * 1024], pb[:, :] if pb.shape[1] == 1024 else pb[:, 0:1024]),
                  (wait,) if h == 0 else (), [I(sem)] if h == 1 else [])
            else:
                p(eng, lambda E, d=dstbuf, h=h, pb=pbuf: E.copy(d[:, h * 1024:(h + 1) * 1024], pb[:, :] if pb.shape[1] == 1024 else pb[:, 0:1024]),
                  (wait,) if h == 0 else (), [I(sem)] if h == 1 else [])
        w = W(sem)
        self.setg(self.bs_keys(bs), w)
        return w

    def rot_apply(self, rawbuf, outbuf, ctab, stab, wait_raw, bs="A"):
        p, I, W = self.op, self.I, self.W
        ws = self.mmw(self.bs_keys(bs), (wait_raw, self.const_done))
        for sp in range(4):
            tgt, _ = self.pf_half(sp, bs)
            p("tensor", lambda t, tg=tgt, r=rawbuf, sp=sp: t.matmul(
                tg, self.bdp[:], r[:, sp * 512:(sp + 1) * 512], start=True, stop=True),
              ws if sp == 0 else (), [I(self.te)] if sp == 3 else ())
        wte = W(self.te)
        for h in range(2):
            pbuf = self.bs_buf(h, bs)
            p("vector", lambda v, h=h, st=stab, pb=pbuf: v.tensor_tensor(
                self.t2[:, h * 1024:(h + 1) * 1024], pb[:, :] if pb.shape[1] == 1024 else pb[:, 0:1024],
                st[:, h * 1024:(h + 1) * 1024], AL.mult),
              (wte,) if h == 0 else (), [I(self.ve)] if h == 1 else [])
        self.setg(self.bs_keys(bs), W(self.ve))
        p("vector", lambda v, r=rawbuf, o=outbuf, ct=ctab: v.tensor_tensor(o[:], r[:], ct[:], AL.mult))
        p("vector", lambda v, o=outbuf: v.tensor_tensor(o[:], o[:], self.t2[:], AL.add), (), [I(self.ve)])
        return W(self.ve)

    def transposes_to(self, outbuf, srcbuf, wait_src, eta_hp=None):
        p, I, W = self.op, self.I, self.W
        ws = self.mmw(("pbt",), (wait_src, self.const_done))
        for cc in range(16):
            p("tensor", lambda t, c=cc, sb_=srcbuf: t.transpose(
                self.PBT[:, c * 128:(c + 1) * 128], sb_[:, c * 128:(c + 1) * 128], self.idb[:]),
              ws if cc == 0 else (), [I(self.te)] if cc == 15 else ())
        wte = W(self.te)
        if eta_hp is None:
            p("vector", lambda v, o=outbuf: v.tensor_copy(o[:], self.PBT[:, :]), (wte,), [I(self.ve)])
        else:
            p("vector", lambda v, o=outbuf, hp=eta_hp: v.tensor_tensor(
                o[:].rearrange("p (c f) -> p c f", f=128),
                self.PBT[:, :].rearrange("p (c f) -> p c f", f=128),
                self.etb[:, None, hp * 128:(hp + 1) * 128].broadcast_to((128, 16, 128)), AL.mult),
              (wte,), [I(self.ve)])
        self.setg(("pbt",), W(self.ve))
        return W(self.ve)

    def state_mm4(self, waits):
        p, I, W = self.op, self.I, self.W
        ws = self.mmw(("pf2a", "pf2b"), waits)
        for cc in range(NCH):
            for w in range(2):
                p("tensor", lambda t, c=cc, w=w: t.matmul(
                    self.PF[2][w * 64:(w + 1) * 64, c * 64:(c + 1) * 64],
                    self.keta[:, c * 128 + w * 64: c * 128 + w * 64 + 64],
                    self.vtm[:, c * 128 + w * 64: c * 128 + w * 64 + 64],
                    start=True, stop=True, tile_position=(0, w * 64)),
                  ws if (cc == 0 and w == 0) else (),
                  [I(self.te)] if (cc == NCH - 1 and w == 1) else ())
        return W(self.te)

    def state_chain(self, hp, wait_mm4, hist):
        p, I, W = self.op, self.I, self.W
        bufs = [self.S_a, self.S_b]
        for cc in range(NCH):
            src, dst = bufs[cc % 2], bufs[(cc + 1) % 2]
            if hist:
                p("vector", lambda v, c=cc, s=src, hp=hp: v.tensor_copy(
                    self.S_hist[:, c * 64:(c + 1) * 64], s[:, hp * 64:(hp + 1) * 64]))
            p("vector", lambda v, c=cc, s=src, d=dst, hp=hp: v.scalar_tensor_tensor(
                d[:, hp * 64:(hp + 1) * 64], s[:, hp * 64:(hp + 1) * 64], self.g128[:, hp:hp + 1],
                self.PF[2][:, c * 64:(c + 1) * 64], AL.mult, AL.add),
              (wait_mm4,) if cc == 0 else (),
              [I(self.ve)] if cc == NCH - 1 else [])
        self.setg(("pf2a", "pf2b"), W(self.ve))
        return W(self.ve)

    def retention_chunks(self, hp, waits_ready, do_spill, stats_off, spill_wait=None):
        p, I, W = self.op, self.I, self.W
        wv_hist = W(self.ve)

        def emit_mm1(cc, first):
            par = cc % 2
            ak = ("pf0a", "pf0b")[par]
            bk = ("pf1a", "pf1b")[par]
            atA = self.PF[0][:, par * 512: par * 512 + 128]
            atB = self.PF[1][:, par * 512: par * 512 + 128]
            ws = self.mmw((ak,), tuple(waits_ready) + (wv_hist,) if first else ())
            p("tensor", lambda t, c=cc, a=atA: t.matmul(
                a, self.kp[0:64, c * 128:(c + 1) * 128], self.qp[0:64, c * 128:(c + 1) * 128],
                start=True, stop=True), ws, [I(self.te)])
            ws = self.mmw((bk,))
            p("tensor", lambda t, c=cc, a=atB: t.matmul(
                a, self.kp[64:128, c * 128:(c + 1) * 128], self.qp[64:128, c * 128:(c + 1) * 128],
                start=True, stop=True), ws, [I(self.te)])
            return W(self.te)

        w_at = {0: emit_mm1(0, True)}
        for cc in range(NCH):
            par = cc % 2
            ak = ("pf0a", "pf0b")[par]
            bk = ("pf1a", "pf1b")[par]
            gk = ("pf2a", "pf2b")[par]
            atA = self.PF[0][:, par * 512: par * 512 + 128]
            atB = self.PF[1][:, par * 512: par * 512 + 128]
            crB = self.PF[1][:, par * 512 + 128: par * 512 + 256]
            rg = self.PF[2][:, par * 512: par * 512 + 128]
            p("vector", lambda v, c=cc, a=atA, hp=hp: v.tensor_tensor(
                self.t2[:, (c % 4) * 256:(c % 4) * 256 + 128], a, self.dtb[:, hp * 256:hp * 256 + 128], AL.mult),
              (w_at[cc],), [I(self.ve)])
            self.setg((ak,), W(self.ve))
            p("vector", lambda v, c=cc, a=atB, hp=hp: v.tensor_tensor(
                self.t2[:, (c % 4) * 256 + 128:(c % 4) * 256 + 256], a, self.dtb[:, hp * 256 + 128:hp * 256 + 256], AL.mult),
              (), [I(self.ve)])
            w_atd = W(self.ve)
            if cc + 1 < NCH:
                w_at[cc + 1] = emit_mm1(cc + 1, False)
            ws2 = self.mmw((gk,), (w_atd,))
            p("tensor", lambda t, c=cc, r=rg: t.matmul(
                r[0:64, :], self.vtm[:, c * 128: c * 128 + 64],
                self.t2[:, (c % 4) * 256:(c % 4) * 256 + 128],
                start=True, stop=False, tile_position=(0, 0), skip_group_check=True), ws2)
            p("tensor", lambda t, c=cc, r=rg: t.matmul(
                r[64:128, :], self.vtm[:, c * 128 + 64: c * 128 + 128],
                self.t2[:, (c % 4) * 256 + 128:(c % 4) * 256 + 256],
                start=True, stop=True, tile_position=(0, 64), skip_group_check=True))
            p("tensor", lambda t, c=cc, r=rg: t.matmul(
                r[0:64, :], self.S_hist[0:64, c * 64:(c + 1) * 64],
                self.xiq[0:64, c * 128:(c + 1) * 128],
                start=False, stop=True, tile_position=(0, 0), skip_group_check=True),
              (), [I(self.te)])
            p("tensor", lambda t, c=cc, a=crB: t.matmul(
                a[64:128, :], self.S_hist[64:128, c * 64:(c + 1) * 64],
                self.xiq[64:128, c * 128:(c + 1) * 128],
                start=True, stop=True, tile_position=(64, 64), skip_group_check=True),
              (), [I(self.te)])
            w_ret = W(self.te)
            p("vector", lambda v, c=cc, r=rg: v.tensor_copy(self.stagf[:, c * 128:(c + 1) * 128], r),
              (w_ret, spill_wait) if (cc == 0 and do_spill) else (w_ret,), [I(self.ve)])
            self.setg((gk,), W(self.ve))
            p("vector", lambda v, c=cc, a=crB: v.tensor_tensor(
                self.stagf[64:128, c * 128:(c + 1) * 128], a[64:128, :],
                self.stagf[64:128, c * 128:(c + 1) * 128], AL.add), (), [I(self.ve)])
            self.setg((bk,), W(self.ve))
            if not do_spill:
                p("vector", lambda v, c=cc, so=stats_off: v.bn_stats(
                    self.stats[:, so + c * 6: so + (c + 1) * 6], self.stagf[:, c * 128:(c + 1) * 128]),
                  (), [I(self.ve)])
        if do_spill:
            for i in range(4):
                p("vector", lambda v, i=i, so=stats_off: v.bn_stats(
                    self.stats[:, so + i * 6: so + (i + 1) * 6], self.stagf[:, i * 512:(i + 1) * 512]))
            p("vector", lambda v: v.tensor_copy(self.sm1[:, 5:6], self.stagf[:, 0:1]), (), [I(self.ve)])
        return W(self.ve)

    def build(self):
        p, I, W = self.op, self.I, self.W
        qd, qg, te, ve, se, os_ = self.qd, self.qg, self.te, self.ve, self.se, self.os
        D = self.D

        for dst, src in ((self.dtb, D["dtb"]), (self.xib, D["xib"]), (self.etb, D["etb"]),
                         (self.bdp, D["bdp"]), (self.idb, D["iden"]),
                         (self.g128, D["g128"]), (self.idf, D["iden"]), (self.sel64, D["sel64"]),
                         (self.rep2, D["rep2"]), (self.gnw2, D["gnw2"]), (self.gnb2, D["gnb2"]),
                         (self.mAb, D["mA"]), (self.mBb, D["mB"])):
            ndim = len(src.shape)
            p("gpsimd", lambda g, a=dst, b=src: g.dma_start(out=a[:], in_=b[:, :] if len(b.shape) == 2 else b[:, :, :]),
              (), [I(qg, 16)])
        self.const_done = W(qg)

        tile_order = list(range(16, 32)) + list(range(16))
        lnw = {}
        for idx, t in enumerate(tile_order):
            bi = idx % 2
            p("gpsimd", lambda g, b=bi, t=t: g.dma_start(out=self.xt[b][:], in_=D["x"][t * 128:(t + 1) * 128, :]),
              (W(ve),) if idx >= 2 else (), [I(qg, 16)])
            w_apply = self.emit_ln(self.xt[bi], self.lnt[bi], W(qg), W(qg) if idx >= 2 else None)
            p("gpsimd", lambda g, b=bi, t=t: g.dma_start(out=self.ln1_sp[t * 128:(t + 1) * 128, :], in_=self.lnt[b][:]),
              (w_apply,), [I(qg, 16)])
            if KDEBUG.get("p0dbg"):
                p("gpsimd", lambda g, t=t: g.dma_start(out=self.taps["p0dbg"][:, t * 8:(t + 1) * 8], in_=self.sm1[:]),
                  (), [I(qg, 16)])
                p("gpsimd", lambda g, t=t: g.dma_start(out=self.taps["p0dbg2"][:, t * 4:t * 4 + 2], in_=self.lmv[:]),
                  (), [I(qg, 16)])
                p("gpsimd", lambda g, t=t: g.dma_start(out=self.taps["p0dbg2"][:, t * 4 + 2:t * 4 + 4], in_=self.sm1[:, 5:7]),
                  (), [I(qg, 16)])
            if KDEBUG.get("ln1"):
                p("gpsimd", lambda g, b=bi, t=t: g.dma_start(out=self.taps["ln1"][t * 128:(t + 1) * 128, :], in_=self.lnt[b][:]),
                  (), [I(qg, 16)])
            lnw[t] = W(qg)
        ln_pb_done, ln_pa_done = lnw[31], lnw[15]
        if KPHASES[0] < 1:
            return self._finish_early()

        w_lnfm = self.load_lnfm(self.ln1_sp, HF, ln_pb_done)
        w_rot = self.load_rot(HF, (("ck", self.ckb), ("sk", self.skb)))
        p("vector", lambda v: v.memset(self.S_a[:], 0.0), (), [I(ve)])
        for hp in range(NHP):
            bi = hp % 2
            wk_w = self.wslice(self.wks[bi], "wkT", hp, (W(te),) if hp >= 2 else ())
            wv_w = self.wslice(self.wvs[bi], "wvT", hp)
            wte = self.proj(self.wks[bi], (wk_w, w_lnfm if hp == 0 else None,
                                           w_rot if hp == 0 else None,
                                           self.const_done if hp == 0 else None), bs="A")
            wte_v = self.proj(self.wvs[bi], (wv_w,), bs="B")
            w_kraw = self.evac2("scalar", self.kraw, wte, bs="A")
            w_kp = self.rot_apply(self.kraw, self.kp, self.ckb, self.skb, w_kraw, bs="A")
            w_vfm = self.evac2("vector", self.vfm, wte_v, bs="B")
            w_vtm = self.transposes_to(self.vtm, self.vfm, w_vfm)
            w_keta = self.transposes_to(self.keta, self.kp, w_kp, eta_hp=hp)
            p("gpsimd", lambda g, hp=hp: g.dma_start(out=self.kv_sp[hp, 0, :, :], in_=self.kp[:]), (w_kp,), [I(qg, 16)])
            p("gpsimd", lambda g, hp=hp: g.dma_start(out=self.kv_sp[hp, 1, :, :], in_=self.vtm[:]), (w_vtm,), [I(qg, 16)])
            w_mm4 = self.state_mm4((w_keta, w_vtm))
            self.state_chain(hp, w_mm4, hist=False)
        if KDEBUG.get("spb"):
            p("gpsimd", lambda g: g.dma_start(out=self.taps["spb"][:, :], in_=self.S_a[:]), (W(ve),), [I(qg, 16)])
        if KPHASES[0] < 2:
            return self._finish_early()

        p("vector", lambda v: v.tensor_scalar(self.S_a[:], self.S_a[:], self.mAb[:, 0:1], None, AL.mult),
          (self.const_done,), [I(ve)])
        w_lnfm = self.load_lnfm(self.ln1_sp, 0, (ln_pa_done, W(te)))
        w_rot = self.load_rot(0, (("cq", self.cqb), ("sq", self.sqb), ("ck", self.ckb), ("sk", self.skb)),
                              waits=(W(ve),))
        for hp in range(NHP):
            bi = hp % 2
            wq_w = self.wslice(self.wqs[bi], "wqT", hp, (W(te),) if hp >= 2 else (ln_pa_done, W(te)))
            wk_w = self.wslice(self.wks[bi], "wkT", hp)
            wv_w = self.wslice(self.wvs[bi], "wvT", hp)
            wte_q = self.proj(self.wqs[bi], (wq_w, w_lnfm if hp == 0 else None, w_rot if hp == 0 else None), bs="A")
            wte_k = self.proj(self.wks[bi], (wk_w,), bs="B")
            w_qraw = self.evac2("scalar", self.qraw, wte_q, bs="A")
            w_qp = self.rot_apply(self.qraw, self.qp, self.cqb, self.sqb, w_qraw, bs="A")
            p("vector", lambda v, hp=hp: v.tensor_tensor(
                self.xiq[:].rearrange("p (c f) -> p c f", f=128),
                self.qp[:].rearrange("p (c f) -> p c f", f=128),
                self.xib[:, None, hp * 128:(hp + 1) * 128].broadcast_to((128, 16, 128)), AL.mult),
              (w_qp,), [I(ve)])
            w_xiq = W(ve)
            w_kraw = self.evac2("scalar", self.kraw, wte_k, bs="B")
            w_kp = self.rot_apply(self.kraw, self.kp, self.ckb, self.skb, w_kraw, bs="B")
            wte_v = self.proj(self.wvs[bi], (wv_w,), bs="A")
            w_vfm = self.evac2("vector", self.vfm, wte_v, bs="A")
            w_vtm = self.transposes_to(self.vtm, self.vfm, w_vfm)
            w_keta = self.transposes_to(self.keta, self.kp, w_kp, eta_hp=hp)
            w_mm4 = self.state_mm4((w_keta, w_vtm))
            self.state_chain(hp, w_mm4, hist=True)
            w_rc = self.retention_chunks(hp, (w_xiq, w_keta), do_spill=True, stats_off=hp * 120,
                                         spill_wait=W(self.qd) if hp > 0 else None)
            p("sync", lambda s, hp=hp: s.dma_start(out=self.ret_sp[hp, :, :], in_=self.stagf[:]), (w_rc,), [I(qd, 16)])
            if KDEBUG.get("ret"):
                p("gpsimd", lambda g, hp=hp: g.dma_start(out=self.taps["ret"][hp, :, :], in_=self.stagf[:]), (w_rc,), [I(qg, 16)])
        w_ret_spill = W(qd)
        if KPHASES[0] < 3:
            return self._finish_early()

        p("vector", lambda v: v.tensor_scalar(self.S_a[:], self.S_a[:], self.mBb[:, 0:1], None, AL.mult), (), [I(ve)])
        w_lnfm = self.load_lnfm(self.ln1_sp, HF, W(te))
        w_rot = self.load_rot(HF, (("cq", self.cqb), ("sq", self.sqb)), waits=(W(ve),))
        for hp in range(NHP):
            bi = hp % 2
            wq_w = self.wslice(self.wqs[bi], "wqT", hp, (W(te),) if hp >= 2 else ())
            p("sync", lambda s, hp=hp: s.dma_start(out=self.kp[:], in_=self.kv_sp[hp, 0, :, :]),
              (W(te),), [I(qd, 16)])
            p("sync", lambda s, hp=hp: s.dma_start(out=self.vtm[:], in_=self.kv_sp[hp, 1, :, :]), (), [I(qd, 16)])
            w_kv = W(qd)
            wte = self.proj(self.wqs[bi], (wq_w, w_lnfm if hp == 0 else None, w_rot if hp == 0 else None))
            w_qraw = self.evac2("scalar", self.qraw, wte)
            w_qp = self.rot_apply(self.qraw, self.qp, self.cqb, self.sqb, w_qraw)
            p("vector", lambda v, hp=hp: v.tensor_tensor(
                self.xiq[:].rearrange("p (c f) -> p c f", f=128),
                self.qp[:].rearrange("p (c f) -> p c f", f=128),
                self.xib[:, None, hp * 128:(hp + 1) * 128].broadcast_to((128, 16, 128)), AL.mult),
              (w_qp,), [I(ve)])
            w_xiq = W(ve)
            w_keta = self.transposes_to(self.keta, self.kp, w_kv, eta_hp=hp)
            w_mm4 = self.state_mm4((w_keta, w_kv))
            self.state_chain(hp, w_mm4, hist=True)
            self.retention_chunks(hp, (w_xiq, w_keta), do_spill=False, stats_off=hp * 120 + 24)

        if KPHASES[0] < 4:
            return self._finish_early()
        self._stats_finalize()
        w_mish = self._mish(ln_pa_done)
        w_h1_all = self._wo_phase(w_mish, w_ret_spill)
        if KPHASES[0] < 5:
            return self._finish_early()
        self._ffn(w_h1_all)

        return self._emit()

    def _finish_early(self):
        return self._emit()

    def _emit(self):
        p, W = self.op, self.W
        p("sync", None, (W(self.qd), W(self.qg), W(self.te), W(self.ve), W(self.se), W(self.os)))
        blk = self.es.enter_context(self.nc.Block())

        def mk(eng_name):
            def f(eng):
                for fn, waits, incs in self.EL[eng_name]:
                    for (h, v) in waits:
                        eng.wait_ge(h, v)
                    inst = fn(eng) if fn is not None else None
                    if inst is not None:
                        for (h, n) in incs:
                            inst.then_inc(h, n)
            return f

        blk.sync(mk("sync"))
        blk.tensor(mk("tensor"))
        blk.vector(mk("vector"))
        blk.scalar(mk("scalar"))
        blk.gpsimd(mk("gpsimd"))
        return self.nc

    def _stats_finalize(self):
        p, I, W = self.op, self.I, self.W
        te, ve, se = self.te, self.ve, self.se
        st3, fin2, sm4, sm1 = self.st3, self.fin2, self.sm4, self.sm1
        scr = self.stagf
        for hp in range(NHP):
            b0 = hp * 120
            p("vector", lambda v, b0=b0: v.tensor_reduce(sm1[:, 0:1], self.stats[:, b0 + 1:b0 + 24:3], mybir.AxisListType.X, AL.add))
            p("vector", lambda v, b0=b0: v.tensor_reduce(sm1[:, 1:2], self.stats[:, b0 + 25:b0 + 120:3], mybir.AxisListType.X, AL.add))
            p("vector", lambda v, b0=b0: v.tensor_reduce(sm1[:, 2:3], self.stats[:, b0 + 2:b0 + 24:3], mybir.AxisListType.X, AL.add))
            p("vector", lambda v, b0=b0: v.tensor_reduce(sm1[:, 3:4], self.stats[:, b0 + 26:b0 + 120:3], mybir.AxisListType.X, AL.add))
            p("vector", lambda v, b0=b0: v.tensor_tensor(scr[:, 1100:1108], self.stats[:, b0 + 1:b0 + 24:3], self.stats[:, b0 + 1:b0 + 24:3], AL.mult))
            p("vector", lambda v, b0=b0: v.tensor_tensor(scr[:, 1108:1140], self.stats[:, b0 + 25:b0 + 120:3], self.stats[:, b0 + 25:b0 + 120:3], AL.mult))
            p("vector", lambda v: v.tensor_reduce(sm1[:, 4:5], scr[:, 1100:1108], mybir.AxisListType.X, AL.add))
            p("vector", lambda v: v.tensor_reduce(sm1[:, 5:6], scr[:, 1108:1140], mybir.AxisListType.X, AL.add))
            p("vector", lambda v: v.tensor_scalar(sm1[:, 1:2], sm1[:, 1:2], 64.0, None, AL.mult))
            self.vsep()
            p("vector", lambda v: v.scalar_tensor_tensor(sm1[:, 0:1], sm1[:, 0:1], 256.0, sm1[:, 1:2], AL.mult, AL.add))
            self.vsep()
            p("vector", lambda v, hp=hp: v.tensor_scalar(st3[:, hp:hp + 1], sm1[:, 0:1], 1.0 / (S * HD / 64), None, AL.mult))
            p("vector", lambda v: v.tensor_tensor(sm1[:, 2:3], sm1[:, 2:3], sm1[:, 3:4], AL.add))
            self.vsep()
            p("vector", lambda v: v.scalar_tensor_tensor(sm1[:, 2:3], sm1[:, 4:5], 256.0, sm1[:, 2:3], AL.mult, AL.add))
            self.vsep()
            p("vector", lambda v: v.scalar_tensor_tensor(sm1[:, 2:3], sm1[:, 5:6], 64.0, sm1[:, 2:3], AL.mult, AL.add))
            self.vsep()
            p("vector", lambda v: v.tensor_scalar(sm1[:, 2:3], sm1[:, 2:3], 1.0 / (S * HD / 64), None, AL.mult))
            p("vector", lambda v, hp=hp: v.tensor_tensor(scr[:, 1140:1141], st3[:, hp:hp + 1], st3[:, hp:hp + 1], AL.mult))
            self.vsep()
            p("vector", lambda v, hp=hp: v.tensor_tensor(st3[:, NHP + hp:NHP + hp + 1], sm1[:, 2:3], scr[:, 1140:1141], AL.subtract))
            p("vector", lambda v, hp=hp: v.tensor_copy(st3[:, 2 * NHP + hp:2 * NHP + hp + 1], scr[:, 1140:1141]))
        p("vector", lambda v: v.engine_nop() if False else v.tensor_copy(sm1[:, 6:7], st3[:, 0:1]), (), [I(ve)])
        p("tensor", lambda t: t.matmul(self.PF[2][0:2, 0:3 * NHP], self.sel64[:], st3[:], start=True, stop=True),
          self.mmw(("pf2a",), (W(ve), self.const_done)), [I(te)])
        p("vector", lambda v: v.tensor_copy(fin2[:, 0:3 * NHP], self.PF[2][0:2, 0:3 * NHP]), (W(te),), [I(ve)])
        self.setg(("pf2a", "pf2b"), W(ve))
        self.vsep()
        p("vector", lambda v: v.tensor_scalar(fin2[:, 0:3 * NHP], fin2[:, 0:3 * NHP], 1.0 / 64, None, AL.mult))
        self.vsep()
        p("vector", lambda v: v.tensor_tensor(sm4[:, 0:NHP], fin2[:, 0:NHP], fin2[:, 0:NHP], AL.mult))
        p("vector", lambda v: v.tensor_tensor(fin2[:, NHP:2 * NHP], fin2[:, NHP:2 * NHP], fin2[:, 2 * NHP:3 * NHP], AL.add))
        self.vsep()
        p("vector", lambda v: v.tensor_tensor(fin2[:, NHP:2 * NHP], fin2[:, NHP:2 * NHP], sm4[:, 0:NHP], AL.subtract))
        self.vsep()
        p("vector", lambda v: v.tensor_scalar(fin2[:, NHP:2 * NHP], fin2[:, NHP:2 * NHP], EPS, None, AL.add), (), [I(ve)])
        p("scalar", lambda s: s.activation(sm4[:, NHP:2 * NHP], fin2[:, NHP:2 * NHP], AF.Sqrt), (W(ve),), [I(se)])
        p("vector", lambda v: v.reciprocal(sm4[:, 0:NHP], sm4[:, NHP:2 * NHP]), (W(se),), [I(ve)])
        p("scalar", lambda s: s.activation(sm4[:, 2 * NHP:3 * NHP], sm4[:, 0:NHP], AF.Identity), (W(ve),))
        p("scalar", lambda s: s.activation(sm4[:, NHP:2 * NHP], sm4[:, 0:NHP], AF.Square), (), [I(se)])
        p("vector", lambda v: v.tensor_tensor(sm4[:, NHP:2 * NHP], sm4[:, NHP:2 * NHP], fin2[:, NHP:2 * NHP], AL.mult), (W(se),))
        self.vsep()
        p("vector", lambda v: v.tensor_scalar(sm4[:, NHP:2 * NHP], sm4[:, NHP:2 * NHP], -0.5, 1.5, AL.mult, AL.add))
        self.vsep()
        p("vector", lambda v: v.tensor_tensor(sm4[:, 0:NHP], sm4[:, 2 * NHP:3 * NHP], sm4[:, NHP:2 * NHP], AL.mult))
        self.vsep()
        p("vector", lambda v: v.tensor_tensor(fin2[:, 3 * NHP:4 * NHP], self.gnw2[:], sm4[:, 0:NHP], AL.mult))
        self.vsep()
        p("vector", lambda v: v.tensor_tensor(sm4[:, NHP:2 * NHP], fin2[:, 0:NHP], fin2[:, 3 * NHP:4 * NHP], AL.mult))
        self.vsep()
        p("vector", lambda v: v.tensor_tensor(fin2[:, 0:NHP], self.gnb2[:], sm4[:, NHP:2 * NHP], AL.subtract), (), [I(ve)])
        p("tensor", lambda t: t.matmul(self.PF[2][:, 0:NHP], self.rep2[:], fin2[0:2, 3 * NHP:4 * NHP], start=True, stop=True),
          self.mmw(("pf2a",), (W(ve),)))
        p("tensor", lambda t: t.matmul(self.PF[2][:, 512:512 + NHP], self.rep2[:], fin2[0:2, 0:NHP], start=True, stop=True),
          self.mmw(("pf2b",)), [I(te)])
        p("vector", lambda v: v.tensor_copy(self.A_vec[:], self.PF[2][:, 0:NHP]), (W(te),))
        p("vector", lambda v: v.tensor_copy(self.C_vec[:], self.PF[2][:, 512:512 + NHP]), (), [I(ve)])
        self.setg(("pf2a", "pf2b"), W(ve))
        if KDEBUG.get("gn"):
            p("gpsimd", lambda g: g.dma_start(out=self.taps["gn"][:, 0:NHP], in_=self.A_vec[:]), (W(ve),), [I(self.qg, 16)])
            p("gpsimd", lambda g: g.dma_start(out=self.taps["gn"][:, NHP:2 * NHP], in_=self.C_vec[:]), (), [I(self.qg, 16)])

    def _mish(self, ln_pa_done):
        p, I, W = self.op, self.I, self.W
        te, ve, se, qg = self.te, self.ve, self.se, self.qg
        w_lnfm = self.load_lnfm(self.ln1_sp, 0, W(self.te))
        keys3 = (("pf0a", "pf0b"), ("pf1a", "pf1b"), ("pf2a", "pf2b"))
        for fo in range(16):
            bi = fo % 2
            wg_w = self.wslice(self.wgen[bi], "wgT", fo, (W(te),))
            for half in range(2):
                mi = (fo * 2 + half) % 2
                k3 = keys3[(fo * 2 + half) % 3]
                pf = self.PF[(fo * 2 + half) % 3]
                ws = self.mmw(k3, (wg_w, w_lnfm if fo == 0 else None))
                for fi in range(16):
                    for sp in range(2):
                        p("tensor", lambda t, pf=pf, w=self.wgen[bi], fi=fi, sp=sp, half=half: t.matmul(
                            pf[:, sp * 512:(sp + 1) * 512], w[:, fi * 128:(fi + 1) * 128],
                            self.BIGA[:, fi * HF + half * 1024 + sp * 512: fi * HF + half * 1024 + sp * 512 + 512],
                            start=(fi == 0), stop=(fi == 15)),
                          ws if (fi == 0 and sp == 0) else (),
                          [I(te)] if (fi == 15 and sp == 1) else ())
                wte = W(te)
                p("scalar", lambda s, pf=pf: s.activation(self.stagf[:, 0:1024], pf[:, :], AF.Exp), (wte,))
                p("scalar", lambda s: s.activation(self.stagf[:, 0:1024], self.stagf[:, 0:1024], AF.Ln, bias=1.0))
                p("scalar", lambda s: s.activation(self.stagf[:, 0:1024], self.stagf[:, 0:1024], AF.Tanh), (), [I(se)])
                p("vector", lambda v, pf=pf, mi=mi: v.tensor_tensor(
                    self.msh[mi][:], pf[:, :], self.stagf[:, 0:1024], AL.mult),
                  (W(se), W(qg, self.qg.v)), [I(ve)])
                self.setg(k3, W(ve))
                p("gpsimd", lambda g, fo=fo, half=half, mi=mi: g.dma_start(
                    out=self.mish_sp[fo, :, half * 1024:(half + 1) * 1024], in_=self.msh[mi][:]),
                  (W(ve),), [I(qg, 16)])
        return W(qg)

    def _wo_phase(self, w_mish, w_ret_spill):
        p, I, W = self.op, self.I, self.W
        qd, qg, te, ve = self.qd, self.qg, self.te, self.ve
        w_free = W(te)
        for cc in range(16):
            p("gpsimd", lambda g, c=cc: g.dma_start(
                out=self.BIGA[:, c * HF:(c + 1) * HF], in_=self.D["woT"][c * 128:(c + 1) * 128, :]),
              (w_free,) if cc == 0 else (), [I(qg, 16)])
        w_wo = W(qg)
        for tt in range(16):
            bi = tt % 2
            p("sync", lambda s, tt=tt, b=bi: s.dma_start(
                out=self.strip[b][:].rearrange("p (h f) -> p h f", f=128),
                in_=self.ret_sp[:, :, tt * 128:(tt + 1) * 128].rearrange("h p f -> p h f")),
              (W(ve),) if tt >= 2 else (w_ret_spill, W(te), W(ve)), [I(qd, 16)])
            p("sync", lambda s, tt=tt, b=bi: s.dma_start(
                out=self.mstrip[b][:].rearrange("p (h f) -> p h f", f=128),
                in_=self.mish_sp[:, :, tt * 128:(tt + 1) * 128].rearrange("h p f -> p h f")),
              (w_mish, W(te)) if tt <= 1 else (), [I(qd, 16)])
            p("sync", lambda s, tt=tt, b=bi: s.dma_start(out=self.xtf[b][:], in_=self.D["x"][tt * 128:(tt + 1) * 128, :]),
              (W(te), W(ve)) if tt <= 1 else (), [I(qd, 16)])
            w_strip = W(qd)
            for hp in range(NHP):
                p("vector", lambda v, b=bi, hp=hp: v.tensor_scalar(
                    self.stagf[:, hp * 128:(hp + 1) * 128],
                    self.strip[b][:, hp * 128:(hp + 1) * 128],
                    self.A_vec[:, hp:hp + 1], self.C_vec[:, hp:hp + 1], AL.mult, AL.add),
                  (w_strip,) if hp == 0 else ())
            p("vector", lambda v, b=bi: v.tensor_tensor(
                self.qp[:], self.stagf[:, 0:2048], self.mstrip[b][:], AL.mult), (), [I(ve)])
            w_z = W(ve)
            bs = "A" if tt % 2 == 0 else "B"
            ws = self.mmw(self.bs_keys(bs), (w_z, w_wo if tt == 0 else None))
            for fi in range(16):
                for sp in range(4):
                    tgt, _ = self.pf_half(sp, bs)
                    p("tensor", lambda t, tg=tgt, fi=fi, sp=sp: t.matmul(
                        tg, self.qp[:, fi * 128:(fi + 1) * 128],
                        self.BIGA[:, fi * HF + sp * 512: fi * HF + sp * 512 + 512],
                        start=(fi == 0), stop=(fi == 15)),
                      ws if (fi == 0 and sp == 0) else (),
                      [I(te)] if (fi == 15 and sp == 3) else ())
            wte = W(te)
            for h in range(2):
                pbuf = self.bs_buf(h, bs)
                p("vector", lambda v, b=bi, h=h, pb=pbuf: v.tensor_tensor(
                    self.h1t[b][:, h * 1024:(h + 1) * 1024], pb[:, 0:1024],
                    self.xtf[b][:, h * 1024:(h + 1) * 1024], AL.add),
                  (wte,) if h == 0 else (), [I(ve)] if h == 1 else [])
            self.setg(self.bs_keys(bs), W(ve))
            w_apply = self.emit_ln(self.h1t[bi], self.ln2t[bi], W(ve), W(qg) if tt >= 2 else None)
            p("gpsimd", lambda g, b=bi, tt=tt: g.dma_start(out=self.h1_sp[tt * 128:(tt + 1) * 128, :], in_=self.h1t[b][:]),
              (w_apply,), [I(qg, 16)])
            p("gpsimd", lambda g, b=bi, tt=tt: g.dma_start(out=self.ln2_sp[tt * 128:(tt + 1) * 128, :], in_=self.ln2t[b][:]),
              (), [I(qg, 16)])
            if KDEBUG.get("h1"):
                p("gpsimd", lambda g, b=bi, tt=tt: g.dma_start(out=self.taps["h1"][tt * 128:(tt + 1) * 128, :], in_=self.h1t[b][:]),
                  (), [I(qg, 16)])
        return W(qg)

    def _ffn(self, w_h1_all):
        p, I, W = self.op, self.I, self.W
        qd, qg, te, ve, se = self.qd, self.qg, self.te, self.ve, self.se
        D = self.D
        w_ln2 = self.load_lnfm(self.ln2_sp, 0, w_h1_all)
        p("gpsimd", lambda g: g.dma_start(
            out=self.gwsb[:].rearrange("p (c f) -> p c f", f=32),
            in_=D["gwT"].rearrange("(c p) f -> p c f", p=128)), (W(te), W(self.ve)), [I(qg, 16)])
        p("gpsimd", lambda g: g.dma_start(
            out=self.rdT_sb[:].rearrange("p (c f) -> p c f", f=128),
            in_=D["RdT"].rearrange("(c p) f -> p c f", p=128)), (), [I(qg, 16)])
        p("gpsimd", lambda g: g.dma_start(out=self.sCg_sb[:], in_=D["sCgT"][:, :]), (), [I(qg, 16)])
        p("gpsimd", lambda g: g.dma_start(out=self.sCu_sb[:], in_=D["sCuT"][:, :]), (), [I(qg, 16)])
        p("gpsimd", lambda g: g.dma_start(
            out=self.sRd_sb[:].rearrange("p (c f) -> p c f", f=128),
            in_=D["sRdT"].rearrange("(c p) f -> p c f", p=128)), (), [I(qg, 16)])
        p("gpsimd", lambda g: g.dma_start(out=self.sU_sb[:, 0:128], in_=D["sUgT"][:, :]), (), [I(qg, 16)])
        p("gpsimd", lambda g: g.dma_start(out=self.sU_sb[:, 128:256], in_=D["sUuT"][:, :]), (), [I(qg, 16)])
        p("gpsimd", lambda g: g.dma_start(out=self.sU_sb[:, 256:384], in_=D["sUdT"][:, :]), (), [I(qg, 16)])
        p("gpsimd", lambda g: g.dma_start(out=self.sCd_sb[:], in_=D["sCdT"][:, :]), (), [I(qg, 16)])
        p("gpsimd", lambda g: g.dma_start(out=self.sel_e[:], in_=D["sel_e"][:, :]), (), [I(qg, 16)])
        w_small = W(qg)

        self._plx = 0

        def projL(dst, wname, extra=()):
            bi2 = self._plx % 2
            bs = "A" if bi2 == 0 else "B"
            self._plx += 1
            wl = self.wslice(self.wgen[bi2], wname, 0, (W(te),))
            wte = self.proj(self.wgen[bi2], (wl,) + tuple(extra), bs=bs)
            return self.evac2("vector", dst, wte, bs=bs)

        w_rg = projL(self.rg_sb, "RgT", (w_ln2,))
        w_ru = projL(self.ru_sb, "RuT")
        w_srg = projL(self.srg_sb, "sRgT")
        w_sru = projL(self.sru_sb, "sRuT")
        if KDEBUG.get("rg"):
            p("gpsimd", lambda g: g.dma_start(out=self.taps["rg"][:, :], in_=self.rg_sb[:]), (w_rg,), [I(qg, 16)])

        for pas in range(2):
            ws = self.mmw(("pf2a", "pf2b"), (w_small,) if pas == 0 else ())
            for fi in range(16):
                for sp in range(2):
                    p("tensor", lambda t, fi=fi, sp=sp, pas=pas: t.matmul(
                        self.PF[2][0:32, sp * 512:sp * 512 + 512],
                        self.gwsb[:, fi * 32:(fi + 1) * 32],
                        self.BIGA[:, fi * HF + pas * 1024 + sp * 512: fi * HF + pas * 1024 + sp * 512 + 512],
                        start=(fi == 0), stop=(fi == 15)),
                      ws if (fi == 0 and sp == 0) else (),
                      [I(te)] if (fi == 15 and sp == 1) else ())
            p("vector", lambda v, pas=pas: v.tensor_copy(self.lgf[:, pas * 1024:(pas + 1) * 1024], self.PF[2][0:32, :]),
              (W(te),), [I(ve)])
            self.setg(("pf2a", "pf2b"), W(ve))
        w_lg = W(ve)

        ws = self.mmw(("pf2a", "pf2b"), (w_lg,))
        for tl in range(16):
            p("tensor", lambda t, tl=tl: t.transpose(
                self.PF[2][:, tl * 32:(tl + 1) * 32],
                self.lgf[:, tl * 128:(tl + 1) * 128], self.idf[0:32, 0:32]),
              ws if tl == 0 else (), [I(te)] if tl == 15 else ())
        p("vector", lambda v: v.tensor_copy(self.lgtm[:], self.PF[2][:, 0:512]), (W(te),), [I(ve)])
        self.setg(("pf2a", "pf2b"), W(ve))

        sm1 = self.sm1
        for tl in range(16):
            lg = self.lgtm[:, tl * 32:(tl + 1) * 32]
            wetile = self.wetm[:, tl * 32:(tl + 1) * 32]
            msk = self.stagf[:, 64:96]
            p("vector", lambda v, lg=lg: v.tensor_reduce(sm1[:, 0:1], lg, mybir.AxisListType.X, AL.max))
            p("vector", lambda v, lg=lg, we=wetile: v.tensor_copy(we, lg))
            for rnd in range(3):
                mcol = sm1[:, 0:1] if rnd == 0 else sm1[:, 1:2]
                self.vsep()
                p("vector", lambda v, we=wetile, mc=mcol, mk=msk: v.tensor_scalar(mk, we, mc[:, 0:1], None, AL.is_ge))
                self.vsep()
                p("vector", lambda v, we=wetile, mk=msk: v.scalar_tensor_tensor(we, mk, -1e30, we, AL.mult, AL.add))
                self.vsep()
                p("vector", lambda v, we=wetile: v.tensor_reduce(sm1[:, 1:2], we, mybir.AxisListType.X, AL.max))
            self.vsep()
            p("vector", lambda v: v.tensor_scalar(sm1[:, 2:3], sm1[:, 0:1], -1.0, None, AL.mult))
            self.vsep()
            p("vector", lambda v, lg=lg: v.tensor_scalar(self.stagf[:, 0:32], lg, sm1[:, 2:3], None, AL.add), (), [I(ve)])
            p("scalar", lambda s: s.activation(self.stagf[:, 32:64], self.stagf[:, 0:32], AF.Exp), (W(ve),), [I(se)])
            p("vector", lambda v: v.tensor_reduce(sm1[:, 3:4], self.stagf[:, 32:64], mybir.AxisListType.X, AL.add), (W(se),))
            p("vector", lambda v, lg=lg, mk=msk: v.tensor_scalar(mk, lg, sm1[:, 1:2], None, AL.is_ge))
            self.vsep()
            p("vector", lambda v, mk=msk: v.tensor_tensor(self.stagf[:, 32:64], self.stagf[:, 32:64], mk, AL.mult))
            self.vsep()
            p("vector", lambda v: v.tensor_reduce(sm1[:, 4:5], self.stagf[:, 32:64], mybir.AxisListType.X, AL.add))
            self.vsep()
            p("vector", lambda v: v.scalar_tensor_tensor(sm1[:, 4:5], sm1[:, 3:4], 1e-20, sm1[:, 4:5], AL.mult, AL.add))
            p("vector", lambda v: v.reciprocal(sm1[:, 5:6], sm1[:, 4:5]), (), [I(ve)])
            p("scalar", lambda s: s.activation(sm1[:, 6:7], sm1[:, 5:6], AF.Identity), (W(ve),), [I(se)])
            p("vector", lambda v, we=wetile: v.tensor_scalar(we, self.stagf[:, 32:64], sm1[:, 6:7], None, AL.mult), (W(se),), [I(ve)])
        w_we = W(ve)
        if KDEBUG.get("we"):
            p("gpsimd", lambda g: g.dma_start(out=self.taps["we"][:, :], in_=self.wetm[:]), (w_we,), [I(qg, 16)])

        for grp in range(4):
            ws = self.mmw(("pf2a", "pf2b"), (w_we,) if grp == 0 else ())
            for q in range(4):
                tl = grp * 4 + q
                p("tensor", lambda t, tl=tl, q=q: t.transpose(
                    self.PF[2][0:32, q * 128:(q + 1) * 128],
                    self.wetm[:, tl * 32:(tl + 1) * 32], self.idf[:]),
                  ws if q == 0 else (), [I(te)] if q == 3 else ())
            p("vector", lambda v, grp=grp: v.tensor_copy(self.weT[:, grp * 512:(grp + 1) * 512], self.PF[2][0:32, 0:512]),
              (W(te),), [I(ve)])
            self.setg(("pf2a", "pf2b"), W(ve))
        w_weT = W(ve)

        def k128(dst, lhsT, rhsbuf, waits):
            ws = self.mmw(self.PK4, waits)
            for sp in range(4):
                tgt, _ = self.pf_half(sp)
                p("tensor", lambda t, tg=tgt, l=lhsT, r=rhsbuf, sp=sp: t.matmul(
                    tg, l, r[:, sp * 512:(sp + 1) * 512], start=True, stop=True),
                  ws if sp == 0 else (), [I(te)] if sp == 3 else ())
            return self.evac2("vector", dst, W(te))

        w_hsg = k128(self.hsg, self.sU_sb[:, 0:128], self.srg_sb, (w_srg, w_small))
        w_hsu = k128(self.hsu, self.sU_sb[:, 128:256], self.sru_sb, (w_sru,))

        def smblk(ft):
            return (self.BIGA[:, ft * HF:(ft + 1) * HF] if ft < 16
                    else self.smX[:, (ft - 16) * HF:(ft - 15) * HF])

        for ft in range(NSI):
            ws = self.mmw(self.PK4, (w_hsg, w_lg) if ft == 0 else ())
            for sp in range(4):
                tgt, _ = self.pf_half(sp, "A")
                p("tensor", lambda t, tg=tgt, ft=ft, sp=sp: t.matmul(
                    tg, self.sCg_sb[:, ft * 128:(ft + 1) * 128],
                    self.hsg[:, sp * 512:(sp + 1) * 512], start=True, stop=True),
                  ws if sp == 0 else (), [I(te)] if sp == 3 else ())
            wte = W(te)
            ws = self.mmw(self.bs_keys("B"), (w_hsu,) if ft == 0 else ())
            for sp in range(4):
                tgt, _ = self.pf_half(sp, "B")
                p("tensor", lambda t, tg=tgt, ft=ft, sp=sp: t.matmul(
                    tg, self.sCu_sb[:, ft * 128:(ft + 1) * 128],
                    self.hsu[:, sp * 512:(sp + 1) * 512], start=True, stop=True),
                  ws if sp == 0 else (), [I(te)] if sp == 3 else ())
            wte_u = W(te)
            for h in range(2):
                p("scalar", lambda s, h=h: s.activation(self.sgb[:, h * 1024:(h + 1) * 1024], self.PF[h][:, :], AF.Silu),
                  (wte,) if h == 0 else (), [I(se)] if h == 1 else [])
            self.setg(self.PK4, W(se))
            w_sg = W(se)
            for h in range(2):
                pbuf = self.bs_buf(h, "B")
                p("vector", lambda v, ft=ft, h=h, pb=pbuf: v.tensor_tensor(
                    smblk(ft)[:, h * 1024:(h + 1) * 1024], pb[:, 0:1024],
                    self.sgb[:, h * 1024:(h + 1) * 1024], AL.mult),
                  (wte_u, w_sg) if h == 0 else (), [I(ve)] if h == 1 else [])
            self.setg(self.bs_keys("B"), W(ve))
        ws = self.mmw(self.PK4, (W(ve),))
        for ki in range(NSI):
            for sp in range(4):
                tgt, _ = self.pf_half(sp)
                p("tensor", lambda t, tg=tgt, ki=ki, sp=sp: t.matmul(
                    tg, self.sRd_sb[:, ki * 128:(ki + 1) * 128],
                    smblk(ki)[:, sp * 512:(sp + 1) * 512],
                    start=(ki == 0), stop=(ki == NSI - 1)),
                  ws if (ki == 0 and sp == 0) else (),
                  [I(te)] if (ki == NSI - 1 and sp == 3) else ())
        w_srd = self.evac2("vector", self.sru_sb, W(te))
        w_srdu = k128(self.srdu_sb, self.sU_sb[:, 256:384], self.sru_sb, (w_srd,))
        if KDEBUG.get("srdu"):
            p("gpsimd", lambda g: g.dma_start(out=self.taps["srdu"][:, :], in_=self.srdu_sb[:]), (w_srdu,), [I(qg, 16)])

        for sp_i in range(NSPAN):
            self._moe_span(sp_i, w_weT, w_rg, w_ru)
            for sub in range(SPAN // 512):
                self._out_span(sp_i, sub)

    def _moe_span(self, sp_i, w_weT, w_rg, w_ru):
        p, I, W = self.op, self.I, self.W
        qg, te, ve, se = self.qg, self.te, self.ve, self.se
        D = self.D
        t0 = sp_i * SPAN
        NQ = SPAN // 512
        for e in range(NE):
            bi = e % 2
            ew = self.eW[bi]
            wld = (W(te), W(ve))
            p("gpsimd", lambda g, ew=ew, e=e: g.dma_start(out=ew["ug"][:], in_=D["UgT"][e, :, :]), wld, [I(qg, 16)])
            p("gpsimd", lambda g, ew=ew, e=e: g.dma_start(out=ew["cg"][:], in_=D["CgT"][e, :, :]), (), [I(qg, 16)])
            p("gpsimd", lambda g, ew=ew, e=e: g.dma_start(out=ew["uu"][:], in_=D["UuT"][e, :, :]), (), [I(qg, 16)])
            p("gpsimd", lambda g, ew=ew, e=e: g.dma_start(out=ew["cu"][:], in_=D["CuT"][e, :, :]), (), [I(qg, 16)])
            p("gpsimd", lambda g, ew=ew, e=e: g.dma_start(out=ew["ud"][:], in_=D["UdT"][e, :, :]), (), [I(qg, 16)])
            w_ew = W(qg)
            ws2 = self.mmw(("pf2a", "pf2b"), (w_weT,) if (e == 0 and sp_i == 0) else ())
            for q in range(NQ):
                p("tensor", lambda t, e=e, q=q: t.matmul(
                    self.PF[2][:, q * 512:q * 512 + 512], self.sel_e[:, e * 128:(e + 1) * 128],
                    self.weT[:, t0 + q * 512:t0 + q * 512 + 512], start=True, stop=True),
                  ws2 if q == 0 else (), [I(te)] if q == NQ - 1 else ())
            w_wbm = W(te)
            ws0 = self.mmw(("pf0a", "pf0b"), (w_ew, w_rg) if (e == 0 and sp_i == 0) else (w_ew,))
            for q in range(NQ):
                p("tensor", lambda t, ew=ew, q=q: t.matmul(
                    self.PF[0][:, q * 512:q * 512 + 512], ew["ug"][:],
                    self.rg_sb[:, t0 + q * 512:t0 + q * 512 + 512], start=True, stop=True),
                  ws0 if q == 0 else (), [I(te)] if q == NQ - 1 else ())
            w_a1g = W(te)
            ws1 = self.mmw(("pf1a", "pf1b"), (w_ru,) if (e == 0 and sp_i == 0) else ())
            for q in range(NQ):
                p("tensor", lambda t, ew=ew, q=q: t.matmul(
                    self.PF[1][:, q * 512:q * 512 + 512], ew["uu"][:],
                    self.ru_sb[:, t0 + q * 512:t0 + q * 512 + 512], start=True, stop=True),
                  ws1 if q == 0 else (), [I(te)] if q == NQ - 1 else ())
            w_a1u = W(te)
            p("scalar", lambda s: s.copy(self.wb_sb[:], self.PF[2][:, 0:SPAN]), (w_wbm,))
            p("scalar", lambda s: s.copy(self.hg_sb[:], self.PF[0][:, 0:SPAN]), (w_a1g,), [I(se)])
            self.setg(("pf2a", "pf2b", "pf0a", "pf0b"), W(se))
            w_hg = W(se)
            p("vector", lambda v: v.tensor_tensor(self.hu_sb[:], self.PF[1][:, 0:SPAN], self.wb_sb[:], AL.mult),
              (w_a1u, w_hg), [I(ve)])
            self.setg(("pf1a", "pf1b"), W(ve))
            w_hu = W(ve)
            w_sil = {}
            for ft in range(NI + 1):
                if ft < NI:
                    ws = self.mmw(("pf0a", "pf0b"), (w_hg,) if ft == 0 else ())
                    for q in range(NQ):
                        p("tensor", lambda t, ew=ew, ft=ft, q=q: t.matmul(
                            self.PF[0][:, q * 512:(q + 1) * 512], ew["cg"][:, ft * 128:(ft + 1) * 128],
                            self.hg_sb[:, q * 512:(q + 1) * 512], start=True, stop=True),
                          ws if q == 0 else (), [I(te)] if q == NQ - 1 else ())
                    wcg = W(te)
                    p("scalar", lambda s, ft=ft: s.activation(self.gblk2[ft % 2][:], self.PF[0][:, 0:SPAN], AF.Silu),
                      (wcg, W(ve)), [I(se)])
                    self.setg(("pf0a", "pf0b"), W(se))
                    w_sil[ft] = W(se)
                if ft >= 1:
                    fu = ft - 1
                    ws = self.mmw(("pf1a", "pf1b"), (w_hu,) if fu == 0 else ())
                    for q in range(NQ):
                        p("tensor", lambda t, ew=ew, fu=fu, q=q: t.matmul(
                            self.PF[1][:, q * 512:(q + 1) * 512], ew["cu"][:, fu * 128:(fu + 1) * 128],
                            self.hu_sb[:, q * 512:(q + 1) * 512], start=True, stop=True),
                          ws if q == 0 else (), [I(te)] if q == NQ - 1 else ())
                    wcu = W(te)
                    p("vector", lambda v, fu=fu: v.tensor_tensor(self.mblk[:], self.PF[1][:, 0:SPAN], self.gblk2[fu % 2][:], AL.mult),
                      (wcu, w_sil[fu], W(te)), [I(ve)])
                    self.setg(("pf1a", "pf1b"), W(ve))
                    w_m = W(ve)
                    ws3 = self.mmw(("pf2a", "pf2b"), (w_m,)) if fu == 0 else (w_m,)
                    for q in range(NQ):
                        p("tensor", lambda t, fu=fu, q=q: t.matmul(
                            self.PF[2][:, q * 512:(q + 1) * 512], self.rdT_sb[:, fu * 128:(fu + 1) * 128],
                            self.mblk[:, q * 512:(q + 1) * 512],
                            start=(fu == 0), stop=(fu == NI - 1)),
                          ws3 if q == 0 else (), [I(te)] if (fu == NI - 1 and q == NQ - 1) else ())
                if ft < NI and ft >= 1:
                    pass
            p("scalar", lambda s: s.copy(self.rd_sb[:], self.PF[2][:, 0:SPAN]), (W(te),), [I(se)])
            self.setg(("pf2a", "pf2b"), W(se))
            w_rd = W(se)
            ws = self.mmw(("pf0a", "pf0b"), (w_rd,))
            for q in range(NQ):
                p("tensor", lambda t, ew=ew, q=q: t.matmul(
                    self.PF[0][:, q * 512:(q + 1) * 512], ew["ud"][:],
                    self.rd_sb[:, q * 512:(q + 1) * 512], start=True, stop=True),
                  ws if q == 0 else (), [I(te)] if q == NQ - 1 else ())
            p("vector", lambda v, e=e: v.tensor_copy(self.BIGA[:, e * SPAN:(e + 1) * SPAN], self.PF[0][:, 0:SPAN]),
              (W(te),), [I(ve)])
            self.setg(("pf0a", "pf0b"), W(ve))
        if KDEBUG.get("rdu") and sp_i == 0:
            p("gpsimd", lambda g: g.dma_start(out=self.taps["rdu"][:, :], in_=self.BIGA[:, 0:NE * SPAN]), (W(ve),), [I(qg, 16)])

    def _out_span(self, sp_i, sub):
        p, I, W = self.op, self.I, self.W
        qd, qg, te, ve, os_ = self.qd, self.qg, self.te, self.ve, self.os
        D = self.D
        w_rdu = W(self.ve)
        w_oasm_free = W(os_)
        soff = sub * 512
        w_ent = (W(te), W(ve))
        cd_g = [w_ent, w_ent]
        for fo in range(16):
            bi = fo % 2
            p("gpsimd", lambda g, fo=fo, b=bi: g.dma_start(
                out=self.cdW[b][:].rearrange("p (e f) -> p e f", f=128),
                in_=D["CdT"][fo].rearrange("e p f -> p e f")),
              cd_g[bi], [I(qg, 16)])
            w_cd = W(qg)
            pk = ("pf0a", "pf0b", "pf1a", "pf1b")[fo % 4]
            pf = self.PF[(fo % 4) // 2]
            reg = pf[:, (fo % 2) * 512:(fo % 2) * 512 + 512]
            ws = self.mmw((pk,), (w_cd, w_rdu) if fo == 0 else (w_cd,))
            for e in range(NE + 1):
                if e < NE:
                    lhsT = self.cdW[bi][:, e * 128:(e + 1) * 128]
                    rhs = self.BIGA[:, e * SPAN + soff: e * SPAN + soff + 512]
                else:
                    lhsT = self.sCd_sb[:, fo * 128:(fo + 1) * 128]
                    rhs = self.srdu_sb[:, sp_i * SPAN + soff: sp_i * SPAN + soff + 512]
                p("tensor", lambda t, l=lhsT, r=rhs, e=e, reg=reg: t.matmul(
                    reg, l, r, start=(e == 0), stop=(e == NE)),
                  ws if e == 0 else (), [I(te)] if e == NE else ())
            wte = W(te)
            cd_g[bi] = (wte,)
            st = self.stagf[:, (fo % 2) * 512:(fo % 2) * 512 + 512]
            p("vector", lambda v, st=st, reg=reg: v.tensor_copy(st, reg), (wte,), [I(ve)])
            self.setg((pk,), W(ve))
            w_st = W(ve)
            tk = "pf2a" if fo % 2 == 0 else "pf2b"
            treg = self.PF[2][:, 0:512] if fo % 2 == 0 else self.PF[2][:, 512:1024]
            ws2 = self.mmw((tk,), (w_st,))
            for q in range(4):
                p("tensor", lambda t, st=st, q=q, treg=treg: t.transpose(
                    treg[:, q * 128:(q + 1) * 128], st[:, q * 128:(q + 1) * 128], self.idf[:]),
                  ws2 if q == 0 else (), [I(te)] if q == 3 else ())
            wtt = W(te)
            for q in range(4):
                p("vector", lambda v, q=q, fo=fo, treg=treg: v.tensor_copy(
                    self.OASM[:, q * HID + fo * 128: q * HID + (fo + 1) * 128],
                    treg[:, q * 128:(q + 1) * 128]),
                  (wtt, w_oasm_free) if (q == 0 and fo == 0) else ((wtt,) if q == 0 else ()),
                  [I(ve)] if q == 3 else [])
            self.setg((tk,), W(ve))
        w_asm = W(ve)
        for q in range(4):
            row = sp_i * SPAN + soff + q * 128
            p("sync", lambda s, row=row: s.dma_start(out=self.h1t[0][:], in_=self.h1_sp[row:row + 128, :]),
              (W(ve),), [I(qd, 16)])
            p("vector", lambda v, q=q: v.tensor_tensor(
                self.OASM[:, q * HID:(q + 1) * HID], self.OASM[:, q * HID:(q + 1) * HID],
                self.h1t[0][:], AL.add), (W(qd), w_asm), [I(ve)])
            p("sync", lambda s, row=row, q=q: s.dma_start(out=self.out[row:row + 128, :],
                                                          in_=self.OASM[:, q * HID:(q + 1) * HID]),
              (W(ve),), [I(os_, 16)])


def _build_cached():
    key = "nc"
    if key not in _CACHE:
        b = B()
        nc = b.build()
        _CACHE[key] = (nc, b.taps, b)
    return _CACHE[key]


def kernel(**inputs):
    nc, taps, _b = _build_cached()
    shared = _prep_shared(inputs)
    rot = _rot_tables()
    in_maps = [_prep_core(inputs, shared, rot, c) for c in range(8)]
    res = run_bass_kernel_spmd(nc, in_maps, core_ids=list(range(8)))
    out = np.empty((4, S, HID), np.float32)
    for c in range(8):
        bb, own = c // 2, c % 2
        out[bb, own * HF:(own + 1) * HF] = res.results[c]["out"]
    if KDEBUG:
        kernel.last_taps = [{k: res.results[c]["tap_" + k] for k in taps} for c in range(8)]
    return out
